# version 01
# baseline (speedup 1.0000x reference)
import numpy as np
import ml_dtypes
from contextlib import ExitStack
import concourse.bass as bass
import concourse.mybir as mybir
from concourse.bass_utils import run_bass_kernel_spmd

F32 = mybir.dt.float32
BF16 = mybir.dt.bfloat16
I32 = mybir.dt.int32
AF = mybir.ActivationFunctionType
ALU = mybir.AluOpType

T = 2048
D = 1024
H = 16
DH = 64
NCH = 8
QB = 512
NQB = T // QB
SCALE = DH ** -0.5
BF = ml_dtypes.bfloat16

NPHASE = 4
F0STEP = 4
DUMP = None

ENGINES = ("tensor", "scalar", "vector", "gpsimd", "sync")


class Sem:
    def __init__(self, nc, ctx, name):
        self.h = ctx.enter_context(nc.semaphore(name))
        self.v = 0


class Orch:

    def __init__(self, nc, ctx, reg=None, regctx=None):
        self.nc = nc
        self.ctx = ctx
        self.reg = reg if reg is not None else {}
        self.regctx = regctx if regctx is not None else ctx
        self.ops = {e: [] for e in ENGINES}

    def sem(self, name):
        if name not in self.reg:
            self.reg[name] = Sem(self.nc, self.regctx, name)
        return self.reg[name]

    def fop(self, eng, fn, sem=None, n=1):
        f = self.sem(f"fence_{eng}")
        if sem is None:
            sem = f
        else:
            self.op(eng, fn, sem, n)
            self.wait(eng, sem)
            return
        self.op(eng, fn, f, 1)
        self.wait(eng, f)

    def wait(self, eng, sem, val=None):
        v = sem.v if val is None else val
        assert v <= sem.v, f"wait on future value {v} > {sem.v}"
        if v <= 0:
            return
        self.ops[eng].append(lambda e, h=sem.h, v=v: e.wait_ge(h, v))

    def op(self, eng, fn, sem=None, n=1):
        if sem is None:
            self.ops[eng].append(lambda e, fn=fn: fn(e))
        else:
            sem.v += n
            self.ops[eng].append(
                lambda e, fn=fn, h=sem.h, n=n: fn(e).then_inc(h, n))

    def flush(self, block):
        for eng in ENGINES:
            lst = self.ops[eng]
            if not lst:
                continue

            def section(e, lst=lst):
                for fn in lst:
                    fn(e)
            getattr(block, eng)(section)
        self.ops = {e: [] for e in ENGINES}




def _bf_i_of_B(tau, pi):
    h = pi & 31
    l01 = pi >> 5
    return (h << 7) | (tau << 2) | l01


def _bf_stage_lookup(w, s, out_i, in_i):
    state = ((out_i & ((1 << s) - 1)) | (in_i & ~((2 << s) - 1))
             | (in_i & (1 << s)))
    low = state & ((1 << s) - 1)
    high = state >> (s + 1)
    k = high * (1 << s) + low
    return w[s, k, (out_i >> s) & 1, (in_i >> s) & 1]


def _bf_expand(w, stages, blayout):
    nst = len(stages)
    U = np.zeros((nst, 128, 4096), np.float32)
    pp = np.arange(128)
    smask = 0
    for s_ in stages:
        smask |= (1 << s_)
    for cc in range(32):
        if blayout:
            out_i = np.array([_bf_i_of_B(cc, q) for q in pp])[:, None]
            in_i = np.array([_bf_i_of_B(cc, q) for q in pp])[None, :]
            kmask = ~smask
        else:
            out_i = (128 * cc + pp)[:, None]
            in_i = (128 * cc + pp)[None, :]
            kmask = (~smask) & 127
        oi = np.broadcast_to(out_i, (128, 128))
        ii = np.broadcast_to(in_i, (128, 128))
        for si, s_ in enumerate(stages):
            U[si, :, 128 * cc:128 * (cc + 1)] = _bf_stage_lookup(w, s_, oi, ii)
        keep = ((oi ^ ii) & kmask) == 0
        U[0, :, 128 * cc:128 * (cc + 1)] = np.where(
            keep, U[0, :, 128 * cc:128 * (cc + 1)], 0.0)
    return np.ascontiguousarray(U)


def colchunk(vec):
    v = np.asarray(vec, np.float32)
    k = v.shape[0] // 128
    return np.ascontiguousarray(v.reshape(k, 128).T)


def l0_masks():
    out = np.zeros((6, 128, 512), BF)
    for i, r in enumerate([-128, 0, 128, 256, 384, 512]):
        p = np.arange(128)[:, None]
        t_ = np.arange(512)[None, :]
        out[i] = (np.abs(r + p - t_) <= 64).astype(BF)
    return np.ascontiguousarray(out.transpose(1, 0, 2).reshape(128, 6 * 512))


def l1_masks():
    out = np.zeros((4, 128, 512), BF)
    for i, r in enumerate([256, 384, 512, 640]):
        p = np.arange(128)[:, None]
        t_ = np.arange(512)[None, :]
        out[i] = ((r + p - t_) <= 256).astype(BF)
    return np.ascontiguousarray(out.transpose(1, 0, 2).reshape(128, 4 * 512))


def hmask16_host():
    m = np.zeros((128, 8 * 16), BF)
    for c in range(8):
        m[0:64, 16 * c + 2 * c] = 1
        m[64:128, 16 * c + 2 * c + 1] = 1
    return m


def selpair_host():
    m = np.zeros((16, 8 * 128), BF)
    for p in range(8):
        m[2 * p, 128 * p:128 * p + 64] = 1
        m[2 * p + 1, 128 * p + 64:128 * p + 128] = 1
    return m


def host_prep(inputs):
    cm = {}
    cm["identf"] = np.eye(128, dtype=np.float32)
    cm["identb"] = np.eye(128, dtype=BF)
    cm["identfn"] = (-np.eye(128)).astype(np.float32)
    cm["hmask16"] = hmask16_host()
    cm["selpair"] = selpair_host()
    cm["onescol"] = np.ones((128, 1), BF)
    cm["onesrow"] = np.ones((1, 128), BF)
    if NPHASE >= 1:
        cm["a0_w"] = np.ascontiguousarray(
            np.asarray(inputs["L0_attn_Wqkv"], np.float32))
        cm["a0_b"] = colchunk(inputs["L0_attn_bqkv"])
        cm["a0_wo"] = np.ascontiguousarray(
            np.asarray(inputs["L0_attn_Wout"], np.float32))
        cm["a0_bo"] = colchunk(inputs["L0_attn_bout"])
        cm["mask_a0"] = l0_masks()
    if NPHASE >= 2:
        cm["f0_wg"] = np.ascontiguousarray(
            np.asarray(inputs["L0_ffn_w_gate"], np.float32))
        cm["f0_wu"] = np.ascontiguousarray(
            np.asarray(inputs["L0_ffn_w_up"], np.float32))
        cm["f0_wd"] = np.ascontiguousarray(
            np.asarray(inputs["L0_ffn_w_down"], np.float32))
        cm["f0_clipx"] = np.asarray(inputs["L0_ffn_clip_x"],
                                    np.float32).reshape(1, 1)
        cm["f0_clipd"] = np.asarray(inputs["L0_ffn_clip_down"],
                                    np.float32).reshape(1, 1)
    if NPHASE >= 3:
        cm["a1_w"] = np.ascontiguousarray(
            np.asarray(inputs["L1_attn_in_proj_w"], np.float32).T)
        cm["a1_b"] = colchunk(inputs["L1_attn_in_proj_b"])
        cm["a1_wo"] = np.ascontiguousarray(
            np.asarray(inputs["L1_attn_out_proj_w"], np.float32).T)
        cm["a1_bo"] = colchunk(inputs["L1_attn_out_proj_b"])
        cm["mask_a1"] = l1_masks()
    if NPHASE >= 4:
        cm["f1_wexp"] = np.ascontiguousarray(
            np.asarray(inputs["L1_ffn_W_exp"], np.float32))
        cm["f1_bexp"] = colchunk(inputs["L1_ffn_b_exp"])
        ub = np.asarray(inputs["L1_ffn_up_bias"], np.float32)
        ubB = np.zeros((128, 32), np.float32)
        for tau in range(32):
            for pi in range(128):
                ubB[pi, tau] = ub[_bf_i_of_B(tau, pi)]
        cm["f1_ubB"] = ubB
        cm["f1_bcon"] = colchunk(inputs["L1_ffn_b_con"])
        cm["f1_dbias"] = colchunk(inputs["L1_ffn_down_bias"])
        wcon = np.asarray(inputs["L1_ffn_W_con"], np.float32)
        perm = np.array([_bf_i_of_B(jB // 128, jB % 128) for jB in range(4096)])
        cm["f1_wconB"] = np.ascontiguousarray(wcon[perm])
        uw = np.asarray(inputs["L1_ffn_up_weights"], np.float32)
        dw = np.asarray(inputs["L1_ffn_down_weights"], np.float32)
        cm["f1_U0"] = _bf_expand(uw, list(range(7)), False)
        cm["f1_U1"] = _bf_expand(uw, list(range(7, 12)), True)
        cm["f1_U2"] = _bf_expand(dw, list(range(7)), False)
        cm["f1_U3"] = _bf_expand(dw, list(range(7, 12)), True)
    return cm



class L0Cfg:
    name = "a0"
    nmask = 6
    kst_w = 768

    @staticmethod
    def window(qb):
        qs = qb * QB
        return max(0, qs - 128), min(T, qs + 640)

    @staticmethod
    def maskidx(qb, ts):
        r = ts - qb * QB
        return {-128: 0, 0: 1, 128: 2, 256: 3, 384: 4, 512: 5}[r]


class L1Cfg:
    name = "a1"
    nmask = 4
    kst_w = 2048

    @staticmethod
    def window(qb):
        qs = qb * QB
        return 0, min(T, qs + 768)

    @staticmethod
    def maskidx(qb, ts):
        r = ts - qb * QB
        return {256: 0, 384: 1, 512: 2, 640: 3}.get(r)




class LNPack:

    def __init__(self, nc, ctx):
        sb = lambda n, s, d: ctx.enter_context(nc.sbuf_tensor(n, s, d))
        self.ln_a = sb("ln_a", [1, 512], F32)
        self.ln_b = sb("ln_b", [1, 512], F32)
        self.mr = sb("ln_mr", [1, 1024], BF16)
        self.mrb = sb("ln_mrb", [128, 1024], BF16)
        self.y2 = [sb(f"ln_y2{i}", [128, 512], BF16) for i in range(2)]
        self.sLN = Sem(nc, ctx, "ln_sLN")
        self.tLS = Sem(nc, ctx, "ln_tLS")
        self.vLS = Sem(nc, ctx, "ln_vLS")
        self.sLR = Sem(nc, ctx, "ln_sLR")
        self.tLB = Sem(nc, ctx, "ln_tLB")
        self.sLB = Sem(nc, ctx, "ln_sLB")
        self.vXW = Sem(nc, ctx, "ln_vXW")

    def emit(self, o, G, PA, PB, yb, qs):
        x_bf = G["x_bf"]
        for c in range(8):
            if c >= 2:
                o.wait("scalar", self.tLS, self.tLS.v - 2)
            o.op("scalar", lambda s, c=c, y2=self.y2[c % 2]: s.activation(
                y2[:], yb[:, 512 * c:512 * (c + 1)], AF.Square), self.sLN)
            o.wait("tensor", self.sLN, self.sLN.v)
            o.op("tensor", lambda t, c=c: t.matmul(
                PA[0:1, 0:512], G["onescol"][:],
                yb[:, 512 * c:512 * (c + 1)],
                start=(c == 0), stop=(c == 7)), self.tLS)
            o.op("tensor", lambda t, c=c, y2=self.y2[c % 2]: t.matmul(
                PA[0:1, 512:1024], G["onescol"][:], y2[:],
                start=(c == 0), stop=(c == 7)), self.tLS)
        o.wait("vector", self.tLS, self.tLS.v)
        o.fop("vector", lambda v: v.tensor_scalar_mul(
            self.mr[0:1, 0:512], PA[0:1, 0:512], 1.0 / D))
        o.fop("vector", lambda v: v.tensor_scalar(
            self.ln_a[:], PA[0:1, 512:1024], 1.0 / D, 1e-5,
            ALU.mult, ALU.add))
        o.fop("vector", lambda v: v.tensor_mul(
            self.ln_b[:], self.mr[0:1, 0:512], self.mr[0:1, 0:512]))
        o.op("vector", lambda v: v.tensor_sub(
            self.ln_a[:], self.ln_a[:], self.ln_b[:]), self.vLS)
        o.wait("scalar", self.vLS, self.vLS.v)
        o.fop("scalar", lambda s: s.activation(
            self.ln_b[:], self.ln_a[:], AF.Ln))
        o.op("scalar", lambda s: s.activation(
            self.mr[0:1, 512:1024], self.ln_b[:], AF.Exp, scale=-0.5),
            self.sLR)
        o.wait("tensor", self.sLR, self.sLR.v)
        o.op("tensor", lambda t: t.matmul(
            PB[:, 0:512], G["onesrow"][:], self.mr[0:1, 0:512],
            start=True, stop=True), self.tLB)
        o.op("tensor", lambda t: t.matmul(
            PB[:, 512:1024], G["onesrow"][:], self.mr[0:1, 512:1024],
            start=True, stop=True), self.tLB)
        o.wait("scalar", self.tLB, self.tLB.v)
        o.op("scalar", lambda s: s.copy(
            self.mrb[:, 0:512], PB[:, 0:512]), self.sLB)
        o.op("scalar", lambda s: s.copy(
            self.mrb[:, 512:1024], PB[:, 512:1024]), self.sLB)
        o.wait("vector", self.sLB, self.sLB.v)
        for c in range(8):
            fn = o.fop if c == 7 else o.op
            fn("vector", lambda v, c=c: v.tensor_sub(
                yb[:, 512 * c:512 * (c + 1)], yb[:, 512 * c:512 * (c + 1)],
                self.mrb[:, 0:512]))
        for c in range(8):
            o.op("vector", lambda v, c=c, qs=qs: v.tensor_mul(
                x_bf[:, T * c + qs:T * c + qs + QB],
                yb[:, 512 * c:512 * (c + 1)], self.mrb[:, 512:1024]),
                self.vXW)


def phase_in(nc, block, G):
    ctx = ExitStack()
    o = Orch(nc, ctx, G['semreg'], G['semctx'])
    xin = ctx.enter_context(nc.sbuf_tensor("in_stage", [128, 2 * D], F32))
    pT = [G["P0"][:, 0:128], G["P0"][:, 512:640]]
    ldc = o.sem("in_ldc")
    ldx = [o.sem("in_ldx0"), o.sem("in_ldx1")]
    mmT = o.sem("in_mmT")
    evT = o.sem("in_evT")

    for nm, tens in G["const_loads"]:
        o.op("gpsimd", lambda g, nm=nm, tens=tens: g.dma_start(
            out=tens[:], in_=G[nm + "_h"]), ldc, 16)
    nconst = ldc.v

    for i in range(16):
        if i >= 2:
            o.wait("sync", mmT, 8 * (i - 1))
        o.op("sync", lambda s, i=i: s.dma_start(
            out=xin[:, (i % 2) * D:(i % 2 + 1) * D],
            in_=G["x_h"][i * 128:(i + 1) * 128, :]), ldx[i % 2], 16)
        o.wait("tensor", ldx[i % 2], 16 * (i // 2 + 1))
        if i == 0:
            o.wait("tensor", ldc, nconst)
        for d in range(NCH):
            j = 8 * i + d
            if j >= 2:
                o.wait("tensor", evT, j - 1)
            o.op("tensor", lambda t, i=i, d=d, j=j: t.matmul(
                pT[j % 2],
                xin[:, (i % 2) * D + d * 128:(i % 2) * D + (d + 1) * 128],
                G["identf"][:], start=True, stop=True), mmT)
            o.wait("vector", mmT, j + 1)
            o.op("vector", lambda v, i=i, d=d, j=j: v.tensor_copy(
                G["x_bf"][:, d * T + i * 128:d * T + (i + 1) * 128],
                pT[j % 2]), evT)
    o.flush(block)
    ctx.close()
    return evT, evT.v


def phase_out(nc, block, G, bar, barv):
    ctx = ExitStack()
    o = Orch(nc, ctx, G['semreg'], G['semctx'])
    xout = ctx.enter_context(nc.sbuf_tensor("out_stage", [128, 2 * D], F32))
    pO = [G["P1"][:, 0:128], G["P1"][:, 512:640]]
    mmO = o.sem("out_mmO")
    evO = o.sem("out_evO")
    stD = [o.sem("out_stD0"), o.sem("out_stD1")]
    o.wait("tensor", bar, barv)
    o.wait("vector", bar, barv)
    o.wait("sync", bar, barv)
    for i in range(16):
        for d in range(NCH):
            j = 8 * i + d
            if j >= 2:
                o.wait("tensor", evO, j - 1)
            if d == 0 and i >= 2:
                o.wait("tensor", stD[i % 2], 16 * (i // 2))
            src = G.get("dump_t", G["x_bf"])
            o.op("tensor", lambda t, i=i, d=d, j=j, src=src: t.matmul(
                pO[j % 2],
                src[:, d * T + i * 128:d * T + (i + 1) * 128],
                G["identb"][:], start=True, stop=True), mmO)
            o.wait("vector", mmO, j + 1)
            o.op("vector", lambda v, i=i, d=d, j=j: v.tensor_copy(
                xout[:, (i % 2) * D + d * 128:(i % 2) * D + (d + 1) * 128],
                pO[j % 2]), evO)
        o.wait("sync", evO, 8 * (i + 1))
        o.op("sync", lambda s, i=i: s.dma_start(
            out=G["out_h"][i * 128:(i + 1) * 128, :],
            in_=xout[:, (i % 2) * D:(i % 2 + 1) * D]), stD[i % 2], 16)
    o.wait("sync", stD[0], 128)
    o.wait("sync", stD[1], 128)
    o.flush(block)
    ctx.close()


def phase_attention(nc, block, G, cfg, bar, barv):
    name = cfg.name
    ctx = ExitStack()
    o = Orch(nc, ctx, G['semreg'], G['semctx'])
    sb = lambda n, s, d: ctx.enter_context(
        nc.sbuf_tensor(f"{name}S_{n}", s, d))
    ps = lambda n, s: ctx.enter_context(
        nc.psum_tensor(f"{name}P_{n}", s, F32))

    x_bf = G["x_bf"]
    identb = G["identb"]
    hm16 = G["hmask16"]
    selp = G["selpair"]
    masks = sb("maskar", [128, cfg.nmask * 512], BF16)
    mask_h = G[f"mask_{name}_h"]
    w_h = G[f"{name}_w_h"]
    b_h = G[f"{name}_b_h"]
    wo_h = G[f"{name}_wo_h"]
    bo_h = G[f"{name}_bo_h"]

    NKST = 2
    bqkv = sb("bqkv", [128, 24], F32)
    bo_s = sb("bo", [128, 8], F32)
    bq_s = sb("bqs", [128, 8], F32)
    k_pack = sb("kpack", [128, 8 * 2048], BF16)
    vt = sb("vt", [128, 16 * 16 * 65], BF16)
    kst = [sb(f"kst{i}", [65, cfg.kst_w], BF16) for i in range(NKST)]
    den = sb("den", [16, 512], F32)
    den4 = sb("den4", [128, 2048], BF16)
    rden = sb("rden", [16, 512], BF16)
    subctx = ExitStack()
    sb1 = lambda n, s, d: subctx.enter_context(
        nc.sbuf_tensor(f"{name}S_{n}", s, d))
    wkv = sb1("wkv", [128, 8 * 2048], BF16)
    vch = sb1("vch", [128, 8 * 2048], BF16)
    subctx.close()
    wq = sb("wq", [128, 8 * 1024], BF16)
    wo = sb("wo", [128, 8 * 1024], BF16)
    qt = sb("qt", [65, 16 * 512], BF16)
    macc = sb("macc", [128, 16], F32)
    rmn = sb("rmn", [128, 1], F32)
    negm = sb("negm", [16, 512], F32)
    psb = [sb(f"psb{i}", [128, 1024], BF16) for i in range(2)]
    araw = sb("araw", [128, 8 * 512], BF16)
    qp = araw
    rb = [sb(f"rb{i}", [128, 512], BF16) for i in range(2)]
    yb = sb("yb", [128, 8 * 512], BF16)

    P0, P1, P2, P3 = G["P0"], G["P1"], G["P2"], G["P3"]

    dWkv = o.sem("at_dWkv")
    dMK = o.sem("at_dMK")
    dWq = o.sem("at_dWq")
    dWo = o.sem("at_dWo")
    dB = o.sem("at_dB")
    dNM = o.sem("at_dNM")
    vPRE = o.sem("at_vPRE")
    tKV = o.sem("at_tKV")
    sKV = o.sem("at_sKV")
    tVT = o.sem("at_tVT")
    sVT = [o.sem("at_sVTs"), o.sem("at_sVTv")]
    tQ = o.sem("at_tQ")
    sQE = o.sem("at_sQE")
    tMX = o.sem("at_tMX")
    vMX = o.sem("at_vMX")
    sMX = o.sem("at_sMX")
    vKC = o.sem("at_vKC")
    tS = o.sem("at_tS")
    sEX = o.sem("at_sEX")
    vMK = o.sem("at_vMK")
    tPV = o.sem("at_tPV")
    sPV = o.sem("at_sPV")
    vRC = o.sem("at_vRC")
    dDN = o.sem("at_dDN")
    tBC = o.sem("at_tBC")
    sBC = o.sem("at_sBC")
    vNE = o.sem("at_vNE")
    tO = o.sem("at_tO")
    vRS = o.sem("at_vRS")
    LN = G["LN"]
    sLN, tLS, vLS = LN.sLN, LN.tLS, LN.vLS
    sLR, tLB, sLB, vXW = LN.sLR, LN.tLB, LN.sLB, LN.vXW

    kv0 = sKV.v
    tKV0 = tKV.v
    tVT0 = tVT.v
    sVT0 = [sVT[0].v, sVT[1].v]
    sQE0 = sQE.v
    o.wait("gpsimd", bar, barv)
    o.op("gpsimd", lambda g: g.dma_start(
        out=masks[:, 0:cfg.nmask * 512], in_=mask_h), dMK, 16)
    for k in range(8):
        o.op("gpsimd", lambda g, k=k: g.dma_start(
            out=wkv[:, 2048 * k:2048 * (k + 1)],
            in_=w_h[128 * k:128 * (k + 1), 1024:3072]), dWkv, 16)
    o.wait("sync", bar, barv)
    o.op("sync", lambda s: s.dma_start(out=bqkv[:], in_=b_h), dB, 16)
    o.op("sync", lambda s: s.dma_start(out=bo_s[:], in_=bo_h), dB, 16)

    o.wait("vector", bar, barv)
    o.wait("vector", dB, dB.v)
    o.op("vector", lambda v: v.tensor_scalar_mul(
        bq_s[:], bqkv[:, 0:8], float(SCALE)), vPRE)
    o.op("vector", lambda v: v.memset(vt[:, 64:16 * 16 * 65:65], 1.0), vPRE)
    for i in range(NKST):
        o.op("vector", lambda v, i=i: v.memset(kst[i][64:65, :], 1.0), vPRE)

    slots_kv = [(P0, 0), (P0, 512), (P1, 0), (P1, 512)]
    o.wait("tensor", bar, barv)
    o.wait("tensor", dWkv, dWkv.v)
    nj = 0
    o.wait("scalar", dB, dB.v)
    for mi in range(16):
        for n in range(4):
            j = nj
            nj += 1
            Pp, off = slots_kv[j % 4]
            if j >= 4:
                o.wait("tensor", sKV, kv0 + j - 3)
            for k in range(8):
                o.op("tensor", lambda t, Pp=Pp, off=off, mi=mi, n=n, k=k:
                     t.matmul(
                         Pp[:, off:off + 512],
                         wkv[:, 2048 * k + 128 * mi:2048 * k + 128 * (mi + 1)],
                         x_bf[:, T * k + 512 * n:T * k + 512 * (n + 1)],
                         start=(k == 0), stop=(k == 7)),
                     tKV if k == 7 else None)
            o.wait("scalar", tKV, tKV0 + j + 1)
            dst = k_pack if mi < 8 else vch
            c = mi % 8
            o.op("scalar", lambda s, Pp=Pp, off=off, dst=dst, c=c, n=n,
                 mi=mi: s.activation(
                     dst[:, 2048 * c + 512 * n:2048 * c + 512 * (n + 1)],
                     Pp[:, off:off + 512], AF.Identity,
                     bias=bqkv[:, 8 + mi:9 + mi]), sKV)

    slots_vt = [(P0, 0), (P0, 512), (P1, 0), (P1, 512)]
    jj = 0
    for h in range(16):
        c = h // 2
        rbase = (h % 2) * 64
        for tt_ in range(16):
            j = jj
            jj += 1
            Pp, off = slots_vt[j % 4]
            o.wait("tensor", sKV, kv0 + (64 if j < 4 else min(64, (9 + c) * 4)))
            if j >= 4:
                jp = j - 4
                o.wait("tensor", sVT[jp % 2], sVT0[jp % 2] + jp // 2 + 1)
            o.op("tensor", lambda t, Pp=Pp, off=off, c=c, rbase=rbase,
                 tt_=tt_: t.matmul(
                     Pp[:, off:off + 64],
                     vch[rbase:rbase + 64,
                         2048 * c + 128 * tt_:2048 * c + 128 * (tt_ + 1)],
                     identb[rbase:rbase + 64, rbase:rbase + 64],
                     start=True, stop=True), tVT)
            eng = "scalar" if (j % 2 == 0) else "vector"
            o.wait(eng, tVT, tVT0 + j + 1)
            col = (h * 16 + tt_) * 65
            if eng == "scalar":
                o.op(eng, lambda e, Pp=Pp, off=off, col=col: e.copy(
                    vt[:, col:col + 64], Pp[:, off:off + 64]), sVT[0])
            else:
                o.op(eng, lambda e, Pp=Pp, off=off, col=col: e.tensor_copy(
                    vt[:, col:col + 64], Pp[:, off:off + 64]), sVT[1])

    o.wait("gpsimd", tVT, tVT.v)
    for k in range(8):
        o.op("gpsimd", lambda g, k=k: g.dma_start(
            out=wq[:, 1024 * k:1024 * (k + 1)],
            in_=w_h[128 * k:128 * (k + 1), 0:1024]), dWq, 16)
    for k in range(8):
        o.op("gpsimd", lambda g, k=k: g.dma_start(
            out=wo[:, 1024 * k:1024 * (k + 1)],
            in_=wo_h[128 * k:128 * (k + 1), :]), dWo, 16)

    ts_after_head = []
    vmx_after = []
    mxstep = [0]
    pv_after_group = []
    ex_after_group = []
    gg = 0
    first_pv = True

    for qb in range(NQB):
        qs = qb * QB
        w0, w1 = cfg.window(qb)
        ntile = (w1 - w0) // 128

        if qb > 0:
            o.wait("tensor", vXW, vXW.v)
            o.wait("tensor", sLB, sLB.v)

        for m in range(8):
            mg = qb * 8 + m
            Pp, off = (P0, 0) if m % 2 == 0 else (P0, 512)
            if mg == 0:
                o.wait("tensor", sKV, kv0 + 64)
                o.wait("tensor", dWq, dWq.v)
                o.wait("tensor", sVT[0], sVT[0].v)
                o.wait("tensor", sVT[1], sVT[1].v)
            if mg >= 2:
                o.wait("tensor", sQE, sQE0 + 3 * (mg - 1))
            for k in range(8):
                o.op("tensor", lambda t, Pp=Pp, off=off, m=m, k=k, qs=qs:
                     t.matmul(
                         Pp[:, off:off + 512],
                         wq[:, 1024 * k + 128 * m:1024 * k + 128 * (m + 1)],
                         x_bf[:, T * k + qs:T * k + qs + QB],
                         start=(k == 0), stop=(k == 7)), tQ)
            if m == 0:
                o.wait("scalar", vPRE, 1)
                o.wait("scalar", tS, tS.v)
                o.wait("scalar", tMX, tMX.v)
                o.wait("scalar", tO, tO.v)
                o.wait("scalar", vNE, vNE.v)
            o.wait("scalar", tQ, tQ.v)
            for half in range(2):
                hh = 2 * m + half
                o.op("scalar", lambda s, Pp=Pp, off=off, half=half, hh=hh,
                     m=m: s.activation(
                         qt[0:64, 512 * hh:512 * (hh + 1)],
                         Pp[64 * half:64 * (half + 1), off:off + 512],
                         AF.Identity, scale=float(SCALE),
                         bias=bq_s[64 * half:64 * (half + 1), m:m + 1]), sQE)
            o.op("scalar", lambda s, Pp=Pp, off=off, m=m: s.activation(
                qp[:, 512 * m:512 * (m + 1)], Pp[:, off:off + 512],
                AF.Identity, scale=float(SCALE),
                bias=bq_s[:, m:m + 1]), sQE)

        nchunk = (w1 - w0 + 511) // 512
        mslots = [(P2, 0), (P2, 512), (P3, 0), (P3, 512)]
        for tqs in range(4):
            for hp in range(8):
                o.wait("tensor", sQE, sQE.v)
                for tkc in range(nchunk):
                    st = mxstep[0]
                    mxstep[0] += 1
                    sl = mslots[(st % 2) * 2:(st % 2) * 2 + 2]
                    if st >= 2:
                        o.wait("tensor", vMX, vmx_after[st - 2])
                    tw = min(512, w1 - w0 - 512 * tkc)
                    for hh in range(2):
                        Pp, off = sl[hh]
                        o.op("tensor", lambda t, Pp=Pp, off=off, hp=hp,
                             hh=hh, tqs=tqs, tkc=tkc, tw=tw, qs=qs, w0=w0:
                             t.matmul(
                                 Pp[:, off:off + tw],
                                 qp[64 * hh:64 * (hh + 1),
                                    512 * hp + 128 * tqs:512 * hp
                                    + 128 * (tqs + 1)],
                                 k_pack[64 * hh:64 * (hh + 1),
                                        2048 * hp + w0 + 512 * tkc:
                                        2048 * hp + w0 + 512 * tkc + tw],
                                 start=True, stop=True,
                                 tile_position=(64 * hh, 0)), tMX)
                    o.wait("vector", tMX, tMX.v)
                    for hh in range(2):
                        Pp, off = sl[hh]
                        if tkc == 0:
                            o.fop("vector", lambda v, Pp=Pp, off=off, hp=hp,
                                  hh=hh, tw=tw: v.reduce_max(
                                      macc[:, 2 * hp + hh:2 * hp + hh + 1],
                                      Pp[:, off:off + tw],
                                      axis=mybir.AxisListType.X), vMX)
                        else:
                            o.fop("vector", lambda v, Pp=Pp, off=off, tw=tw:
                                  v.reduce_max(
                                      rmn[:], Pp[:, off:off + tw],
                                      axis=mybir.AxisListType.X), vMX)
                            o.fop("vector", lambda v, hp=hp, hh=hh:
                                  v.tensor_max(
                                      macc[:, 2 * hp + hh:2 * hp + hh + 1],
                                      macc[:, 2 * hp + hh:2 * hp + hh + 1],
                                      rmn[:]), vMX)
                    vmx_after.append(vMX.v)
            o.wait("tensor", vMX, vMX.v)
            o.wait("tensor", sMX, sMX.v)
            o.op("tensor", lambda t, tqs=tqs: t.matmul(
                P1[0:16, 0:128], macc[:], G["identfn"][:, 0:128],
                start=True, stop=True), tMX)
            o.wait("scalar", tMX, tMX.v)
            o.op("scalar", lambda s, tqs=tqs: s.copy(
                negm[:, 128 * tqs:128 * (tqs + 1)], P1[0:16, 0:128]), sMX)
        o.wait("gpsimd", sMX, sMX.v)
        if qb > 0:
            o.wait("gpsimd", tS, tS.v)
        o.op("gpsimd", lambda g: g.dma_start(
            out=qt[64:65, :], in_=negm[:]), dNM, 16)

        for h in range(16):
            hg = qb * 16 + h
            if hg >= NKST:
                o.wait("vector", tS, ts_after_head[hg - NKST])
            o.op("vector", lambda v, h=h, w0=w0, w1=w1, hg=hg: v.tensor_copy(
                kst[hg % NKST][0:64, 0:w1 - w0],
                k_pack[(h % 2) * 64:(h % 2) * 64 + 64,
                       2048 * (h // 2) + w0:2048 * (h // 2) + w1]), vKC)
            ngrp = (ntile + 1) // 2
            for g in range(ngrp):
                t0 = 2 * g
                tiles = [t0] if t0 + 1 >= ntile else [t0, t0 + 1]
                Pg = P2 if gg % 2 == 0 else P3
                pb = psb[gg % 2]
                o.wait("tensor", vKC, vKC.v)
                o.wait("tensor", dNM, dNM.v)
                o.wait("tensor", vMX, vMX.v)
                if gg >= 2:
                    o.wait("tensor", sEX, ex_after_group[gg - 2])

                for ti in tiles:
                    o.op("tensor", lambda t, Pg=Pg, ti=ti, t0=t0, h=h,
                         hg=hg: t.matmul(
                             Pg[:, 512 * (ti - t0):512 * (ti - t0 + 1)],
                             kst[hg % NKST][0:65, 128 * ti:128 * (ti + 1)],
                             qt[0:65, 512 * h:512 * (h + 1)],
                             start=True, stop=True), tS)
                o.wait("scalar", tS, tS.v)
                if gg >= 2:
                    o.wait("scalar", tPV, pv_after_group[gg - 2])
                nel = 512 * len(tiles)
                o.op("scalar", lambda s, Pg=Pg, pb=pb, nel=nel: s.activation(
                    pb[:, 0:nel], Pg[:, 0:nel], AF.Exp), sEX)
                ex_after_group.append(sEX.v)
                need_mask = []
                for ti in tiles:
                    midx = cfg.maskidx(qb, w0 + 128 * ti)
                    if midx is not None:
                        need_mask.append((ti - t0, midx))
                if need_mask:
                    o.wait("vector", sEX, sEX.v)
                    o.wait("vector", dMK, dMK.v)
                    for rel, midx in need_mask:
                        o.op("vector", lambda v, pb=pb, rel=rel, midx=midx:
                             v.tensor_mul(
                                 pb[:, 512 * rel:512 * (rel + 1)],
                                 pb[:, 512 * rel:512 * (rel + 1)],
                                 masks[:, 512 * midx:512 * (midx + 1)]), vMK)
                o.wait("tensor", sEX, sEX.v)
                if need_mask:
                    o.wait("tensor", vMK, vMK.v)
                if first_pv:
                    o.wait("tensor", vPRE, vPRE.v)
                    first_pv = False
                if g == 0 and hg > 0:
                    o.wait("tensor", sPV, sPV.v)
                for idx, ti in enumerate(tiles):
                    gt = (w0 + 128 * ti) // 128
                    o.op("tensor", lambda t, pb=pb, idx=idx, gt=gt, h=h,
                         first=(g == 0 and idx == 0),
                         last=(ti == ntile - 1): t.matmul(
                             P1[0:65, 512:1024],
                             vt[:, (h * 16 + gt) * 65:(h * 16 + gt) * 65 + 65],
                             pb[:, 512 * idx:512 * (idx + 1)],
                             start=first, stop=last), tPV)
                pv_after_group.append(tPV.v)
                gg += 1
            ts_after_head.append(tS.v)
            o.wait("scalar", tPV, tPV.v)
            o.wait("scalar", tMX, tMX.v)
            if hg == 0:
                o.wait("scalar", vPRE, vPRE.v)
            else:
                o.wait("scalar", tO, tO.v)
            o.op("scalar", lambda s, h=h: s.copy(
                araw[(h % 2) * 64:(h % 2) * 64 + 64,
                     512 * (h // 2):512 * (h // 2) + 512],
                P1[0:64, 512:1024]), sPV)
            o.op("scalar", lambda s, h=h: s.copy(
                den4[32 * (h // 4):32 * (h // 4) + 1,
                     512 * (h % 4):512 * (h % 4) + 512],
                P1[64:65, 512:1024]), sPV)

        o.wait("sync", sPV, sPV.v)
        o.wait("sync", vRC, vRC.v)
        o.op("sync", lambda sy: sy.dma_start(
            out=rden[:],
            in_=den4[0:128:32, :].rearrange("p (c e) -> p c e", e=512)),
            dDN, 16)
        o.wait("vector", dDN, dDN.v)
        o.fop("vector", lambda v: v.reciprocal(den[:], rden[:]))
        o.op("vector", lambda v: v.tensor_copy(rden[:], den[:]), vRC)
        for p in range(8):
            Pp, off = (P2, 0) if p % 2 == 0 else (P2, 512)
            o.wait("tensor", vRC, vRC.v)
            if p < 2:
                o.wait("tensor", sEX, sEX.v)
            else:
                o.wait("tensor", sBC, sBC.v - 1)
                o.wait("tensor", vNE, vNE.v - 1)
            o.op("tensor", lambda t, Pp=Pp, off=off, p=p: t.matmul(
                Pp[:, off:off + 512], selp[:, 128 * p:128 * (p + 1)],
                rden[:], start=True, stop=True), tBC)
            o.wait("scalar", tBC, tBC.v)
            o.op("scalar", lambda s, Pp=Pp, off=off, p=p: s.copy(
                rb[p % 2][:], Pp[:, off:off + 512]), sBC)
            o.wait("vector", sBC, sBC.v)
            o.op("vector", lambda v, p=p: v.tensor_mul(
                araw[:, 512 * p:512 * (p + 1)],
                araw[:, 512 * p:512 * (p + 1)], rb[p % 2][:]), vNE)

        o.wait("tensor", dWo, dWo.v)
        for m in range(8):
            Pp, off = (P3, 0) if m % 2 == 0 else (P3, 512)
            if m < 2:
                o.wait("tensor", sEX, sEX.v)
            else:
                o.wait("tensor", vRS, vRS.v - 1)
            o.wait("tensor", vNE, vNE.v)
            for k in range(8):
                o.op("tensor", lambda t, Pp=Pp, off=off, m=m, k=k: t.matmul(
                    Pp[:, off:off + 512],
                    wo[:, 1024 * k + 128 * m:1024 * k + 128 * (m + 1)],
                    araw[:, 512 * k:512 * (k + 1)],
                    start=(k == 0), stop=(k == 7)), tO)
            o.wait("vector", tO, tO.v)
            if m == 0:
                o.wait("vector", sLN, sLN.v)
            o.op("vector", lambda v, Pp=Pp, off=off, m=m, qs=qs:
                 v.scalar_tensor_tensor(
                     yb[:, 512 * m:512 * (m + 1)], Pp[:, off:off + 512],
                     bo_s[:, m:m + 1], x_bf[:, T * m + qs:T * m + qs + QB],
                     ALU.add, ALU.add), vRS)

        o.wait("scalar", vRS, vRS.v)
        o.wait("scalar", vXW, vXW.v)
        o.wait("tensor", sBC, sBC.v)
        o.wait("tensor", vRS, vRS.v)
        o.wait("vector", vXW, vXW.v)
        LN.emit(o, G, P2, P3, yb, qs)

    o.flush(block)
    ctx.close()
    return vXW, vXW.v



def emit_quant(o, eng_v, eng_s, out_ap, in_ap, rs_ap, Q, sem,
               clip=False):
    qv, qi, qf, qg, vq, sc = Q.qv, Q.qi, Q.qf, Q.qg, Q.vq, Q.sc
    w = out_ap.shape[-1]
    if Q.last is not None:
        o.wait(eng_v, Q.last[0], Q.last[1])
    o.wait(eng_v, sc, sc.v)
    o.op(eng_v, lambda v: v.tensor_scalar(
        qv[:, 0:w], in_ap, rs_ap, 0.5, ALU.mult, ALU.add), vq)
    o.wait(eng_s, vq, vq.v)
    o.fop(eng_s, lambda s: s.copy(qi[:, 0:w], qv[:, 0:w]))
    o.op(eng_s, lambda s: s.copy(qf[:, 0:w], qi[:, 0:w]), sc)
    o.wait(eng_v, sc, sc.v)
    o.fop(eng_v, lambda v: v.tensor_tensor(
        qg[:, 0:w], qf[:, 0:w], qv[:, 0:w], ALU.is_gt))
    if clip:
        o.fop(eng_v, lambda v: v.scalar_tensor_tensor(
            qv[:, 0:w], qg[:, 0:w], -1.0, qf[:, 0:w], ALU.mult, ALU.add))
        fin = sem if sem is not None else vq
        o.op(eng_v, lambda v: v.tensor_scalar(
            out_ap, qv[:, 0:w], -128.0, 127.0, ALU.max, ALU.min), fin)
    else:
        fin = sem if sem is not None else vq
        o.op(eng_v, lambda v: v.scalar_tensor_tensor(
            out_ap, qg[:, 0:w], -1.0, qf[:, 0:w], ALU.mult, ALU.add), fin)
    Q.last = (fin, fin.v)


class QPack:
    def __init__(self, qv, qi, qf, qg, vq, sc):
        self.qv, self.qi, self.qf, self.qg = qv, qi, qf, qg
        self.vq, self.sc = vq, sc
        self.last = None


def phase_ffn0(nc, block, G, bar, barv):
    name = "f0"
    ctx = ExitStack()
    o = Orch(nc, ctx, G['semreg'], G['semctx'])
    sb = lambda n, s, d: ctx.enter_context(
        nc.sbuf_tensor(f"{name}S_{n}", s, d))
    ps = lambda n, s: ctx.enter_context(
        nc.psum_tensor(f"{name}P_{n}", s, F32))
    x_bf = G["x_bf"]
    LN = G["LN"]
    NI = 22

    g_sb = sb("g", [128, NI * 2048], BF16)
    qv = sb("qv", [128, 512], F32)
    qi = sb("qi", [128, 512], I32)
    qf = sb("qf", [128, 512], BF16)
    qg = sb("qg", [128, 512], BF16)
    sgx = sb("sgx", [128, NI], F32)
    sux = sb("sux", [128, NI], F32)
    rsc = sb("rsc", [128, 4], F32)
    rmx = sb("rmx", [128, 1], F32)
    xm8 = sb("xm8", [128, 8], F32)
    gmA = sb("gmA", [128, 1], F32)
    gmT = sb("gmT", [128, 1], F32)
    sc1 = sb("sc1", [1, 16], F32)
    scb = sb("scb", [128, 4], F32)
    hh = [sb(f"hh{i}", [128, 512], BF16) for i in range(2)]
    sub = ExitStack()
    sb1 = lambda n, s, d: sub.enter_context(
        nc.sbuf_tensor(f"{name}S_{n}", s, d))
    xh = sb1("xh", [128, 8 * 2048], BF16)
    wnat = [sb1(f"wnat{i}", [128, 1024], F32) for i in range(2)]
    wqn = [sb1(f"wqn{i}", [128, 1024], BF16) for i in range(2)]
    wT = [sb1(f"wT{i}", [128, 8 * 128], BF16) for i in range(4)]
    sil = [sb1(f"sil{i}", [128, 512], F32) for i in range(2)]
    sub.close()
    wdT = sb("wdT", [128, NI * 1024], BF16)
    wdn = sb("wdn", [128, 1408], F32)
    wdq = sb("wdq", [128, 1024], BF16)
    yb = sb("yb", [128, 8 * 512], BF16)
    sdh = sb("sdh", [128, 8], F32)

    P0, P1, P2, P3 = G["P0"], G["P1"], G["P2"], G["P3"]

    wg_h = G["f0_wg_h"]
    wu_h = G["f0_wu_h"]
    wd_h = G["f0_wd_h"]
    clipx_h = G["f0_clipx_h"]
    clipd_h = G["f0_clipd_h"]
    cc0_in, cc0_out = G["cc0_in"], G["cc0_out"]
    cc1_in, cc1_out = G["cc1_in"], G["cc1_out"]

    dL = o.sem("at_dL")
    dCC = o.sem("at_dCC")
    cc = o.sem("at_cc")
    dBC = o.sem("at_dBC")
    vST = o.sem("at_vST")
    sST = o.sem("at_sST")
    tST = o.sem("at_tST")
    vXH = o.sem("at_vXH")
    vWQ = o.sem("at_vWQ")
    tWT = o.sem("at_tWT")
    sWT = o.sem("at_sWT")
    tGU = o.sem("at_tGU")
    sSI = o.sem("at_sSI")
    vG = o.sem("at_vG")
    vGM = o.sem("at_vGM")
    tDN = o.sem("at_tDN")
    vHQ = o.sem("at_vHQ")
    vRS = o.sem("at_vRS")
    vq_s = o.sem("at_vq")
    sc_s = o.sem("at_sc")
    Q = QPack(qv, qi, qf, qg, vq_s, sc_s)
    vXW = LN.vXW

    o.wait("vector", bar, barv)
    for c in range(8):
        (o.fop if c == 7 else o.op)("vector", lambda v, c=c: v.reduce_max(
            xm8[:, c:c + 1], x_bf[:, T * c:T * (c + 1)],
            axis=mybir.AxisListType.X, apply_absolute_value=True))
    o.op("vector", lambda v: v.reduce_max(
        rmx[:], xm8[:], axis=mybir.AxisListType.X), vST)
    o.wait("tensor", bar, barv)
    o.wait("tensor", vST, vST.v)
    o.op("tensor", lambda t: t.matmul(
        P0[0:1, 0:128], rmx[:], G["identf"][:], start=True, stop=True), tST)
    o.wait("vector", tST, tST.v)
    o.op("vector", lambda v: v.reduce_max(
        sc1[0:1, 0:1], P0[0:1, 0:128], axis=mybir.AxisListType.X), vST)
    o.wait("sync", bar, barv)
    o.wait("sync", vST, vST.v)
    o.op("sync", lambda sy: sy.dma_start(out=cc0_in, in_=sc1[0:1, 0:1]),
         dCC, 16)
    o.wait("gpsimd", bar, barv)
    o.wait("gpsimd", dCC, dCC.v)
    o.op("gpsimd", lambda g: g.collective_compute(
        "AllGather", ALU.bypass,
        replica_groups=[[0, 1, 2, 3, 4, 5, 6, 7]],
        ins=[cc0_in], outs=[cc0_out]), cc)
    o.wait("sync", cc, cc.v)
    o.op("sync", lambda sy: sy.dma_start(out=sc1[0:1, 8:16], in_=cc0_out),
         dCC, 16)
    o.op("sync", lambda sy: sy.dma_start(
        out=sc1[0:1, 1:2], in_=clipx_h), dCC, 16)
    o.op("sync", lambda sy: sy.dma_start(
        out=sc1[0:1, 2:3], in_=clipd_h), dCC, 16)
    o.wait("vector", dCC, dCC.v)
    o.fop("vector", lambda v: v.reduce_max(
        sc1[0:1, 3:4], sc1[0:1, 8:16], axis=mybir.AxisListType.X), vST)
    o.wait("scalar", bar, barv)
    o.wait("scalar", dCC, dCC.v)
    o.op("scalar", lambda s: s.activation(
        sc1[0:1, 4:5], sc1[0:1, 1:2], AF.Sigmoid), sST)
    o.wait("vector", sST, sST.v)
    o.fop("vector", lambda v: v.tensor_mul(
        sc1[0:1, 5:6], sc1[0:1, 4:5], sc1[0:1, 3:4]))
    o.fop("vector", lambda v: v.tensor_scalar_mul(
        sc1[0:1, 5:6], sc1[0:1, 5:6], 1.0 / 127.0))
    o.fop("vector", lambda v: v.reciprocal(
        sc1[0:1, 6:7], sc1[0:1, 5:6]), vST)
    o.wait("sync", vST, vST.v)
    o.op("sync", lambda sy: sy.dma_start(
        out=scb[:, 0:1], in_=sc1[0:1, 5:6].to_broadcast([1, 128, 1])),
        dBC, 16)
    o.op("sync", lambda sy: sy.dma_start(
        out=scb[:, 1:2], in_=sc1[0:1, 6:7].to_broadcast([1, 128, 1])),
        dBC, 16)
    o.wait("vector", dBC, dBC.v)
    for c in range(8):
        for n in range(4):
            emit_quant(o, "vector", "scalar",
                       xh[:, 2048 * c + 512 * n:2048 * c + 512 * (n + 1)],
                       x_bf[:, T * c + 512 * n:T * c + 512 * (n + 1)],
                       scb[:, 1:2], Q, vXH, clip=True)

    if DUMP == "xh":
        G["dump_t"] = xh
        o.wait("vector", dBC, dBC.v)
        o.op("vector", lambda v: v.tensor_copy(xh[0:1, 16352:16368],
                                               sc1[0:1, 0:16]))
        o.op("vector", lambda v: v.tensor_copy(xh[:, 16368:16372],
                                               scb[:, 0:4]), vXH)
    if DUMP == "g":
        G["dump_t"] = g_sb
    if F0STEP < 2:
        o.flush(block)
        ctx.close()
        return vXH, vXH.v
    o.wait("sync", bar, barv)
    gslots = [(P0, 0), (P0, 512), (P1, 0), (P1, 512)]
    uslots = [(P2, 0), (P2, 512), (P3, 0), (P3, 512)]
    silu_after = {}
    g_after = {}
    wq_after = []
    g_after_gidx = []
    for i in range(NI):
        bw = i % 2
        if i >= 1:
            o.wait("sync", vWQ, wq_after[i - 1])
        o.op("sync", lambda sy, i=i: sy.dma_start(
            out=wnat[0][:], in_=wg_h[128 * i:128 * (i + 1), :]), dL, 16)
        o.op("sync", lambda sy, i=i: sy.dma_start(
            out=wnat[1][:], in_=wu_h[128 * i:128 * (i + 1), :]), dL, 16)
        o.wait("vector", dL, dL.v)
        for wi, sall in enumerate((sgx, sux)):
            wn = wnat[wi]
            wq_ = wqn[wi]
            o.fop("vector", lambda v, wn=wn: v.reduce_max(
                rmx[:], wn[:], axis=mybir.AxisListType.X,
                apply_absolute_value=True))
            o.fop("vector", lambda v, sall=sall, i=i: v.tensor_scalar_mul(
                sall[:, i:i + 1], rmx[:], 1.0 / 127.0))
            rcol = rsc[:, wi:wi + 1]
            o.fop("vector", lambda v, rcol=rcol, sall=sall, i=i:
                 v.reciprocal(rcol, sall[:, i:i + 1]))
            if i >= 1:
                o.wait("vector", tWT, 16 * i)
            for n2 in range(2):
                emit_quant(o, "vector", "scalar",
                           wq_[:, 512 * n2:512 * (n2 + 1)],
                           wn[:, 512 * n2:512 * (n2 + 1)],
                           rcol, Q, vWQ if n2 == 1 else None)
            o.op("vector", lambda v, sall=sall, i=i: v.tensor_scalar(
                sall[:, i:i + 1], sall[:, i:i + 1], scb[:, 0:1], None,
                ALU.mult), vWQ)
        wq_after.append(vWQ.v)
        o.wait("tensor", vWQ, vWQ.v)
        if i >= 1:
            o.wait("tensor", sSI, silu_after[(i - 1, 3)])
            o.wait("tensor", vG, g_after[(i - 1, 3)])
        if i == 0:
            o.wait("tensor", vST, 2)
        for wi in range(2):
            wq_ = wqn[wi]
            wt_ = wT[2 * wi + bw]
            for hc in range(8):
                pslot = P0 if wi == 0 else P1
                poff = 0 if hc % 2 == 0 else 512
                jj = i * 16 + wi * 8 + hc
                if jj >= 2:
                    o.wait("tensor", sWT, jj - 1)
                o.op("tensor", lambda t, wq_=wq_, hc=hc, pslot=pslot,
                     poff=poff: t.matmul(
                         pslot[:, poff:poff + 128],
                         wq_[:, 128 * hc:128 * (hc + 1)], G["identb"][:],
                         start=True, stop=True), tWT)
                o.wait("scalar", tWT, tWT.v)
                o.op("scalar", lambda s, wt_=wt_, hc=hc, pslot=pslot,
                     poff=poff: s.copy(
                         wt_[:, 128 * hc:128 * (hc + 1)],
                         pslot[:, poff:poff + 128]), sWT)
        o.wait("tensor", sWT, sWT.v)
        if i == 0:
            o.wait("tensor", vXH, 32)
        for n in range(4):
            gP, go = gslots[n]
            uP, uo = uslots[n]
            if i >= 1:
                o.wait("tensor", sSI, silu_after[(i - 1, n)])
                o.wait("tensor", vG, g_after[(i - 1, n)])
            for hc in range(8):
                o.op("tensor", lambda t, gP=gP, go=go, bw=bw, hc=hc, n=n:
                     t.matmul(
                         gP[:, go:go + 512],
                         wT[0 + bw][:, 128 * hc:128 * (hc + 1)],
                         xh[:, 2048 * hc + 512 * n:2048 * hc + 512 * (n + 1)],
                         start=(hc == 0), stop=(hc == 7)), None)
                o.op("tensor", lambda t, uP=uP, uo=uo, bw=bw, hc=hc, n=n:
                     t.matmul(
                         uP[:, uo:uo + 512],
                         wT[2 + bw][:, 128 * hc:128 * (hc + 1)],
                         xh[:, 2048 * hc + 512 * n:2048 * hc + 512 * (n + 1)],
                         start=(hc == 0), stop=(hc == 7)),
                     tGU if hc == 7 else None)
            o.wait("scalar", tGU, tGU.v)
            gidx = 4 * i + n
            if gidx >= 2:
                o.wait("scalar", vG, g_after_gidx[gidx - 2])
            o.op("scalar", lambda s, gP=gP, go=go, i=i, n=n: s.activation(
                sil[n % 2][:], gP[:, go:go + 512], AF.Silu,
                scale=sgx[:, i:i + 1]), sSI)
            silu_after[(i, n)] = sSI.v
            o.wait("vector", sSI, sSI.v)
            o.op("vector", lambda v, uP=uP, uo=uo, i=i, n=n:
                 v.scalar_tensor_tensor(
                     g_sb[:, 2048 * i + 512 * n:2048 * i + 512 * (n + 1)],
                     uP[:, uo:uo + 512], sux[:, i:i + 1], sil[n % 2][:],
                     ALU.mult, ALU.mult), vG)
            g_after[(i, n)] = vG.v
            g_after_gidx.append(vG.v)
        o.wait("vector", vG, vG.v)
        o.fop("vector", lambda v, i=i: v.reduce_max(
            gmT[:], g_sb[:, 2048 * i:2048 * (i + 1)],
            axis=mybir.AxisListType.X, apply_absolute_value=True))
        if i == 0:
            o.fop("vector", lambda v: v.tensor_copy(gmA[:], gmT[:]))
        elif i < NI - 1:
            o.fop("vector", lambda v: v.tensor_max(gmA[:], gmA[:], gmT[:]))
        else:
            o.fop("vector", lambda v: v.tensor_max(
                gmA[:], gmA[:], gmT[:]), vGM)

    if DUMP == "wt":
        G["dump_t"] = g_sb
        o.wait("vector", vG, vG.v)
        o.op("vector", lambda v: v.tensor_copy(g_sb[:, 0:1024], wT[0][:]))
        o.op("vector", lambda v: v.tensor_copy(g_sb[:, 1024:2048],
                                               wqn[0][:]))
        o.op("vector", lambda v: v.tensor_copy(g_sb[:, 2048:2070],
                                               sgx[:, 0:22]))
        o.op("vector", lambda v: v.tensor_copy(g_sb[:, 2070:2092],
                                               sux[:, 0:22]), vG)
    if F0STEP < 3:
        o.flush(block)
        ctx.close()
        return vG, vG.v
    o.wait("tensor", sSI, sSI.v)
    o.wait("tensor", vG, vG.v)
    o.wait("tensor", vGM, 1)
    o.op("tensor", lambda t: t.matmul(
        P0[0:1, 0:128], gmA[:], G["identf"][:], start=True, stop=True), tST)
    o.wait("vector", tST, tST.v)
    o.op("vector", lambda v: v.reduce_max(
        sc1[0:1, 9:10], P0[0:1, 0:128], axis=mybir.AxisListType.X), vST)
    o.wait("sync", vST, vST.v)
    o.op("sync", lambda sy: sy.dma_start(out=cc1_in, in_=sc1[0:1, 9:10]),
         dCC, 16)
    o.wait("gpsimd", dCC, dCC.v)
    o.op("gpsimd", lambda g: g.collective_compute(
        "AllGather", ALU.bypass,
        replica_groups=[[0, 1, 2, 3, 4, 5, 6, 7]],
        ins=[cc1_in], outs=[cc1_out]), cc)
    o.wait("sync", cc, cc.v)
    o.op("sync", lambda sy: sy.dma_start(out=sc1[0:1, 8:16], in_=cc1_out),
         dCC, 16)
    o.wait("vector", dCC, dCC.v)
    o.fop("vector", lambda v: v.reduce_max(
        sc1[0:1, 10:11], sc1[0:1, 8:16], axis=mybir.AxisListType.X), vST)
    o.wait("scalar", vST, vST.v)
    o.op("scalar", lambda s: s.activation(
        sc1[0:1, 11:12], sc1[0:1, 2:3], AF.Sigmoid), sST)
    o.wait("vector", sST, sST.v)
    o.fop("vector", lambda v: v.tensor_mul(
        sc1[0:1, 12:13], sc1[0:1, 11:12], sc1[0:1, 10:11]))
    o.fop("vector", lambda v: v.tensor_scalar_mul(
        sc1[0:1, 12:13], sc1[0:1, 12:13], 1.0 / 127.0))
    o.fop("vector", lambda v: v.reciprocal(
        sc1[0:1, 13:14], sc1[0:1, 12:13]), vST)
    o.wait("sync", vST, vST.v)
    o.op("sync", lambda sy: sy.dma_start(
        out=scb[:, 2:3], in_=sc1[0:1, 12:13].to_broadcast([1, 128, 1])),
        dBC, 16)
    o.op("sync", lambda sy: sy.dma_start(
        out=scb[:, 3:4], in_=sc1[0:1, 13:14].to_broadcast([1, 128, 1])),
        dBC, 16)

    o.wait("sync", tGU, tGU.v)
    o.wait("sync", vG, vG.v)
    segw = [1024, 1024, 768]
    for c in range(8):
        o.wait("sync", vST, vST.v)
        o.wait("sync", vWQ, vWQ.v)
        o.op("sync", lambda sy, c=c: sy.dma_start(
            out=wdn[:], in_=wd_h[128 * c:128 * (c + 1), 0:1408]), dL, 16)
        o.wait("vector", dL, dL.v)
        o.fop("vector", lambda v: v.reduce_max(
            rmx[:], wdn[:], axis=mybir.AxisListType.X,
            apply_absolute_value=True), vST)
        o.wait("sync", vST, vST.v)
        o.op("sync", lambda sy, c=c: sy.dma_start(
            out=wdn[:], in_=wd_h[128 * c:128 * (c + 1), 1408:2816]), dL, 16)
        o.wait("vector", dL, dL.v)
        o.fop("vector", lambda v: v.reduce_max(
            gmT[:], wdn[:], axis=mybir.AxisListType.X,
            apply_absolute_value=True))
        o.fop("vector", lambda v: v.tensor_max(rmx[:], rmx[:], gmT[:]))
        o.fop("vector", lambda v, c=c: v.tensor_scalar_mul(
            sdh[:, c:c + 1], rmx[:], 1.0 / 127.0))
        o.fop("vector", lambda v, c=c: v.reciprocal(
            rsc[:, 0:1], sdh[:, c:c + 1]), vST)
        for seg in range(3):
            sw = segw[seg]
            s0 = 1024 * seg
            o.wait("sync", vWQ, vWQ.v)
            o.wait("sync", vST, vST.v)
            o.op("sync", lambda sy, c=c, s0=s0, sw=sw: sy.dma_start(
                out=wdn[:, 0:sw], in_=wd_h[128 * c:128 * (c + 1),
                                           s0:s0 + sw]), dL, 16)
            o.wait("vector", dL, dL.v)
            o.wait("vector", tWT, tWT.v)
            for n3 in range(2):
                w0c = 512 * n3
                w1c = min(sw, w0c + 512)
                if w0c >= sw:
                    continue
                emit_quant(o, "vector", "scalar",
                           wdq[:, w0c:w1c], wdn[:, w0c:w1c], rsc[:, 0:1],
                           Q, vWQ)
            o.wait("tensor", vWQ, vWQ.v)
            if c == 0 and seg == 0:
                o.wait("tensor", vST, vST.v)
            nic = sw // 128
            for ic2 in range(nic):
                ic = seg * 8 + ic2
                jj = c * NI + ic
                slot = jj % 8
                Pp = (P0, P1, P2, P3)[slot // 2]
                poff = (slot % 2) * 512
                if jj >= 8:
                    o.wait("tensor", sWT, sWT.v - 7)
                o.op("tensor", lambda t, ic2=ic2, Pp=Pp, poff=poff: t.matmul(
                    Pp[:, poff:poff + 128],
                    wdq[:, 128 * ic2:128 * (ic2 + 1)], G["identb"][:],
                    start=True, stop=True), tWT)
                o.wait("scalar", tWT, tWT.v)
                o.op("scalar", lambda s, ic=ic, c=c, Pp=Pp, poff=poff: s.copy(
                    wdT[:, 1024 * ic + 128 * c:1024 * ic + 128 * (c + 1)],
                    Pp[:, poff:poff + 128]), sWT)
    o.wait("vector", dBC, dBC.v)
    o.fop("vector", lambda v: v.tensor_scalar(
        sdh[:], sdh[:], scb[:, 2:3], None, ALU.mult), vST)

    if F0STEP < 4:
        o.flush(block)
        ctx.close()
        return vST, vST.v
    dslots = [(P0, 0), (P0, 512), (P1, 0), (P1, 512),
              (P2, 0), (P2, 512), (P3, 0), (P3, 512)]
    for tc in range(4):
        qs = 512 * tc
        for ic in range(NI):
            b = ic % 2
            if ic >= 2:
                o.wait("vector", tDN, tDN.v - 8)
            if ic == 0:
                o.wait("vector", vST, vST.v)
            emit_quant(o, "vector", "scalar",
                       hh[b][:], g_sb[:, 2048 * ic + qs:2048 * ic + qs + 512],
                       scb[:, 3:4], Q, vHQ, clip=True)
            o.wait("tensor", vHQ, vHQ.v)
            if ic == 0:
                o.wait("tensor", sWT, sWT.v)
                o.wait("tensor", vRS, vRS.v)
                o.wait("tensor", LN.vLS, LN.vLS.v)
                o.wait("tensor", LN.sLB, LN.sLB.v)
            for dm in range(8):
                Pp, poff = dslots[dm]
                o.op("tensor", lambda t, ic=ic, dm=dm, b=b, Pp=Pp, poff=poff:
                     t.matmul(
                         Pp[:, poff:poff + 512],
                         wdT[:, 1024 * ic + 128 * dm:
                             1024 * ic + 128 * (dm + 1)],
                         hh[b][:], start=(ic == 0), stop=(ic == NI - 1)),
                     tDN)
        o.wait("vector", tDN, tDN.v)
        o.wait("vector", vXW, vXW.v)
        for dm in range(8):
            Pp, poff = dslots[dm]
            o.op("vector", lambda v, dm=dm, Pp=Pp, poff=poff, qs=qs:
                 v.scalar_tensor_tensor(
                     yb[:, 512 * dm:512 * (dm + 1)], Pp[:, poff:poff + 512],
                     sdh[:, dm:dm + 1], x_bf[:, T * dm + qs:T * dm + qs + QB],
                     ALU.mult, ALU.add), vRS)
        o.wait("scalar", vRS, vRS.v)
        o.wait("scalar", vXW, vXW.v)
        o.wait("tensor", vRS, vRS.v)
        o.wait("vector", vXW, vXW.v)
        LN.emit(o, G, P0, P1, yb, qs)

    o.flush(block)
    ctx.close()
    return vXW, vXW.v



def phase_ffn1(nc, block, G, bar, barv):
    name = "f1"
    ctx = ExitStack()
    o = Orch(nc, ctx, G['semreg'], G['semctx'])
    sb = lambda n, s, d: ctx.enter_context(
        nc.sbuf_tensor(f"{name}S_{n}", s, d))
    x_bf = G["x_bf"]
    LN = G["LN"]
    P0, P1, P2, P3 = G["P0"], G["P1"], G["P2"], G["P3"]

    Mbuf = [sb(f"Mb{i}", [128, 32 * 128], BF16) for i in range(2)]
    bexp = sb("bexp", [128, 32], F32)
    ubB = sb("ubB", [128, 32], F32)
    bsum = sb("bsum", [128, 8], F32)
    bc1 = sb("bc1", [128, 8], F32)
    bc2 = sb("bc2", [128, 8], F32)
    b0 = sb("b0", [128, 32 * 512], BF16)
    b1 = sb("b1", [128, 32 * 512], BF16)
    yb = sb("yb", [128, 8 * 512], BF16)
    wcon = [sb(f"wcon{i}", [128, 1024], BF16) for i in range(2)]
    sub = ExitStack()
    sb1 = lambda n, s, d: sub.enter_context(
        nc.sbuf_tensor(f"{name}S_{n}", s, d))
    uacc = sb1("uacc", [128, 4096], F32)
    ubuf = [sb1(f"ubuf{i}", [128, 4096], F32) for i in range(2)]
    mres = [sb1(f"mres{i}", [128, 4096], BF16) for i in range(2)]
    sub.close()
    wexp = sb("wexp", [128, 8 * 4096], BF16)

    dU = [o.sem(f"{name}_dU0"), o.sem(f"{name}_dU1")]
    dMS = [o.sem(f"{name}_dMS0"), o.sem(f"{name}_dMS1")]
    dMB = [o.sem(f"{name}_dMB0"), o.sem(f"{name}_dMB1")]
    vCM = o.sem(f"{name}_vCM")
    dWX = o.sem(f"{name}_dWX")
    dWC = [o.sem(f"{name}_dWC0"), o.sem(f"{name}_dWC1")]
    dB2 = o.sem(f"{name}_dB2")
    vBS = o.sem(f"{name}_vBS")
    tEX = o.sem(f"{name}_tEX")
    sEV = o.sem(f"{name}_sEV")
    tAP = o.sem(f"{name}_tAP")
    sAE = o.sem(f"{name}_sAE")
    dTN = o.sem(f"{name}_dTN")
    tCN = o.sem(f"{name}_tCN")
    vRS = o.sem(f"{name}_vRS")
    vXW = LN.vXW

    NST = [7, 5, 7, 5]
    UH = [G[f"f1_U{g}_h"] for g in range(4)]

    o.wait("sync", bar, barv)
    nld = 0
    for g in range(4):
        for st in range(NST[g]):
            if nld >= 2:
                o.wait("sync", vCM, nld - 1)
            o.op("sync", lambda sy, g=g, st=st, b=nld % 2: sy.dma_start(
                out=ubuf[b][:], in_=UH[g][st]), dU[nld % 2], 16)
            o.wait("vector", dU[nld % 2], dU[nld % 2].v)
            if g == 0 and st == 0:
                o.wait("vector", bar, barv)
            if st == 0:
                if g >= 2:
                    o.wait("vector", dMS[g % 2], 16)
                o.fop("vector", lambda v, b=nld % 2: v.tensor_copy(
                    uacc[:], ubuf[b][:]), vCM)
            elif st < NST[g] - 1:
                o.fop("vector", lambda v, b=nld % 2: v.tensor_mul(
                    uacc[:], uacc[:], ubuf[b][:]), vCM)
            else:
                o.fop("vector", lambda v, g=g, b=nld % 2: v.tensor_mul(
                    mres[g % 2][:], uacc[:], ubuf[b][:]), vCM)
                o.wait("sync", vCM, vCM.v)
                o.op("sync", lambda sy, g=g: sy.dma_start(
                    out=G["f1_Mscr"][g], in_=mres[g % 2][:]), dMS[g % 2], 16)
            nld += 1

    o.wait("gpsimd", bar, barv)
    o.wait("gpsimd", vCM, vCM.v)
    o.wait("gpsimd", dMS[0], 32)
    o.wait("gpsimd", dMS[1], 32)
    for k in range(8):
        o.op("gpsimd", lambda gg, k=k: gg.dma_start(
            out=wexp[:, 4096 * k:4096 * (k + 1)],
            in_=G["f1_wexp_h"][128 * k:128 * (k + 1), :]), dWX, 16)
    o.op("sync", lambda sy: sy.dma_start(
        out=bexp[:], in_=G["f1_bexp_h"]), dB2, 16)
    o.op("sync", lambda sy: sy.dma_start(
        out=ubB[:], in_=G["f1_ubB_h"]), dB2, 16)
    o.op("sync", lambda sy: sy.dma_start(
        out=bc1[:], in_=G["f1_bcon_h"]), dB2, 16)
    o.op("sync", lambda sy: sy.dma_start(
        out=bc2[:], in_=G["f1_dbias_h"]), dB2, 16)
    o.wait("vector", dB2, dB2.v)
    o.fop("vector", lambda v: v.tensor_add(bsum[:], bc1[:], bc2[:]), vBS)

    aslots = [(P0, 0), (P0, 512), (P1, 0), (P1, 512)]
    dslots = [(P0, 0), (P0, 512), (P1, 0), (P1, 512),
              (P2, 0), (P2, 512), (P3, 0), (P3, 512)]
    ap_after = []

    mstep = [0]

    def emit_apply(gidx, src, dst, qs, bias=None, gelu=False):
        mi = mstep[0]
        mstep[0] += 1
        mb = Mbuf[mi % 2]
        if mi >= 2:
            o.wait("gpsimd", tAP, ap_after[mi - 2])
        o.wait("gpsimd", dMS[0], 32)
        o.wait("gpsimd", dMS[1], 32)
        o.op("gpsimd", lambda gg, gidx=gidx, mb=mb: gg.dma_start(
            out=mb[:], in_=G["f1_Mscr"][gidx]), dMB[mi % 2], 16)
        for c in range(32):
            Pp, off = aslots[c % 4]
            if c >= 4:
                o.wait("tensor", sEV, sEV.v - 3)
            else:
                o.wait("tensor", sEV, max(0, sEV.v - 3 + c))
            if c == 0:
                o.wait("tensor", dMB[mi % 2], dMB[mi % 2].v)
            o.op("tensor", lambda t, Pp=Pp, off=off, c=c, mb=mb, src=src:
                 t.matmul(
                     Pp[:, off:off + 512], mb[:, 128 * c:128 * (c + 1)],
                     src[:, 512 * c:512 * (c + 1)],
                     start=True, stop=True), tAP)
            o.wait("scalar", tAP, tAP.v)
            if gelu:
                o.op("scalar", lambda ss, Pp=Pp, off=off, c=c, dst=dst:
                     ss.activation(
                         dst[:, 512 * c:512 * (c + 1)], Pp[:, off:off + 512],
                         AF.Gelu, bias=ubB[:, c:c + 1]), sEV)
            elif bias is not None:
                o.op("scalar", lambda ss, Pp=Pp, off=off, c=c, dst=dst,
                     bias=bias: ss.activation(
                         dst[:, 512 * c:512 * (c + 1)], Pp[:, off:off + 512],
                         AF.Identity, bias=bias[:, c:c + 1]), sEV)
            else:
                o.op("scalar", lambda ss, Pp=Pp, off=off, c=c, dst=dst:
                     ss.copy(dst[:, 512 * c:512 * (c + 1)],
                             Pp[:, off:off + 512]), sEV)
        ap_after.append(tAP.v)

    def emit_turn_AB(src, dst):
        for tau in range(32):
            o.op("sync", lambda sy, tau=tau, src=src, dst=dst: sy.dma_start(
                out=dst[:, 512 * tau:512 * (tau + 1)],
                in_=src[4 * tau:4 * tau + 4, :].rearrange(
                    "p (c e) -> p c e", e=512)), dTN, 16)

    def emit_turn_BA(src, dst):
        scr = G["f1_Tscr"]
        o.op("sync", lambda sy, src=src: sy.dma_start(
            out=scr, in_=src[:]), dTN, 16)
        o.wait("sync", dTN, dTN.v)
        sv = scr.rearrange("(l c) (t e) -> c t l e", c=32, e=512)
        for c in range(32):
            o.op("sync", lambda sy, c=c, dst=dst, sv=sv: sy.dma_start(
                out=dst[:, 512 * c:512 * (c + 1)].rearrange(
                    "p (x e) -> p x e", x=1),
                in_=sv[c]), dTN, 16)

    for tc in range(4):
        qs = 512 * tc
        o.wait("tensor", bar, barv)
        if tc == 0:
            o.wait("tensor", dWX, dWX.v)
            o.wait("tensor", vCM, vCM.v)
        for m in range(32):
            Pp, off = aslots[m % 4]
            if tc > 0 or m >= 4:
                o.wait("tensor", sEV, sEV.v - 3)
            if tc > 0 and m == 0:
                o.wait("tensor", dTN, dTN.v)
                o.wait("tensor", vRS, vRS.v)
                o.wait("tensor", LN.vLS, LN.vLS.v)
                o.wait("tensor", LN.sLB, LN.sLB.v)
            for k in range(8):
                o.op("tensor", lambda t, Pp=Pp, off=off, m=m, k=k, qs=qs:
                     t.matmul(
                         Pp[:, off:off + 512],
                         wexp[:, 4096 * k + 128 * m:4096 * k + 128 * (m + 1)],
                         x_bf[:, T * k + qs:T * k + qs + QB],
                         start=(k == 0), stop=(k == 7)), tEX)
            o.wait("scalar", tEX, tEX.v)
            if tc == 0 and m == 0:
                o.wait("scalar", dB2, dB2.v)
            if tc > 0 and m == 0:
                o.wait("scalar", dTN, dTN.v)
            o.op("scalar", lambda ss, Pp=Pp, off=off, m=m: ss.activation(
                b0[:, 512 * m:512 * (m + 1)], Pp[:, off:off + 512],
                AF.Identity, bias=bexp[:, m:m + 1]), sEV)
        emit_apply(0, b0, b1, qs)
        o.wait("sync", sEV, sEV.v)
        emit_turn_AB(b1, b0)
        o.wait("tensor", dTN, dTN.v)
        emit_apply(1, b0, b1, qs, gelu=True)
        o.wait("sync", sEV, sEV.v)
        emit_turn_BA(b1, b0)
        o.wait("tensor", dTN, dTN.v)
        emit_apply(2, b0, b1, qs)
        o.wait("sync", sEV, sEV.v)
        emit_turn_AB(b1, b0)
        o.wait("tensor", dTN, dTN.v)
        emit_apply(3, b0, b1, qs)
        for k in range(32):
            bw = k % 2
            if k >= 2:
                o.wait("gpsimd", tCN, tCN.v - 8)
            o.op("gpsimd", lambda gy, k=k, bw=bw: gy.dma_start(
                out=wcon[bw][:],
                in_=G["f1_wconB_h"][128 * k:128 * (k + 1), :]), dWC[bw], 16)
            o.wait("tensor", dWC[bw], dWC[bw].v)
            if k == 0:
                o.wait("tensor", sEV, sEV.v)
                o.wait("tensor", vRS, vRS.v)
                o.wait("tensor", LN.vLS, LN.vLS.v)
                o.wait("tensor", LN.sLB, LN.sLB.v)
            for dm in range(8):
                Pp, off = dslots[dm]
                o.op("tensor", lambda t, Pp=Pp, off=off, k=k, dm=dm, bw=bw:
                     t.matmul(
                         Pp[:, off:off + 512],
                         wcon[bw][:, 128 * dm:128 * (dm + 1)],
                         b1[:, 512 * k:512 * (k + 1)],
                         start=(k == 0), stop=(k == 31)), tCN)
        o.wait("vector", tCN, tCN.v)
        o.wait("vector", vBS, 1)
        o.wait("vector", vXW, vXW.v)
        for dm in range(8):
            Pp, off = dslots[dm]
            o.op("vector", lambda v, Pp=Pp, off=off, dm=dm, qs=qs:
                 v.scalar_tensor_tensor(
                     yb[:, 512 * dm:512 * (dm + 1)], Pp[:, off:off + 512],
                     bsum[:, dm:dm + 1], x_bf[:, T * dm + qs:T * dm + qs + QB],
                     ALU.add, ALU.add), vRS)
        o.wait("scalar", vRS, vRS.v)
        o.wait("scalar", vXW, vXW.v)
        o.wait("tensor", vRS, vRS.v)
        o.wait("vector", vXW, vXW.v)
        LN.emit(o, G, P0, P1, yb, qs)

    o.flush(block)
    ctx.close()
    return vXW, vXW.v




def build_program():
    nc = bass.Bass("TRN2", target_bir_lowering=False, debug=False)
    octx = ExitStack()
    G = {}
    G["semreg"] = {}
    G["semctx"] = octx
    G["x_h"] = nc.dram_tensor("x", [T, D], F32, kind="ExternalInput").ap()
    G["out_h"] = nc.dram_tensor("out", [T, D], F32,
                                kind="ExternalOutput").ap()

    def din(nm, shape, dt=F32):
        G[nm + "_h"] = nc.dram_tensor(nm, list(shape), dt,
                                      kind="ExternalInput").ap()

    din("identf", [128, 128])
    din("identb", [128, 128], BF16)
    din("identfn", [128, 128])
    din("hmask16", [128, 8 * 16], BF16)
    din("selpair", [16, 8 * 128], BF16)
    din("onescol", [128, 1], BF16)
    din("onesrow", [1, 128], BF16)

    sbp = lambda n, s, d: octx.enter_context(nc.sbuf_tensor(n, s, d))
    for pi in range(4):
        G[f"P{pi}"] = octx.enter_context(
            nc.psum_tensor(f"g_P{pi}", [128, 1024], F32))
    G["x_bf"] = sbp("g_xbf", [128, NCH * T], BF16)
    G["identf"] = sbp("g_identf", [128, 128], F32)
    G["identb"] = sbp("g_identb", [128, 128], BF16)
    G["identfn"] = sbp("g_identfn", [128, 128], F32)
    G["hmask16"] = sbp("g_hm16", [128, 8 * 16], BF16)
    G["selpair"] = sbp("g_selp", [16, 8 * 128], BF16)
    G["onescol"] = sbp("g_onesc", [128, 1], BF16)
    G["onesrow"] = sbp("g_onesr", [1, 128], BF16)
    G["LN"] = LNPack(nc, octx)
    G["const_loads"] = [
        ("identf", G["identf"]), ("identb", G["identb"]),
        ("identfn", G["identfn"]),
        ("hmask16", G["hmask16"]), ("selpair", G["selpair"]),
        ("onescol", G["onescol"]), ("onesrow", G["onesrow"]),
    ]
    if NPHASE >= 1:
        din("a0_w", [1024, 3072])
        din("a0_b", [128, 24])
        din("a0_wo", [1024, 1024])
        din("a0_bo", [128, 8])
        din("mask_a0", [128, 6 * 512], BF16)
    if NPHASE >= 2:
        din("f0_wg", [2816, 1024])
        din("f0_wu", [2816, 1024])
        din("f0_wd", [1024, 2816])
        din("f0_clipx", [1, 1])
        din("f0_clipd", [1, 1])
        G["cc0_in"] = nc.dram_tensor("cc0_in", [1, 1], F32,
                                     kind="Internal").ap()
        G["cc0_out"] = nc.dram_tensor("cc0_out", [1, 8], F32,
                                      kind="Internal").ap()
        G["cc1_in"] = nc.dram_tensor("cc1_in", [1, 1], F32,
                                     kind="Internal").ap()
        G["cc1_out"] = nc.dram_tensor("cc1_out", [1, 8], F32,
                                      kind="Internal").ap()
    if NPHASE >= 3:
        din("a1_w", [1024, 3072])
        din("a1_b", [128, 24])
        din("a1_wo", [1024, 1024])
        din("a1_bo", [128, 8])
        din("mask_a1", [128, 4 * 512], BF16)

    if NPHASE >= 4:
        G["f1_Mscr"] = nc.dram_tensor("f1_Mscr", [4, 128, 4096], BF16,
                                      kind="Internal").ap()
        G["f1_Tscr"] = nc.dram_tensor("f1_Tscr", [128, 16384], BF16,
                                      kind="Internal").ap()
        din("f1_wexp", [1024, 4096])
        din("f1_bexp", [128, 32])
        din("f1_ubB", [128, 32])
        din("f1_bcon", [128, 8])
        din("f1_dbias", [128, 8])
        din("f1_wconB", [4096, 1024])
        for g_ in range(4):
            din(f"f1_U{g_}", [[7, 5, 7, 5][g_], 128, 4096])

    with nc.Block() as block:
        bar, barv = phase_in(nc, block, G)
        if NPHASE >= 1:
            bar, barv = phase_attention(nc, block, G, L0Cfg, bar, barv)
        if NPHASE >= 2:
            bar, barv = phase_ffn0(nc, block, G, bar, barv)
        if NPHASE >= 3:
            bar, barv = phase_attention(nc, block, G, L1Cfg, bar, barv)
        if NPHASE >= 4:
            bar, barv = phase_ffn1(nc, block, G, bar, barv)
        phase_out(nc, block, G, bar, barv)
    octx.close()
    return nc


_PROGRAM = None


def _get_program():
    global _PROGRAM
    if _PROGRAM is None:
        _PROGRAM = build_program()
    return _PROGRAM


def kernel(**inputs):
    nc = _get_program()
    x = np.ascontiguousarray(np.asarray(inputs["L0_attn_x"], np.float32))
    cm = host_prep(inputs)
    in_maps = []
    for c in range(8):
        m = dict(cm)
        m["x"] = np.ascontiguousarray(x[c])
        in_maps.append(m)
    res = run_bass_kernel_spmd(nc, in_maps, core_ids=list(range(8)))
    out = np.stack([res.results[c]["out"] for c in range(8)], axis=0)
    return out.astype(np.float32)


# version 03
# speedup vs baseline: 1.0032x; 1.0032x over previous
import numpy as np
import ml_dtypes
from contextlib import ExitStack
import concourse.bass as bass
import concourse.mybir as mybir
from concourse.bass_utils import run_bass_kernel_spmd

F32 = mybir.dt.float32
BF16 = mybir.dt.bfloat16
I32 = mybir.dt.int32
AF = mybir.ActivationFunctionType
ALU = mybir.AluOpType

T = 2048
D = 1024
H = 16
DH = 64
NCH = 8
QB = 512
NQB = T // QB
SCALE = DH ** -0.5
BF = ml_dtypes.bfloat16

NPHASE = 4
F0STEP = 4
DUMP = None

ENGINES = ("tensor", "scalar", "vector", "gpsimd", "sync")


class Sem:
    def __init__(self, nc, ctx, name):
        self.h = ctx.enter_context(nc.semaphore(name))
        self.v = 0


class Orch:

    def __init__(self, nc, ctx, reg=None, regctx=None):
        self.nc = nc
        self.ctx = ctx
        self.reg = reg if reg is not None else {}
        self.regctx = regctx if regctx is not None else ctx
        self.ops = {e: [] for e in ENGINES}

    def sem(self, name):
        if name not in self.reg:
            self.reg[name] = Sem(self.nc, self.regctx, name)
        return self.reg[name]

    def fop(self, eng, fn, sem=None, n=1):
        f = self.sem(f"fence_{eng}")
        if sem is None:
            sem = f
        else:
            self.op(eng, fn, sem, n)
            self.wait(eng, sem)
            return
        self.op(eng, fn, f, 1)
        self.wait(eng, f)

    def wait(self, eng, sem, val=None):
        v = sem.v if val is None else val
        assert v <= sem.v, f"wait on future value {v} > {sem.v}"
        if v <= 0:
            return
        self.ops[eng].append(lambda e, h=sem.h, v=v: e.wait_ge(h, v))

    def op(self, eng, fn, sem=None, n=1):
        if sem is None:
            self.ops[eng].append(lambda e, fn=fn: fn(e))
        else:
            sem.v += n
            self.ops[eng].append(
                lambda e, fn=fn, h=sem.h, n=n: fn(e).then_inc(h, n))

    def flush(self, block):
        for eng in ENGINES:
            lst = self.ops[eng]
            if not lst:
                continue

            def section(e, lst=lst):
                for fn in lst:
                    fn(e)
            getattr(block, eng)(section)
        self.ops = {e: [] for e in ENGINES}




def _bf_i_of_B(tau, pi):
    h = pi & 31
    l01 = pi >> 5
    return (h << 7) | (tau << 2) | l01


def _bf_stage_lookup(w, s, out_i, in_i):
    state = ((out_i & ((1 << s) - 1)) | (in_i & ~((2 << s) - 1))
             | (in_i & (1 << s)))
    low = state & ((1 << s) - 1)
    high = state >> (s + 1)
    k = high * (1 << s) + low
    return w[s, k, (out_i >> s) & 1, (in_i >> s) & 1]


def _bf_expand(w, stages, blayout):
    nst = len(stages)
    U = np.zeros((nst, 128, 4096), np.float32)
    pp = np.arange(128)
    smask = 0
    for s_ in stages:
        smask |= (1 << s_)
    for cc in range(32):
        if blayout:
            out_i = np.array([_bf_i_of_B(cc, q) for q in pp])[:, None]
            in_i = np.array([_bf_i_of_B(cc, q) for q in pp])[None, :]
            kmask = ~smask
        else:
            out_i = (128 * cc + pp)[:, None]
            in_i = (128 * cc + pp)[None, :]
            kmask = (~smask) & 127
        oi = np.broadcast_to(out_i, (128, 128))
        ii = np.broadcast_to(in_i, (128, 128))
        for si, s_ in enumerate(stages):
            U[si, :, 128 * cc:128 * (cc + 1)] = _bf_stage_lookup(w, s_, oi, ii)
        keep = ((oi ^ ii) & kmask) == 0
        U[0, :, 128 * cc:128 * (cc + 1)] = np.where(
            keep, U[0, :, 128 * cc:128 * (cc + 1)], 0.0)
    return np.ascontiguousarray(U)


def colchunk(vec):
    v = np.asarray(vec, np.float32)
    k = v.shape[0] // 128
    return np.ascontiguousarray(v.reshape(k, 128).T)


def l0_masks():
    out = np.zeros((6, 128, 512), BF)
    for i, r in enumerate([-128, 0, 128, 256, 384, 512]):
        p = np.arange(128)[:, None]
        t_ = np.arange(512)[None, :]
        out[i] = (np.abs(r + p - t_) <= 64).astype(BF)
    return np.ascontiguousarray(out.transpose(1, 0, 2).reshape(128, 6 * 512))


def l1_masks():
    out = np.zeros((4, 128, 512), BF)
    for i, r in enumerate([256, 384, 512, 640]):
        p = np.arange(128)[:, None]
        t_ = np.arange(512)[None, :]
        out[i] = ((r + p - t_) <= 256).astype(BF)
    return np.ascontiguousarray(out.transpose(1, 0, 2).reshape(128, 4 * 512))


def hmask16_host():
    m = np.zeros((128, 8 * 16), BF)
    for c in range(8):
        m[0:64, 16 * c + 2 * c] = 1
        m[64:128, 16 * c + 2 * c + 1] = 1
    return m


def selpair_host():
    m = np.zeros((16, 8 * 128), BF)
    for p in range(8):
        m[2 * p, 128 * p:128 * p + 64] = 1
        m[2 * p + 1, 128 * p + 64:128 * p + 128] = 1
    return m


def host_prep(inputs):
    cm = {}
    cm["identf"] = np.eye(128, dtype=np.float32)
    cm["identb"] = np.eye(128, dtype=BF)
    cm["identfn"] = (-np.eye(128)).astype(np.float32)
    cm["hmask16"] = hmask16_host()
    cm["selpair"] = selpair_host()
    cm["onescol"] = np.ones((128, 1), BF)
    cm["onesrow"] = np.ones((1, 128), BF)
    if NPHASE >= 1:
        cm["a0_w"] = np.ascontiguousarray(
            np.asarray(inputs["L0_attn_Wqkv"], np.float32))
        cm["a0_b"] = colchunk(inputs["L0_attn_bqkv"])
        cm["a0_wo"] = np.ascontiguousarray(
            np.asarray(inputs["L0_attn_Wout"], np.float32))
        cm["a0_bo"] = colchunk(inputs["L0_attn_bout"])
        cm["mask_a0"] = l0_masks()
    if NPHASE >= 2:
        cm["f0_wg"] = np.ascontiguousarray(
            np.asarray(inputs["L0_ffn_w_gate"], np.float32))
        cm["f0_wu"] = np.ascontiguousarray(
            np.asarray(inputs["L0_ffn_w_up"], np.float32))
        cm["f0_wd"] = np.ascontiguousarray(
            np.asarray(inputs["L0_ffn_w_down"], np.float32))
        cm["f0_clipx"] = np.asarray(inputs["L0_ffn_clip_x"],
                                    np.float32).reshape(1, 1)
        cm["f0_clipd"] = np.asarray(inputs["L0_ffn_clip_down"],
                                    np.float32).reshape(1, 1)
    if NPHASE >= 3:
        cm["a1_w"] = np.ascontiguousarray(
            np.asarray(inputs["L1_attn_in_proj_w"], np.float32).T)
        cm["a1_b"] = colchunk(inputs["L1_attn_in_proj_b"])
        cm["a1_wo"] = np.ascontiguousarray(
            np.asarray(inputs["L1_attn_out_proj_w"], np.float32).T)
        cm["a1_bo"] = colchunk(inputs["L1_attn_out_proj_b"])
        cm["mask_a1"] = l1_masks()
    if NPHASE >= 4:
        cm["f1_wexp"] = np.ascontiguousarray(
            np.asarray(inputs["L1_ffn_W_exp"], np.float32))
        cm["f1_bexp"] = colchunk(inputs["L1_ffn_b_exp"])
        ub = np.asarray(inputs["L1_ffn_up_bias"], np.float32)
        ubB = np.zeros((128, 32), np.float32)
        for tau in range(32):
            for pi in range(128):
                ubB[pi, tau] = ub[_bf_i_of_B(tau, pi)]
        cm["f1_ubB"] = ubB
        cm["f1_bcon"] = colchunk(inputs["L1_ffn_b_con"])
        cm["f1_dbias"] = colchunk(inputs["L1_ffn_down_bias"])
        wcon = np.asarray(inputs["L1_ffn_W_con"], np.float32)
        perm = np.array([_bf_i_of_B(jB // 128, jB % 128) for jB in range(4096)])
        cm["f1_wconB"] = np.ascontiguousarray(wcon[perm])
        uw = np.asarray(inputs["L1_ffn_up_weights"], np.float32)
        dw = np.asarray(inputs["L1_ffn_down_weights"], np.float32)
        cm["f1_U0"] = _bf_expand(uw, list(range(7)), False)
        cm["f1_U1"] = _bf_expand(uw, list(range(7, 12)), True)
        cm["f1_U2"] = _bf_expand(dw, list(range(7)), False)
        cm["f1_U3"] = _bf_expand(dw, list(range(7, 12)), True)
    return cm



class L0Cfg:
    name = "a0"
    nmask = 6
    kst_w = 768

    @staticmethod
    def window(qb):
        qs = qb * QB
        return max(0, qs - 128), min(T, qs + 640)

    @staticmethod
    def maskidx(qb, ts):
        r = ts - qb * QB
        return {-128: 0, 0: 1, 128: 2, 256: 3, 384: 4, 512: 5}[r]


class L1Cfg:
    name = "a1"
    nmask = 4
    kst_w = 2048

    @staticmethod
    def window(qb):
        qs = qb * QB
        return 0, min(T, qs + 768)

    @staticmethod
    def maskidx(qb, ts):
        r = ts - qb * QB
        return {256: 0, 384: 1, 512: 2, 640: 3}.get(r)




class LNPack:

    def __init__(self, nc, ctx):
        sb = lambda n, s, d: ctx.enter_context(nc.sbuf_tensor(n, s, d))
        self.ln_a = sb("ln_a", [1, 512], F32)
        self.ln_b = sb("ln_b", [1, 512], F32)
        self.mr = sb("ln_mr", [1, 1024], BF16)
        self.mrb = sb("ln_mrb", [128, 1024], BF16)
        self.y2 = [sb(f"ln_y2{i}", [128, 512], BF16) for i in range(2)]
        self.sLN = Sem(nc, ctx, "ln_sLN")
        self.tLS = Sem(nc, ctx, "ln_tLS")
        self.vLS = Sem(nc, ctx, "ln_vLS")
        self.sLR = Sem(nc, ctx, "ln_sLR")
        self.tLB = Sem(nc, ctx, "ln_tLB")
        self.sLB = Sem(nc, ctx, "ln_sLB")
        self.vXW = Sem(nc, ctx, "ln_vXW")

    def emit(self, o, G, PA, PB, yb, qs):
        x_bf = G["x_bf"]
        for c in range(8):
            if c >= 2:
                o.wait("scalar", self.tLS, self.tLS.v - 2)
            o.op("scalar", lambda s, c=c, y2=self.y2[c % 2]: s.activation(
                y2[:], yb[:, 512 * c:512 * (c + 1)], AF.Square), self.sLN)
            o.wait("tensor", self.sLN, self.sLN.v)
            o.op("tensor", lambda t, c=c: t.matmul(
                PA[0:1, 0:512], G["onescol"][:],
                yb[:, 512 * c:512 * (c + 1)],
                start=(c == 0), stop=(c == 7)), self.tLS)
            o.op("tensor", lambda t, c=c, y2=self.y2[c % 2]: t.matmul(
                PA[0:1, 512:1024], G["onescol"][:], y2[:],
                start=(c == 0), stop=(c == 7)), self.tLS)
        o.wait("vector", self.tLS, self.tLS.v)
        o.fop("vector", lambda v: v.tensor_scalar_mul(
            self.mr[0:1, 0:512], PA[0:1, 0:512], 1.0 / D))
        o.fop("vector", lambda v: v.tensor_scalar(
            self.ln_a[:], PA[0:1, 512:1024], 1.0 / D, 1e-5,
            ALU.mult, ALU.add))
        o.fop("vector", lambda v: v.tensor_mul(
            self.ln_b[:], self.mr[0:1, 0:512], self.mr[0:1, 0:512]))
        o.op("vector", lambda v: v.tensor_sub(
            self.ln_a[:], self.ln_a[:], self.ln_b[:]), self.vLS)
        o.wait("scalar", self.vLS, self.vLS.v)
        o.fop("scalar", lambda s: s.activation(
            self.ln_b[:], self.ln_a[:], AF.Ln))
        o.op("scalar", lambda s: s.activation(
            self.mr[0:1, 512:1024], self.ln_b[:], AF.Exp, scale=-0.5),
            self.sLR)
        o.wait("tensor", self.sLR, self.sLR.v)
        o.op("tensor", lambda t: t.matmul(
            PB[:, 0:512], G["onesrow"][:], self.mr[0:1, 0:512],
            start=True, stop=True), self.tLB)
        o.op("tensor", lambda t: t.matmul(
            PB[:, 512:1024], G["onesrow"][:], self.mr[0:1, 512:1024],
            start=True, stop=True), self.tLB)
        o.wait("scalar", self.tLB, self.tLB.v)
        o.op("scalar", lambda s: s.copy(
            self.mrb[:, 0:512], PB[:, 0:512]), self.sLB)
        o.op("scalar", lambda s: s.copy(
            self.mrb[:, 512:1024], PB[:, 512:1024]), self.sLB)
        o.wait("vector", self.sLB, self.sLB.v)
        for c in range(8):
            fn = o.fop if c == 7 else o.op
            fn("vector", lambda v, c=c: v.tensor_sub(
                yb[:, 512 * c:512 * (c + 1)], yb[:, 512 * c:512 * (c + 1)],
                self.mrb[:, 0:512]))
        for c in range(8):
            o.op("vector", lambda v, c=c, qs=qs: v.tensor_mul(
                x_bf[:, T * c + qs:T * c + qs + QB],
                yb[:, 512 * c:512 * (c + 1)], self.mrb[:, 512:1024]),
                self.vXW)


def phase_in(nc, block, G):
    ctx = ExitStack()
    o = Orch(nc, ctx, G['semreg'], G['semctx'])
    xin = ctx.enter_context(nc.sbuf_tensor("in_stage", [128, 2 * D], F32))
    pT = [G["P0"][:, 0:128], G["P0"][:, 512:640]]
    ldc = o.sem("in_ldc")
    ldx = [o.sem("in_ldx0"), o.sem("in_ldx1")]
    mmT = o.sem("in_mmT")
    evT = o.sem("in_evT")

    for nm, tens in G["const_loads"]:
        o.op("gpsimd", lambda g, nm=nm, tens=tens: g.dma_start(
            out=tens[:], in_=G[nm + "_h"]), ldc, 16)
    nconst = ldc.v

    for i in range(16):
        if i >= 2:
            o.wait("sync", mmT, 8 * (i - 1))
        o.op("sync", lambda s, i=i: s.dma_start(
            out=xin[:, (i % 2) * D:(i % 2 + 1) * D],
            in_=G["x_h"][i * 128:(i + 1) * 128, :]), ldx[i % 2], 16)
        o.wait("tensor", ldx[i % 2], 16 * (i // 2 + 1))
        if i == 0:
            o.wait("tensor", ldc, nconst)
        for d in range(NCH):
            j = 8 * i + d
            if j >= 2:
                o.wait("tensor", evT, j - 1)
            o.op("tensor", lambda t, i=i, d=d, j=j: t.matmul(
                pT[j % 2],
                xin[:, (i % 2) * D + d * 128:(i % 2) * D + (d + 1) * 128],
                G["identf"][:], start=True, stop=True), mmT)
            o.wait("vector", mmT, j + 1)
            o.op("vector", lambda v, i=i, d=d, j=j: v.tensor_copy(
                G["x_bf"][:, d * T + i * 128:d * T + (i + 1) * 128],
                pT[j % 2]), evT)
    o.flush(block)
    ctx.close()
    return evT, evT.v


def phase_out(nc, block, G, bar, barv):
    ctx = ExitStack()
    o = Orch(nc, ctx, G['semreg'], G['semctx'])
    xout = ctx.enter_context(nc.sbuf_tensor("out_stage", [128, 2 * D], F32))
    pO = [G["P1"][:, 0:128], G["P1"][:, 512:640]]
    mmO = o.sem("out_mmO")
    evO = o.sem("out_evO")
    stD = [o.sem("out_stD0"), o.sem("out_stD1")]
    o.wait("tensor", bar, barv)
    o.wait("vector", bar, barv)
    o.wait("sync", bar, barv)
    for i in range(16):
        for d in range(NCH):
            j = 8 * i + d
            if j >= 2:
                o.wait("tensor", evO, j - 1)
            if d == 0 and i >= 2:
                o.wait("tensor", stD[i % 2], 16 * (i // 2))
            src = G.get("dump_t", G["x_bf"])
            o.op("tensor", lambda t, i=i, d=d, j=j, src=src: t.matmul(
                pO[j % 2],
                src[:, d * T + i * 128:d * T + (i + 1) * 128],
                G["identb"][:], start=True, stop=True), mmO)
            o.wait("vector", mmO, j + 1)
            o.op("vector", lambda v, i=i, d=d, j=j: v.tensor_copy(
                xout[:, (i % 2) * D + d * 128:(i % 2) * D + (d + 1) * 128],
                pO[j % 2]), evO)
        o.wait("sync", evO, 8 * (i + 1))
        o.op("sync", lambda s, i=i: s.dma_start(
            out=G["out_h"][i * 128:(i + 1) * 128, :],
            in_=xout[:, (i % 2) * D:(i % 2 + 1) * D]), stD[i % 2], 16)
    o.wait("sync", stD[0], 128)
    o.wait("sync", stD[1], 128)
    o.flush(block)
    ctx.close()


def phase_attention(nc, block, G, cfg, bar, barv):
    name = cfg.name
    ctx = ExitStack()
    o = Orch(nc, ctx, G['semreg'], G['semctx'])
    sb = lambda n, s, d: ctx.enter_context(
        nc.sbuf_tensor(f"{name}S_{n}", s, d))
    ps = lambda n, s: ctx.enter_context(
        nc.psum_tensor(f"{name}P_{n}", s, F32))

    x_bf = G["x_bf"]
    identb = G["identb"]
    hm16 = G["hmask16"]
    selp = G["selpair"]
    masks = sb("maskar", [128, cfg.nmask * 512], BF16)
    mask_h = G[f"mask_{name}_h"]
    w_h = G[f"{name}_w_h"]
    b_h = G[f"{name}_b_h"]
    wo_h = G[f"{name}_wo_h"]
    bo_h = G[f"{name}_bo_h"]

    NKST = 2
    bqkv = sb("bqkv", [128, 24], F32)
    bo_s = sb("bo", [128, 8], F32)
    bq_s = sb("bqs", [128, 8], F32)
    k_pack = sb("kpack", [128, 8 * 2048], BF16)
    vt = sb("vt", [128, 16 * 16 * 65], BF16)
    kst = [sb(f"kst{i}", [66, cfg.kst_w], BF16) for i in range(NKST)]
    den = sb("den", [16, 512], F32)
    den4 = sb("den4", [128, 2048], BF16)
    rden = sb("rden", [16, 512], BF16)
    subctx = ExitStack()
    sb1 = lambda n, s, d: subctx.enter_context(
        nc.sbuf_tensor(f"{name}S_{n}", s, d))
    wkv = sb1("wkv", [128, 8 * 2048], BF16)
    vch = sb1("vch", [128, 8 * 2048], BF16)
    subctx.close()
    wq = sb("wq", [128, 8 * 1024], BF16)
    wo = sb("wo", [128, 8 * 1024], BF16)
    qt = sb("qt", [66, 16 * 512], BF16)
    macc = sb("macc", [128, 16], F32)
    rmn = sb("rmn", [128, 1], F32)
    negm = sb("negm", [16, 512], F32)
    psb = [sb(f"psb{i}", [128, 1024], BF16) for i in range(2)]
    araw = sb("araw", [128, 8 * 512], BF16)
    qp = araw
    rb = [sb(f"rb{i}", [128, 512], BF16) for i in range(2)]
    yb = sb("yb", [128, 8 * 512], BF16)

    P0, P1, P2, P3 = G["P0"], G["P1"], G["P2"], G["P3"]

    dWkv = o.sem("at_dWkv")
    dMK = o.sem("at_dMK")
    dWq = o.sem("at_dWq")
    dWo = o.sem("at_dWo")
    dB = o.sem("at_dB")
    dNM = o.sem("at_dNM")
    vPRE = o.sem("at_vPRE")
    tKV = o.sem("at_tKV")
    sKV = o.sem("at_sKV")
    tVT = o.sem("at_tVT")
    sVT = [o.sem("at_sVTs"), o.sem("at_sVTv")]
    tQ = o.sem("at_tQ")
    sQE = o.sem("at_sQE")
    tMX = o.sem("at_tMX")
    vMX = o.sem("at_vMX")
    sMX = o.sem("at_sMX")
    vKC = o.sem("at_vKC")
    tS = o.sem("at_tS")
    sEX = o.sem("at_sEX")
    vMK = o.sem("at_vMK")
    tPV = o.sem("at_tPV")
    sPV = o.sem("at_sPV")
    vRC = o.sem("at_vRC")
    dDN = o.sem("at_dDN")
    tBC = o.sem("at_tBC")
    sBC = o.sem("at_sBC")
    vNE = o.sem("at_vNE")
    tO = o.sem("at_tO")
    vRS = o.sem("at_vRS")
    LN = G["LN"]
    sLN, tLS, vLS = LN.sLN, LN.tLS, LN.vLS
    sLR, tLB, sLB, vXW = LN.sLR, LN.tLB, LN.sLB, LN.vXW

    kv0 = sKV.v
    tKV0 = tKV.v
    tVT0 = tVT.v
    sVT0 = [sVT[0].v, sVT[1].v]
    sQE0 = sQE.v
    o.wait("gpsimd", bar, barv)
    o.op("gpsimd", lambda g: g.dma_start(
        out=masks[:, 0:cfg.nmask * 512], in_=mask_h), dMK, 16)
    for k in range(8):
        o.op("gpsimd", lambda g, k=k: g.dma_start(
            out=wkv[:, 2048 * k:2048 * (k + 1)],
            in_=w_h[128 * k:128 * (k + 1), 1024:3072]), dWkv, 16)
    o.wait("sync", bar, barv)
    o.op("sync", lambda s: s.dma_start(out=bqkv[:], in_=b_h), dB, 16)
    o.op("sync", lambda s: s.dma_start(out=bo_s[:], in_=bo_h), dB, 16)

    o.wait("vector", bar, barv)
    o.wait("vector", dB, dB.v)
    o.op("vector", lambda v: v.tensor_scalar_mul(
        bq_s[:], bqkv[:, 0:8], float(SCALE)), vPRE)
    o.op("vector", lambda v: v.memset(vt[:, 64:16 * 16 * 65:65], 1.0), vPRE)
    for i in range(NKST):
        o.op("vector", lambda v, i=i: v.memset(kst[i][64:66, :], 0.0), vPRE)
        o.fop("vector", lambda v, i=i: v.memset(kst[i][64:65, :], 1.0), vPRE)
    o.op("vector", lambda v: v.memset(qt[64:66, :], 0.0), vPRE)

    slots_kv = [(P0, 0), (P0, 512), (P1, 0), (P1, 512)]
    o.wait("tensor", bar, barv)
    o.wait("tensor", dWkv, dWkv.v)
    nj = 0
    o.wait("scalar", dB, dB.v)
    for mi in range(16):
        for n in range(4):
            j = nj
            nj += 1
            Pp, off = slots_kv[j % 4]
            if j >= 4:
                o.wait("tensor", sKV, kv0 + j - 3)
            for k in range(8):
                o.op("tensor", lambda t, Pp=Pp, off=off, mi=mi, n=n, k=k:
                     t.matmul(
                         Pp[:, off:off + 512],
                         wkv[:, 2048 * k + 128 * mi:2048 * k + 128 * (mi + 1)],
                         x_bf[:, T * k + 512 * n:T * k + 512 * (n + 1)],
                         start=(k == 0), stop=(k == 7)),
                     tKV if k == 7 else None)
            o.wait("scalar", tKV, tKV0 + j + 1)
            dst = k_pack if mi < 8 else vch
            c = mi % 8
            o.op("scalar", lambda s, Pp=Pp, off=off, dst=dst, c=c, n=n,
                 mi=mi: s.activation(
                     dst[:, 2048 * c + 512 * n:2048 * c + 512 * (n + 1)],
                     Pp[:, off:off + 512], AF.Identity,
                     bias=bqkv[:, 8 + mi:9 + mi]), sKV)

    slots_vt = [(P0, 0), (P0, 512), (P1, 0), (P1, 512)]
    jj = 0
    for h in range(16):
        c = h // 2
        rbase = (h % 2) * 64
        for tt_ in range(16):
            j = jj
            jj += 1
            Pp, off = slots_vt[j % 4]
            o.wait("tensor", sKV, kv0 + (64 if j < 4 else min(64, (9 + c) * 4)))
            if j >= 4:
                jp = j - 4
                o.wait("tensor", sVT[jp % 2], sVT0[jp % 2] + jp // 2 + 1)
            o.op("tensor", lambda t, Pp=Pp, off=off, c=c, rbase=rbase,
                 tt_=tt_: t.matmul(
                     Pp[:, off:off + 64],
                     vch[rbase:rbase + 64,
                         2048 * c + 128 * tt_:2048 * c + 128 * (tt_ + 1)],
                     identb[rbase:rbase + 64, rbase:rbase + 64],
                     start=True, stop=True), tVT)
            eng = "scalar" if (j % 2 == 0) else "vector"
            o.wait(eng, tVT, tVT0 + j + 1)
            col = (h * 16 + tt_) * 65
            if eng == "scalar":
                o.op(eng, lambda e, Pp=Pp, off=off, col=col: e.copy(
                    vt[:, col:col + 64], Pp[:, off:off + 64]), sVT[0])
            else:
                o.op(eng, lambda e, Pp=Pp, off=off, col=col: e.tensor_copy(
                    vt[:, col:col + 64], Pp[:, off:off + 64]), sVT[1])

    o.wait("gpsimd", tVT, tVT.v)
    for k in range(8):
        o.op("gpsimd", lambda g, k=k: g.dma_start(
            out=wq[:, 1024 * k:1024 * (k + 1)],
            in_=w_h[128 * k:128 * (k + 1), 0:1024]), dWq, 16)
    for k in range(8):
        o.op("gpsimd", lambda g, k=k: g.dma_start(
            out=wo[:, 1024 * k:1024 * (k + 1)],
            in_=wo_h[128 * k:128 * (k + 1), :]), dWo, 16)

    ts_after_head = []
    vmx_after = []
    mxstep = [0]
    pv_after_group = []
    ex_after_group = []
    gg = 0
    first_pv = True

    for qb in range(NQB):
        qs = qb * QB
        w0, w1 = cfg.window(qb)
        ntile = (w1 - w0) // 128

        if qb > 0:
            o.wait("tensor", vXW, vXW.v)
            o.wait("tensor", sLB, sLB.v)

        for m in range(8):
            mg = qb * 8 + m
            Pp, off = (P0, 0) if m % 2 == 0 else (P0, 512)
            if mg == 0:
                o.wait("tensor", sKV, kv0 + 64)
                o.wait("tensor", dWq, dWq.v)
                o.wait("tensor", sVT[0], sVT[0].v)
                o.wait("tensor", sVT[1], sVT[1].v)
            if mg >= 2:
                o.wait("tensor", sQE, sQE0 + 3 * (mg - 1))
            for k in range(8):
                o.op("tensor", lambda t, Pp=Pp, off=off, m=m, k=k, qs=qs:
                     t.matmul(
                         Pp[:, off:off + 512],
                         wq[:, 1024 * k + 128 * m:1024 * k + 128 * (m + 1)],
                         x_bf[:, T * k + qs:T * k + qs + QB],
                         start=(k == 0), stop=(k == 7)), tQ)
            if m == 0:
                o.wait("scalar", vPRE, 1)
                o.wait("scalar", tS, tS.v)
                o.wait("scalar", tMX, tMX.v)
                o.wait("scalar", tO, tO.v)
                o.wait("scalar", vNE, vNE.v)
            o.wait("scalar", tQ, tQ.v)
            for half in range(2):
                hh = 2 * m + half
                o.op("scalar", lambda s, Pp=Pp, off=off, half=half, hh=hh,
                     m=m: s.activation(
                         qt[0:64, 512 * hh:512 * (hh + 1)],
                         Pp[64 * half:64 * (half + 1), off:off + 512],
                         AF.Identity, scale=float(SCALE),
                         bias=bq_s[64 * half:64 * (half + 1), m:m + 1]), sQE)
            o.op("scalar", lambda s, Pp=Pp, off=off, m=m: s.activation(
                qp[:, 512 * m:512 * (m + 1)], Pp[:, off:off + 512],
                AF.Identity, scale=float(SCALE),
                bias=bq_s[:, m:m + 1]), sQE)

        nchunk = (w1 - w0 + 511) // 512
        mslots = [(P2, 0), (P2, 512), (P3, 0), (P3, 512)]
        for tqs in range(4):
            for hp in range(8):
                o.wait("tensor", sQE, sQE.v)
                for tkc in range(nchunk):
                    st = mxstep[0]
                    mxstep[0] += 1
                    sl = mslots[(st % 2) * 2:(st % 2) * 2 + 2]
                    if st >= 2:
                        o.wait("tensor", vMX, vmx_after[st - 2])
                    tw = min(512, w1 - w0 - 512 * tkc)
                    for hh in range(2):
                        Pp, off = sl[hh]
                        o.op("tensor", lambda t, Pp=Pp, off=off, hp=hp,
                             hh=hh, tqs=tqs, tkc=tkc, tw=tw, qs=qs, w0=w0:
                             t.matmul(
                                 Pp[:, off:off + tw],
                                 qp[64 * hh:64 * (hh + 1),
                                    512 * hp + 128 * tqs:512 * hp
                                    + 128 * (tqs + 1)],
                                 k_pack[64 * hh:64 * (hh + 1),
                                        2048 * hp + w0 + 512 * tkc:
                                        2048 * hp + w0 + 512 * tkc + tw],
                                 start=True, stop=True,
                                 tile_position=(64 * hh, 0)), tMX)
                    o.wait("vector", tMX, tMX.v)
                    for hh in range(2):
                        Pp, off = sl[hh]
                        if tkc == 0:
                            o.fop("vector", lambda v, Pp=Pp, off=off, hp=hp,
                                  hh=hh, tw=tw: v.reduce_max(
                                      macc[:, 2 * hp + hh:2 * hp + hh + 1],
                                      Pp[:, off:off + tw],
                                      axis=mybir.AxisListType.X), vMX)
                        else:
                            o.fop("vector", lambda v, Pp=Pp, off=off, tw=tw:
                                  v.reduce_max(
                                      rmn[:], Pp[:, off:off + tw],
                                      axis=mybir.AxisListType.X), vMX)
                            o.fop("vector", lambda v, hp=hp, hh=hh:
                                  v.tensor_max(
                                      macc[:, 2 * hp + hh:2 * hp + hh + 1],
                                      macc[:, 2 * hp + hh:2 * hp + hh + 1],
                                      rmn[:]), vMX)
                    vmx_after.append(vMX.v)
            o.wait("tensor", vMX, vMX.v)
            o.wait("tensor", sMX, sMX.v)
            o.op("tensor", lambda t, tqs=tqs: t.matmul(
                P1[0:16, 0:128], macc[:], G["identfn"][:, 0:128],
                start=True, stop=True), tMX)
            o.wait("scalar", tMX, tMX.v)
            o.op("scalar", lambda s, tqs=tqs: s.copy(
                negm[:, 128 * tqs:128 * (tqs + 1)], P1[0:16, 0:128]), sMX)
        o.wait("gpsimd", sMX, sMX.v)
        if qb > 0:
            o.wait("gpsimd", tS, tS.v)
        o.op("gpsimd", lambda g: g.dma_start(
            out=qt[64:65, :], in_=negm[:]), dNM, 16)

        for h in range(16):
            hg = qb * 16 + h
            if hg >= NKST:
                o.wait("vector", tS, ts_after_head[hg - NKST])
            o.op("vector", lambda v, h=h, w0=w0, w1=w1, hg=hg: v.tensor_copy(
                kst[hg % NKST][0:64, 0:w1 - w0],
                k_pack[(h % 2) * 64:(h % 2) * 64 + 64,
                       2048 * (h // 2) + w0:2048 * (h // 2) + w1]), vKC)
            ngrp = (ntile + 1) // 2
            for g in range(ngrp):
                t0 = 2 * g
                tiles = [t0] if t0 + 1 >= ntile else [t0, t0 + 1]
                Pg = P2 if gg % 2 == 0 else P3
                pb = psb[gg % 2]
                o.wait("tensor", vKC, vKC.v)
                o.wait("tensor", dNM, dNM.v)
                o.wait("tensor", vMX, vMX.v)
                if gg >= 2:
                    o.wait("tensor", sEX, ex_after_group[gg - 2])

                for ti in tiles:
                    o.op("tensor", lambda t, Pg=Pg, ti=ti, t0=t0, h=h,
                         hg=hg: t.matmul(
                             Pg[:, 512 * (ti - t0):512 * (ti - t0 + 1)],
                             kst[hg % NKST][0:66, 128 * ti:128 * (ti + 1)],
                             qt[0:66, 512 * h:512 * (h + 1)],
                             start=True, stop=True), tS)
                o.wait("scalar", tS, tS.v)
                if gg >= 2:
                    o.wait("scalar", tPV, pv_after_group[gg - 2])
                nel = 512 * len(tiles)
                o.op("scalar", lambda s, Pg=Pg, pb=pb, nel=nel: s.activation(
                    pb[:, 0:nel], Pg[:, 0:nel], AF.Exp), sEX)
                ex_after_group.append(sEX.v)
                need_mask = []
                for ti in tiles:
                    midx = cfg.maskidx(qb, w0 + 128 * ti)
                    if midx is not None:
                        need_mask.append((ti - t0, midx))
                if need_mask:
                    o.wait("vector", sEX, sEX.v)
                    o.wait("vector", dMK, dMK.v)
                    for rel, midx in need_mask:
                        o.op("vector", lambda v, pb=pb, rel=rel, midx=midx:
                             v.tensor_mul(
                                 pb[:, 512 * rel:512 * (rel + 1)],
                                 pb[:, 512 * rel:512 * (rel + 1)],
                                 masks[:, 512 * midx:512 * (midx + 1)]), vMK)
                o.wait("tensor", sEX, sEX.v)
                if need_mask:
                    o.wait("tensor", vMK, vMK.v)
                if first_pv:
                    o.wait("tensor", vPRE, vPRE.v)
                    first_pv = False
                if g == 0 and hg > 0:
                    o.wait("tensor", sPV, sPV.v)
                for idx, ti in enumerate(tiles):
                    gt = (w0 + 128 * ti) // 128
                    o.op("tensor", lambda t, pb=pb, idx=idx, gt=gt, h=h,
                         first=(g == 0 and idx == 0),
                         last=(ti == ntile - 1): t.matmul(
                             P1[0:65, 512:1024],
                             vt[:, (h * 16 + gt) * 65:(h * 16 + gt) * 65 + 65],
                             pb[:, 512 * idx:512 * (idx + 1)],
                             start=first, stop=last), tPV)
                pv_after_group.append(tPV.v)
                gg += 1
            ts_after_head.append(tS.v)
            o.wait("scalar", tPV, tPV.v)
            o.wait("scalar", tMX, tMX.v)
            if hg == 0:
                o.wait("scalar", vPRE, vPRE.v)
            else:
                o.wait("scalar", tO, tO.v)
            o.op("scalar", lambda s, h=h: s.copy(
                araw[(h % 2) * 64:(h % 2) * 64 + 64,
                     512 * (h // 2):512 * (h // 2) + 512],
                P1[0:64, 512:1024]), sPV)
            o.op("scalar", lambda s, h=h: s.copy(
                den4[32 * (h // 4):32 * (h // 4) + 1,
                     512 * (h % 4):512 * (h % 4) + 512],
                P1[64:65, 512:1024]), sPV)

        o.wait("sync", sPV, sPV.v)
        o.wait("sync", vRC, vRC.v)
        o.op("sync", lambda sy: sy.dma_start(
            out=rden[:],
            in_=den4[0:128:32, :].rearrange("p (c e) -> p c e", e=512)),
            dDN, 16)
        o.wait("vector", dDN, dDN.v)
        o.fop("vector", lambda v: v.reciprocal(den[:], rden[:]))
        o.op("vector", lambda v: v.tensor_copy(rden[:], den[:]), vRC)
        for p in range(8):
            Pp, off = (P2, 0) if p % 2 == 0 else (P2, 512)
            o.wait("tensor", vRC, vRC.v)
            if p < 2:
                o.wait("tensor", sEX, sEX.v)
            else:
                o.wait("tensor", sBC, sBC.v - 1)
                o.wait("tensor", vNE, vNE.v - 1)
            o.op("tensor", lambda t, Pp=Pp, off=off, p=p: t.matmul(
                Pp[:, off:off + 512], selp[:, 128 * p:128 * (p + 1)],
                rden[:], start=True, stop=True), tBC)
            o.wait("scalar", tBC, tBC.v)
            o.op("scalar", lambda s, Pp=Pp, off=off, p=p: s.copy(
                rb[p % 2][:], Pp[:, off:off + 512]), sBC)
            o.wait("vector", sBC, sBC.v)
            o.op("vector", lambda v, p=p: v.tensor_mul(
                araw[:, 512 * p:512 * (p + 1)],
                araw[:, 512 * p:512 * (p + 1)], rb[p % 2][:]), vNE)

        o.wait("tensor", dWo, dWo.v)
        for m in range(8):
            Pp, off = (P3, 0) if m % 2 == 0 else (P3, 512)
            if m < 2:
                o.wait("tensor", sEX, sEX.v)
            else:
                o.wait("tensor", vRS, vRS.v - 1)
            o.wait("tensor", vNE, vNE.v)
            for k in range(8):
                o.op("tensor", lambda t, Pp=Pp, off=off, m=m, k=k: t.matmul(
                    Pp[:, off:off + 512],
                    wo[:, 1024 * k + 128 * m:1024 * k + 128 * (m + 1)],
                    araw[:, 512 * k:512 * (k + 1)],
                    start=(k == 0), stop=(k == 7)), tO)
            o.wait("vector", tO, tO.v)
            if m == 0:
                o.wait("vector", sLN, sLN.v)
            o.op("vector", lambda v, Pp=Pp, off=off, m=m, qs=qs:
                 v.scalar_tensor_tensor(
                     yb[:, 512 * m:512 * (m + 1)], Pp[:, off:off + 512],
                     bo_s[:, m:m + 1], x_bf[:, T * m + qs:T * m + qs + QB],
                     ALU.add, ALU.add), vRS)

        o.wait("scalar", vRS, vRS.v)
        o.wait("scalar", vXW, vXW.v)
        o.wait("tensor", sBC, sBC.v)
        o.wait("tensor", vRS, vRS.v)
        o.wait("vector", vXW, vXW.v)
        LN.emit(o, G, P2, P3, yb, qs)

    o.flush(block)
    ctx.close()
    return vXW, vXW.v



def emit_quant(o, eng_v, eng_s, out_ap, in_ap, rs_ap, Q, sem,
               clip=False):
    qv, qi, qf, qg, vq, sc = Q.qv, Q.qi, Q.qf, Q.qg, Q.vq, Q.sc
    w = out_ap.shape[-1]
    if Q.last is not None:
        o.wait(eng_v, Q.last[0], Q.last[1])
    o.wait(eng_v, sc, sc.v)
    o.op(eng_v, lambda v: v.tensor_scalar(
        qv[:, 0:w], in_ap, rs_ap, 0.5, ALU.mult, ALU.add), vq)
    o.wait(eng_s, vq, vq.v)
    o.fop(eng_s, lambda s: s.copy(qi[:, 0:w], qv[:, 0:w]))
    o.op(eng_s, lambda s: s.copy(qf[:, 0:w], qi[:, 0:w]), sc)
    o.wait(eng_v, sc, sc.v)
    o.fop(eng_v, lambda v: v.tensor_tensor(
        qg[:, 0:w], qf[:, 0:w], qv[:, 0:w], ALU.is_gt))
    if clip:
        o.fop(eng_v, lambda v: v.scalar_tensor_tensor(
            qv[:, 0:w], qg[:, 0:w], -1.0, qf[:, 0:w], ALU.mult, ALU.add))
        fin = sem if sem is not None else vq
        o.op(eng_v, lambda v: v.tensor_scalar(
            out_ap, qv[:, 0:w], -128.0, 127.0, ALU.max, ALU.min), fin)
    else:
        fin = sem if sem is not None else vq
        o.op(eng_v, lambda v: v.scalar_tensor_tensor(
            out_ap, qg[:, 0:w], -1.0, qf[:, 0:w], ALU.mult, ALU.add), fin)
    Q.last = (fin, fin.v)


class QPack:
    def __init__(self, qv, qi, qf, qg, vq, sc):
        self.qv, self.qi, self.qf, self.qg = qv, qi, qf, qg
        self.vq, self.sc = vq, sc
        self.last = None


def phase_ffn0(nc, block, G, bar, barv):
    name = "f0"
    ctx = ExitStack()
    o = Orch(nc, ctx, G['semreg'], G['semctx'])
    sb = lambda n, s, d: ctx.enter_context(
        nc.sbuf_tensor(f"{name}S_{n}", s, d))
    ps = lambda n, s: ctx.enter_context(
        nc.psum_tensor(f"{name}P_{n}", s, F32))
    x_bf = G["x_bf"]
    LN = G["LN"]
    NI = 22

    g_sb = sb("g", [128, NI * 2048], BF16)
    qv = sb("qv", [128, 512], F32)
    qi = sb("qi", [128, 512], I32)
    qf = sb("qf", [128, 512], BF16)
    qg = sb("qg", [128, 512], BF16)
    sgx = sb("sgx", [128, NI], F32)
    sux = sb("sux", [128, NI], F32)
    rsc = sb("rsc", [128, 4], F32)
    rmx = sb("rmx", [128, 1], F32)
    xm8 = sb("xm8", [128, 8], F32)
    gmA = sb("gmA", [128, 1], F32)
    gmT = sb("gmT", [128, 1], F32)
    sc1 = sb("sc1", [1, 16], F32)
    scb = sb("scb", [128, 4], F32)
    hh = [sb(f"hh{i}", [128, 512], BF16) for i in range(2)]
    sub = ExitStack()
    sb1 = lambda n, s, d: sub.enter_context(
        nc.sbuf_tensor(f"{name}S_{n}", s, d))
    xh = sb1("xh", [128, 8 * 2048], BF16)
    wnat = [sb1(f"wnat{i}", [128, 1024], F32) for i in range(2)]
    wqn = [sb1(f"wqn{i}", [128, 1024], BF16) for i in range(2)]
    wT = [sb1(f"wT{i}", [128, 8 * 128], BF16) for i in range(4)]
    sil = [sb1(f"sil{i}", [128, 512], F32) for i in range(2)]
    sub.close()
    wdT = sb("wdT", [128, NI * 1024], BF16)
    wdn = sb("wdn", [128, 1408], F32)
    wdq = sb("wdq", [128, 1024], BF16)
    yb = sb("yb", [128, 8 * 512], BF16)
    sdh = sb("sdh", [128, 8], F32)

    P0, P1, P2, P3 = G["P0"], G["P1"], G["P2"], G["P3"]

    wg_h = G["f0_wg_h"]
    wu_h = G["f0_wu_h"]
    wd_h = G["f0_wd_h"]
    clipx_h = G["f0_clipx_h"]
    clipd_h = G["f0_clipd_h"]
    cc0_in, cc0_out = G["cc0_in"], G["cc0_out"]
    cc1_in, cc1_out = G["cc1_in"], G["cc1_out"]

    dL = o.sem("at_dL")
    dCC = o.sem("at_dCC")
    cc = o.sem("at_cc")
    dBC = o.sem("at_dBC")
    vST = o.sem("at_vST")
    sST = o.sem("at_sST")
    tST = o.sem("at_tST")
    vXH = o.sem("at_vXH")
    vWQ = o.sem("at_vWQ")
    tWT = o.sem("at_tWT")
    sWT = o.sem("at_sWT")
    tGU = o.sem("at_tGU")
    sSI = o.sem("at_sSI")
    vG = o.sem("at_vG")
    vGM = o.sem("at_vGM")
    tDN = o.sem("at_tDN")
    vHQ = o.sem("at_vHQ")
    vRS = o.sem("at_vRS")
    vq_s = o.sem("at_vq")
    sc_s = o.sem("at_sc")
    Q = QPack(qv, qi, qf, qg, vq_s, sc_s)
    vXW = LN.vXW

    o.wait("vector", bar, barv)
    for c in range(8):
        (o.fop if c == 7 else o.op)("vector", lambda v, c=c: v.reduce_max(
            xm8[:, c:c + 1], x_bf[:, T * c:T * (c + 1)],
            axis=mybir.AxisListType.X, apply_absolute_value=True))
    o.op("vector", lambda v: v.reduce_max(
        rmx[:], xm8[:], axis=mybir.AxisListType.X), vST)
    o.wait("tensor", bar, barv)
    o.wait("tensor", vST, vST.v)
    o.op("tensor", lambda t: t.matmul(
        P0[0:1, 0:128], rmx[:], G["identf"][:], start=True, stop=True), tST)
    o.wait("vector", tST, tST.v)
    o.op("vector", lambda v: v.reduce_max(
        sc1[0:1, 0:1], P0[0:1, 0:128], axis=mybir.AxisListType.X), vST)
    o.wait("sync", bar, barv)
    o.wait("sync", vST, vST.v)
    o.op("sync", lambda sy: sy.dma_start(out=cc0_in, in_=sc1[0:1, 0:1]),
         dCC, 16)
    o.wait("gpsimd", bar, barv)
    o.wait("gpsimd", dCC, dCC.v)
    o.op("gpsimd", lambda g: g.collective_compute(
        "AllGather", ALU.bypass,
        replica_groups=[[0, 1, 2, 3, 4, 5, 6, 7]],
        ins=[cc0_in], outs=[cc0_out]), cc)
    o.wait("sync", cc, cc.v)
    o.op("sync", lambda sy: sy.dma_start(out=sc1[0:1, 8:16], in_=cc0_out),
         dCC, 16)
    o.op("sync", lambda sy: sy.dma_start(
        out=sc1[0:1, 1:2], in_=clipx_h), dCC, 16)
    o.op("sync", lambda sy: sy.dma_start(
        out=sc1[0:1, 2:3], in_=clipd_h), dCC, 16)
    o.wait("vector", dCC, dCC.v)
    o.fop("vector", lambda v: v.reduce_max(
        sc1[0:1, 3:4], sc1[0:1, 8:16], axis=mybir.AxisListType.X), vST)
    o.wait("scalar", bar, barv)
    o.wait("scalar", dCC, dCC.v)
    o.op("scalar", lambda s: s.activation(
        sc1[0:1, 4:5], sc1[0:1, 1:2], AF.Sigmoid), sST)
    o.wait("vector", sST, sST.v)
    o.fop("vector", lambda v: v.tensor_mul(
        sc1[0:1, 5:6], sc1[0:1, 4:5], sc1[0:1, 3:4]))
    o.fop("vector", lambda v: v.tensor_scalar_mul(
        sc1[0:1, 5:6], sc1[0:1, 5:6], 1.0 / 127.0))
    o.fop("vector", lambda v: v.reciprocal(
        sc1[0:1, 6:7], sc1[0:1, 5:6]), vST)
    o.wait("sync", vST, vST.v)
    o.op("sync", lambda sy: sy.dma_start(
        out=scb[:, 0:1], in_=sc1[0:1, 5:6].to_broadcast([1, 128, 1])),
        dBC, 16)
    o.op("sync", lambda sy: sy.dma_start(
        out=scb[:, 1:2], in_=sc1[0:1, 6:7].to_broadcast([1, 128, 1])),
        dBC, 16)
    o.wait("vector", dBC, dBC.v)
    for c in range(8):
        for n in range(4):
            emit_quant(o, "vector", "scalar",
                       xh[:, 2048 * c + 512 * n:2048 * c + 512 * (n + 1)],
                       x_bf[:, T * c + 512 * n:T * c + 512 * (n + 1)],
                       scb[:, 1:2], Q, vXH, clip=True)

    if DUMP == "xh":
        G["dump_t"] = xh
        o.wait("vector", dBC, dBC.v)
        o.op("vector", lambda v: v.tensor_copy(xh[0:1, 16352:16368],
                                               sc1[0:1, 0:16]))
        o.op("vector", lambda v: v.tensor_copy(xh[:, 16368:16372],
                                               scb[:, 0:4]), vXH)
    if DUMP == "g":
        G["dump_t"] = g_sb
    if F0STEP < 2:
        o.flush(block)
        ctx.close()
        return vXH, vXH.v
    o.wait("sync", bar, barv)
    gslots = [(P0, 0), (P0, 512), (P1, 0), (P1, 512)]
    uslots = [(P2, 0), (P2, 512), (P3, 0), (P3, 512)]
    silu_after = {}
    g_after = {}
    wq_after = []
    g_after_gidx = []
    for i in range(NI):
        bw = i % 2
        if i >= 1:
            o.wait("sync", vWQ, wq_after[i - 1])
        o.op("sync", lambda sy, i=i: sy.dma_start(
            out=wnat[0][:], in_=wg_h[128 * i:128 * (i + 1), :]), dL, 16)
        o.op("sync", lambda sy, i=i: sy.dma_start(
            out=wnat[1][:], in_=wu_h[128 * i:128 * (i + 1), :]), dL, 16)
        o.wait("vector", dL, dL.v)
        for wi, sall in enumerate((sgx, sux)):
            wn = wnat[wi]
            wq_ = wqn[wi]
            o.fop("vector", lambda v, wn=wn: v.reduce_max(
                rmx[:], wn[:], axis=mybir.AxisListType.X,
                apply_absolute_value=True))
            o.fop("vector", lambda v, sall=sall, i=i: v.tensor_scalar_mul(
                sall[:, i:i + 1], rmx[:], 1.0 / 127.0))
            rcol = rsc[:, wi:wi + 1]
            o.fop("vector", lambda v, rcol=rcol, sall=sall, i=i:
                 v.reciprocal(rcol, sall[:, i:i + 1]))
            if i >= 1:
                o.wait("vector", tWT, 16 * i)
            for n2 in range(2):
                emit_quant(o, "vector", "scalar",
                           wq_[:, 512 * n2:512 * (n2 + 1)],
                           wn[:, 512 * n2:512 * (n2 + 1)],
                           rcol, Q, vWQ if n2 == 1 else None)
            o.op("vector", lambda v, sall=sall, i=i: v.tensor_scalar(
                sall[:, i:i + 1], sall[:, i:i + 1], scb[:, 0:1], None,
                ALU.mult), vWQ)
        wq_after.append(vWQ.v)
        o.wait("tensor", vWQ, vWQ.v)
        if i >= 1:
            o.wait("tensor", sSI, silu_after[(i - 1, 3)])
            o.wait("tensor", vG, g_after[(i - 1, 3)])
        if i == 0:
            o.wait("tensor", vST, 2)
        for wi in range(2):
            wq_ = wqn[wi]
            wt_ = wT[2 * wi + bw]
            for hc in range(8):
                pslot = P0 if wi == 0 else P1
                poff = 0 if hc % 2 == 0 else 512
                jj = i * 16 + wi * 8 + hc
                if jj >= 2:
                    o.wait("tensor", sWT, jj - 1)
                o.op("tensor", lambda t, wq_=wq_, hc=hc, pslot=pslot,
                     poff=poff: t.matmul(
                         pslot[:, poff:poff + 128],
                         wq_[:, 128 * hc:128 * (hc + 1)], G["identb"][:],
                         start=True, stop=True), tWT)
                o.wait("scalar", tWT, tWT.v)
                o.op("scalar", lambda s, wt_=wt_, hc=hc, pslot=pslot,
                     poff=poff: s.copy(
                         wt_[:, 128 * hc:128 * (hc + 1)],
                         pslot[:, poff:poff + 128]), sWT)
        o.wait("tensor", sWT, sWT.v)
        if i == 0:
            o.wait("tensor", vXH, 32)
        for n in range(4):
            gP, go = gslots[n]
            uP, uo = uslots[n]
            if i >= 1:
                o.wait("tensor", sSI, silu_after[(i - 1, n)])
                o.wait("tensor", vG, g_after[(i - 1, n)])
            for hc in range(8):
                o.op("tensor", lambda t, gP=gP, go=go, bw=bw, hc=hc, n=n:
                     t.matmul(
                         gP[:, go:go + 512],
                         wT[0 + bw][:, 128 * hc:128 * (hc + 1)],
                         xh[:, 2048 * hc + 512 * n:2048 * hc + 512 * (n + 1)],
                         start=(hc == 0), stop=(hc == 7)), None)
                o.op("tensor", lambda t, uP=uP, uo=uo, bw=bw, hc=hc, n=n:
                     t.matmul(
                         uP[:, uo:uo + 512],
                         wT[2 + bw][:, 128 * hc:128 * (hc + 1)],
                         xh[:, 2048 * hc + 512 * n:2048 * hc + 512 * (n + 1)],
                         start=(hc == 0), stop=(hc == 7)),
                     tGU if hc == 7 else None)
            o.wait("scalar", tGU, tGU.v)
            gidx = 4 * i + n
            if gidx >= 2:
                o.wait("scalar", vG, g_after_gidx[gidx - 2])
            o.op("scalar", lambda s, gP=gP, go=go, i=i, n=n: s.activation(
                sil[n % 2][:], gP[:, go:go + 512], AF.Silu,
                scale=sgx[:, i:i + 1]), sSI)
            silu_after[(i, n)] = sSI.v
            o.wait("vector", sSI, sSI.v)
            o.op("vector", lambda v, uP=uP, uo=uo, i=i, n=n:
                 v.scalar_tensor_tensor(
                     g_sb[:, 2048 * i + 512 * n:2048 * i + 512 * (n + 1)],
                     uP[:, uo:uo + 512], sux[:, i:i + 1], sil[n % 2][:],
                     ALU.mult, ALU.mult), vG)
            g_after[(i, n)] = vG.v
            g_after_gidx.append(vG.v)
        o.wait("vector", vG, vG.v)
        o.fop("vector", lambda v, i=i: v.reduce_max(
            gmT[:], g_sb[:, 2048 * i:2048 * (i + 1)],
            axis=mybir.AxisListType.X, apply_absolute_value=True))
        if i == 0:
            o.fop("vector", lambda v: v.tensor_copy(gmA[:], gmT[:]))
        elif i < NI - 1:
            o.fop("vector", lambda v: v.tensor_max(gmA[:], gmA[:], gmT[:]))
        else:
            o.fop("vector", lambda v: v.tensor_max(
                gmA[:], gmA[:], gmT[:]), vGM)

    if DUMP == "wt":
        G["dump_t"] = g_sb
        o.wait("vector", vG, vG.v)
        o.op("vector", lambda v: v.tensor_copy(g_sb[:, 0:1024], wT[0][:]))
        o.op("vector", lambda v: v.tensor_copy(g_sb[:, 1024:2048],
                                               wqn[0][:]))
        o.op("vector", lambda v: v.tensor_copy(g_sb[:, 2048:2070],
                                               sgx[:, 0:22]))
        o.op("vector", lambda v: v.tensor_copy(g_sb[:, 2070:2092],
                                               sux[:, 0:22]), vG)
    if F0STEP < 3:
        o.flush(block)
        ctx.close()
        return vG, vG.v
    o.wait("tensor", sSI, sSI.v)
    o.wait("tensor", vG, vG.v)
    o.wait("tensor", vGM, 1)
    o.op("tensor", lambda t: t.matmul(
        P0[0:1, 0:128], gmA[:], G["identf"][:], start=True, stop=True), tST)
    o.wait("vector", tST, tST.v)
    o.op("vector", lambda v: v.reduce_max(
        sc1[0:1, 9:10], P0[0:1, 0:128], axis=mybir.AxisListType.X), vST)
    o.wait("sync", vST, vST.v)
    o.op("sync", lambda sy: sy.dma_start(out=cc1_in, in_=sc1[0:1, 9:10]),
         dCC, 16)
    o.wait("gpsimd", dCC, dCC.v)
    o.op("gpsimd", lambda g: g.collective_compute(
        "AllGather", ALU.bypass,
        replica_groups=[[0, 1, 2, 3, 4, 5, 6, 7]],
        ins=[cc1_in], outs=[cc1_out]), cc)
    o.wait("sync", cc, cc.v)
    o.op("sync", lambda sy: sy.dma_start(out=sc1[0:1, 8:16], in_=cc1_out),
         dCC, 16)
    o.wait("vector", dCC, dCC.v)
    o.fop("vector", lambda v: v.reduce_max(
        sc1[0:1, 10:11], sc1[0:1, 8:16], axis=mybir.AxisListType.X), vST)
    o.wait("scalar", vST, vST.v)
    o.op("scalar", lambda s: s.activation(
        sc1[0:1, 11:12], sc1[0:1, 2:3], AF.Sigmoid), sST)
    o.wait("vector", sST, sST.v)
    o.fop("vector", lambda v: v.tensor_mul(
        sc1[0:1, 12:13], sc1[0:1, 11:12], sc1[0:1, 10:11]))
    o.fop("vector", lambda v: v.tensor_scalar_mul(
        sc1[0:1, 12:13], sc1[0:1, 12:13], 1.0 / 127.0))
    o.fop("vector", lambda v: v.reciprocal(
        sc1[0:1, 13:14], sc1[0:1, 12:13]), vST)
    o.wait("sync", vST, vST.v)
    o.op("sync", lambda sy: sy.dma_start(
        out=scb[:, 2:3], in_=sc1[0:1, 12:13].to_broadcast([1, 128, 1])),
        dBC, 16)
    o.op("sync", lambda sy: sy.dma_start(
        out=scb[:, 3:4], in_=sc1[0:1, 13:14].to_broadcast([1, 128, 1])),
        dBC, 16)

    o.wait("sync", tGU, tGU.v)
    o.wait("sync", vG, vG.v)
    segw = [1024, 1024, 768]
    for c in range(8):
        o.wait("sync", vST, vST.v)
        o.wait("sync", vWQ, vWQ.v)
        o.op("sync", lambda sy, c=c: sy.dma_start(
            out=wdn[:], in_=wd_h[128 * c:128 * (c + 1), 0:1408]), dL, 16)
        o.wait("vector", dL, dL.v)
        o.fop("vector", lambda v: v.reduce_max(
            rmx[:], wdn[:], axis=mybir.AxisListType.X,
            apply_absolute_value=True), vST)
        o.wait("sync", vST, vST.v)
        o.op("sync", lambda sy, c=c: sy.dma_start(
            out=wdn[:], in_=wd_h[128 * c:128 * (c + 1), 1408:2816]), dL, 16)
        o.wait("vector", dL, dL.v)
        o.fop("vector", lambda v: v.reduce_max(
            gmT[:], wdn[:], axis=mybir.AxisListType.X,
            apply_absolute_value=True))
        o.fop("vector", lambda v: v.tensor_max(rmx[:], rmx[:], gmT[:]))
        o.fop("vector", lambda v, c=c: v.tensor_scalar_mul(
            sdh[:, c:c + 1], rmx[:], 1.0 / 127.0))
        o.fop("vector", lambda v, c=c: v.reciprocal(
            rsc[:, 0:1], sdh[:, c:c + 1]), vST)
        for seg in range(3):
            sw = segw[seg]
            s0 = 1024 * seg
            o.wait("sync", vWQ, vWQ.v)
            o.wait("sync", vST, vST.v)
            o.op("sync", lambda sy, c=c, s0=s0, sw=sw: sy.dma_start(
                out=wdn[:, 0:sw], in_=wd_h[128 * c:128 * (c + 1),
                                           s0:s0 + sw]), dL, 16)
            o.wait("vector", dL, dL.v)
            o.wait("vector", tWT, tWT.v)
            for n3 in range(2):
                w0c = 512 * n3
                w1c = min(sw, w0c + 512)
                if w0c >= sw:
                    continue
                emit_quant(o, "vector", "scalar",
                           wdq[:, w0c:w1c], wdn[:, w0c:w1c], rsc[:, 0:1],
                           Q, vWQ)
            o.wait("tensor", vWQ, vWQ.v)
            if c == 0 and seg == 0:
                o.wait("tensor", vST, vST.v)
            nic = sw // 128
            for ic2 in range(nic):
                ic = seg * 8 + ic2
                jj = c * NI + ic
                slot = jj % 8
                Pp = (P0, P1, P2, P3)[slot // 2]
                poff = (slot % 2) * 512
                if jj >= 8:
                    o.wait("tensor", sWT, sWT.v - 7)
                o.op("tensor", lambda t, ic2=ic2, Pp=Pp, poff=poff: t.matmul(
                    Pp[:, poff:poff + 128],
                    wdq[:, 128 * ic2:128 * (ic2 + 1)], G["identb"][:],
                    start=True, stop=True), tWT)
                o.wait("scalar", tWT, tWT.v)
                o.op("scalar", lambda s, ic=ic, c=c, Pp=Pp, poff=poff: s.copy(
                    wdT[:, 1024 * ic + 128 * c:1024 * ic + 128 * (c + 1)],
                    Pp[:, poff:poff + 128]), sWT)
    o.wait("vector", dBC, dBC.v)
    o.fop("vector", lambda v: v.tensor_scalar(
        sdh[:], sdh[:], scb[:, 2:3], None, ALU.mult), vST)

    if F0STEP < 4:
        o.flush(block)
        ctx.close()
        return vST, vST.v
    dslots = [(P0, 0), (P0, 512), (P1, 0), (P1, 512),
              (P2, 0), (P2, 512), (P3, 0), (P3, 512)]
    for tc in range(4):
        qs = 512 * tc
        for ic in range(NI):
            b = ic % 2
            if ic >= 2:
                o.wait("vector", tDN, tDN.v - 8)
            if ic == 0:
                o.wait("vector", vST, vST.v)
            emit_quant(o, "vector", "scalar",
                       hh[b][:], g_sb[:, 2048 * ic + qs:2048 * ic + qs + 512],
                       scb[:, 3:4], Q, vHQ, clip=True)
            o.wait("tensor", vHQ, vHQ.v)
            if ic == 0:
                o.wait("tensor", sWT, sWT.v)
                o.wait("tensor", vRS, vRS.v)
                o.wait("tensor", LN.vLS, LN.vLS.v)
                o.wait("tensor", LN.sLB, LN.sLB.v)
            for dm in range(8):
                Pp, poff = dslots[dm]
                o.op("tensor", lambda t, ic=ic, dm=dm, b=b, Pp=Pp, poff=poff:
                     t.matmul(
                         Pp[:, poff:poff + 512],
                         wdT[:, 1024 * ic + 128 * dm:
                             1024 * ic + 128 * (dm + 1)],
                         hh[b][:], start=(ic == 0), stop=(ic == NI - 1)),
                     tDN)
        o.wait("vector", tDN, tDN.v)
        o.wait("vector", vXW, vXW.v)
        for dm in range(8):
            Pp, poff = dslots[dm]
            o.op("vector", lambda v, dm=dm, Pp=Pp, poff=poff, qs=qs:
                 v.scalar_tensor_tensor(
                     yb[:, 512 * dm:512 * (dm + 1)], Pp[:, poff:poff + 512],
                     sdh[:, dm:dm + 1], x_bf[:, T * dm + qs:T * dm + qs + QB],
                     ALU.mult, ALU.add), vRS)
        o.wait("scalar", vRS, vRS.v)
        o.wait("scalar", vXW, vXW.v)
        o.wait("tensor", vRS, vRS.v)
        o.wait("vector", vXW, vXW.v)
        LN.emit(o, G, P0, P1, yb, qs)

    o.flush(block)
    ctx.close()
    return vXW, vXW.v



def phase_ffn1(nc, block, G, bar, barv):
    name = "f1"
    ctx = ExitStack()
    o = Orch(nc, ctx, G['semreg'], G['semctx'])
    sb = lambda n, s, d: ctx.enter_context(
        nc.sbuf_tensor(f"{name}S_{n}", s, d))
    x_bf = G["x_bf"]
    LN = G["LN"]
    P0, P1, P2, P3 = G["P0"], G["P1"], G["P2"], G["P3"]

    Mbuf = [sb(f"Mb{i}", [128, 32 * 128], BF16) for i in range(2)]
    bexp = sb("bexp", [128, 32], F32)
    ubB = sb("ubB", [128, 32], F32)
    bsum = sb("bsum", [128, 8], F32)
    bc1 = sb("bc1", [128, 8], F32)
    bc2 = sb("bc2", [128, 8], F32)
    b0 = sb("b0", [128, 32 * 512], BF16)
    b1 = sb("b1", [128, 32 * 512], BF16)
    yb = sb("yb", [128, 8 * 512], BF16)
    wcon = [sb(f"wcon{i}", [128, 1024], BF16) for i in range(2)]
    sub = ExitStack()
    sb1 = lambda n, s, d: sub.enter_context(
        nc.sbuf_tensor(f"{name}S_{n}", s, d))
    uacc = sb1("uacc", [128, 4096], F32)
    ubuf = [sb1(f"ubuf{i}", [128, 4096], F32) for i in range(2)]
    mres = [sb1(f"mres{i}", [128, 4096], BF16) for i in range(2)]
    sub.close()
    wexp = sb("wexp", [128, 8 * 4096], BF16)

    dU = [o.sem(f"{name}_dU0"), o.sem(f"{name}_dU1")]
    dMS = [o.sem(f"{name}_dMS0"), o.sem(f"{name}_dMS1")]
    dMB = [o.sem(f"{name}_dMB0"), o.sem(f"{name}_dMB1")]
    vCM = o.sem(f"{name}_vCM")
    dWX = o.sem(f"{name}_dWX")
    dWC = [o.sem(f"{name}_dWC0"), o.sem(f"{name}_dWC1")]
    dB2 = o.sem(f"{name}_dB2")
    vBS = o.sem(f"{name}_vBS")
    tEX = o.sem(f"{name}_tEX")
    sEV = o.sem(f"{name}_sEV")
    tAP = o.sem(f"{name}_tAP")
    sAE = o.sem(f"{name}_sAE")
    dTN = o.sem(f"{name}_dTN")
    tCN = o.sem(f"{name}_tCN")
    vRS = o.sem(f"{name}_vRS")
    vXW = LN.vXW

    NST = [7, 5, 7, 5]
    UH = [G[f"f1_U{g}_h"] for g in range(4)]

    o.wait("sync", bar, barv)
    nld = 0
    for g in range(4):
        for st in range(NST[g]):
            if nld >= 2:
                o.wait("sync", vCM, nld - 1)
            o.op("sync", lambda sy, g=g, st=st, b=nld % 2: sy.dma_start(
                out=ubuf[b][:], in_=UH[g][st]), dU[nld % 2], 16)
            o.wait("vector", dU[nld % 2], dU[nld % 2].v)
            if g == 0 and st == 0:
                o.wait("vector", bar, barv)
            if st == 0:
                if g >= 2:
                    o.wait("vector", dMS[g % 2], 16)
                o.fop("vector", lambda v, b=nld % 2: v.tensor_copy(
                    uacc[:], ubuf[b][:]), vCM)
            elif st < NST[g] - 1:
                o.fop("vector", lambda v, b=nld % 2: v.tensor_mul(
                    uacc[:], uacc[:], ubuf[b][:]), vCM)
            else:
                o.fop("vector", lambda v, g=g, b=nld % 2: v.tensor_mul(
                    mres[g % 2][:], uacc[:], ubuf[b][:]), vCM)
                o.wait("sync", vCM, vCM.v)
                o.op("sync", lambda sy, g=g: sy.dma_start(
                    out=G["f1_Mscr"][g], in_=mres[g % 2][:]), dMS[g % 2], 16)
            nld += 1

    o.wait("gpsimd", bar, barv)
    o.wait("gpsimd", vCM, vCM.v)
    o.wait("gpsimd", dMS[0], 32)
    o.wait("gpsimd", dMS[1], 32)
    for k in range(8):
        o.op("gpsimd", lambda gg, k=k: gg.dma_start(
            out=wexp[:, 4096 * k:4096 * (k + 1)],
            in_=G["f1_wexp_h"][128 * k:128 * (k + 1), :]), dWX, 16)
    o.op("sync", lambda sy: sy.dma_start(
        out=bexp[:], in_=G["f1_bexp_h"]), dB2, 16)
    o.op("sync", lambda sy: sy.dma_start(
        out=ubB[:], in_=G["f1_ubB_h"]), dB2, 16)
    o.op("sync", lambda sy: sy.dma_start(
        out=bc1[:], in_=G["f1_bcon_h"]), dB2, 16)
    o.op("sync", lambda sy: sy.dma_start(
        out=bc2[:], in_=G["f1_dbias_h"]), dB2, 16)
    o.wait("vector", dB2, dB2.v)
    o.fop("vector", lambda v: v.tensor_add(bsum[:], bc1[:], bc2[:]), vBS)

    aslots = [(P0, 0), (P0, 512), (P1, 0), (P1, 512)]
    dslots = [(P0, 0), (P0, 512), (P1, 0), (P1, 512),
              (P2, 0), (P2, 512), (P3, 0), (P3, 512)]
    ap_after = []

    mstep = [0]

    def emit_apply(gidx, src, dst, qs, bias=None, gelu=False):
        mi = mstep[0]
        mstep[0] += 1
        mb = Mbuf[mi % 2]
        if mi >= 2:
            o.wait("gpsimd", tAP, ap_after[mi - 2])
        o.wait("gpsimd", dMS[0], 32)
        o.wait("gpsimd", dMS[1], 32)
        o.op("gpsimd", lambda gg, gidx=gidx, mb=mb: gg.dma_start(
            out=mb[:], in_=G["f1_Mscr"][gidx]), dMB[mi % 2], 16)
        for c in range(32):
            Pp, off = aslots[c % 4]
            if c >= 4:
                o.wait("tensor", sEV, sEV.v - 3)
            else:
                o.wait("tensor", sEV, max(0, sEV.v - 3 + c))
            if c == 0:
                o.wait("tensor", dMB[mi % 2], dMB[mi % 2].v)
            o.op("tensor", lambda t, Pp=Pp, off=off, c=c, mb=mb, src=src:
                 t.matmul(
                     Pp[:, off:off + 512], mb[:, 128 * c:128 * (c + 1)],
                     src[:, 512 * c:512 * (c + 1)],
                     start=True, stop=True), tAP)
            o.wait("scalar", tAP, tAP.v)
            if gelu:
                o.op("scalar", lambda ss, Pp=Pp, off=off, c=c, dst=dst:
                     ss.activation(
                         dst[:, 512 * c:512 * (c + 1)], Pp[:, off:off + 512],
                         AF.Gelu, bias=ubB[:, c:c + 1]), sEV)
            elif bias is not None:
                o.op("scalar", lambda ss, Pp=Pp, off=off, c=c, dst=dst,
                     bias=bias: ss.activation(
                         dst[:, 512 * c:512 * (c + 1)], Pp[:, off:off + 512],
                         AF.Identity, bias=bias[:, c:c + 1]), sEV)
            else:
                o.op("scalar", lambda ss, Pp=Pp, off=off, c=c, dst=dst:
                     ss.copy(dst[:, 512 * c:512 * (c + 1)],
                             Pp[:, off:off + 512]), sEV)
        ap_after.append(tAP.v)

    def emit_turn_AB(src, dst):
        for tau in range(32):
            o.op("sync", lambda sy, tau=tau, src=src, dst=dst: sy.dma_start(
                out=dst[:, 512 * tau:512 * (tau + 1)],
                in_=src[4 * tau:4 * tau + 4, :].rearrange(
                    "p (c e) -> p c e", e=512)), dTN, 16)

    def emit_turn_BA(src, dst):
        scr = G["f1_Tscr"]
        o.op("sync", lambda sy, src=src: sy.dma_start(
            out=scr, in_=src[:]), dTN, 16)
        o.wait("sync", dTN, dTN.v)
        sv = scr.rearrange("(l c) (t e) -> c t l e", c=32, e=512)
        for c in range(32):
            o.op("sync", lambda sy, c=c, dst=dst, sv=sv: sy.dma_start(
                out=dst[:, 512 * c:512 * (c + 1)].rearrange(
                    "p (x e) -> p x e", x=1),
                in_=sv[c]), dTN, 16)

    for tc in range(4):
        qs = 512 * tc
        o.wait("tensor", bar, barv)
        if tc == 0:
            o.wait("tensor", dWX, dWX.v)
            o.wait("tensor", vCM, vCM.v)
        for m in range(32):
            Pp, off = aslots[m % 4]
            if tc > 0 or m >= 4:
                o.wait("tensor", sEV, sEV.v - 3)
            if tc > 0 and m == 0:
                o.wait("tensor", dTN, dTN.v)
                o.wait("tensor", vRS, vRS.v)
                o.wait("tensor", LN.vLS, LN.vLS.v)
                o.wait("tensor", LN.sLB, LN.sLB.v)
            for k in range(8):
                o.op("tensor", lambda t, Pp=Pp, off=off, m=m, k=k, qs=qs:
                     t.matmul(
                         Pp[:, off:off + 512],
                         wexp[:, 4096 * k + 128 * m:4096 * k + 128 * (m + 1)],
                         x_bf[:, T * k + qs:T * k + qs + QB],
                         start=(k == 0), stop=(k == 7)), tEX)
            o.wait("scalar", tEX, tEX.v)
            if tc == 0 and m == 0:
                o.wait("scalar", dB2, dB2.v)
            if tc > 0 and m == 0:
                o.wait("scalar", dTN, dTN.v)
            o.op("scalar", lambda ss, Pp=Pp, off=off, m=m: ss.activation(
                b0[:, 512 * m:512 * (m + 1)], Pp[:, off:off + 512],
                AF.Identity, bias=bexp[:, m:m + 1]), sEV)
        emit_apply(0, b0, b1, qs)
        o.wait("sync", sEV, sEV.v)
        emit_turn_AB(b1, b0)
        o.wait("tensor", dTN, dTN.v)
        emit_apply(1, b0, b1, qs, gelu=True)
        o.wait("sync", sEV, sEV.v)
        emit_turn_BA(b1, b0)
        o.wait("tensor", dTN, dTN.v)
        emit_apply(2, b0, b1, qs)
        o.wait("sync", sEV, sEV.v)
        emit_turn_AB(b1, b0)
        o.wait("tensor", dTN, dTN.v)
        emit_apply(3, b0, b1, qs)
        for k in range(32):
            bw = k % 2
            if k >= 2:
                o.wait("gpsimd", tCN, tCN.v - 8)
            o.op("gpsimd", lambda gy, k=k, bw=bw: gy.dma_start(
                out=wcon[bw][:],
                in_=G["f1_wconB_h"][128 * k:128 * (k + 1), :]), dWC[bw], 16)
            o.wait("tensor", dWC[bw], dWC[bw].v)
            if k == 0:
                o.wait("tensor", sEV, sEV.v)
                o.wait("tensor", vRS, vRS.v)
                o.wait("tensor", LN.vLS, LN.vLS.v)
                o.wait("tensor", LN.sLB, LN.sLB.v)
            for dm in range(8):
                Pp, off = dslots[dm]
                o.op("tensor", lambda t, Pp=Pp, off=off, k=k, dm=dm, bw=bw:
                     t.matmul(
                         Pp[:, off:off + 512],
                         wcon[bw][:, 128 * dm:128 * (dm + 1)],
                         b1[:, 512 * k:512 * (k + 1)],
                         start=(k == 0), stop=(k == 31)), tCN)
        o.wait("vector", tCN, tCN.v)
        o.wait("vector", vBS, 1)
        o.wait("vector", vXW, vXW.v)
        for dm in range(8):
            Pp, off = dslots[dm]
            o.op("vector", lambda v, Pp=Pp, off=off, dm=dm, qs=qs:
                 v.scalar_tensor_tensor(
                     yb[:, 512 * dm:512 * (dm + 1)], Pp[:, off:off + 512],
                     bsum[:, dm:dm + 1], x_bf[:, T * dm + qs:T * dm + qs + QB],
                     ALU.add, ALU.add), vRS)
        o.wait("scalar", vRS, vRS.v)
        o.wait("scalar", vXW, vXW.v)
        o.wait("tensor", vRS, vRS.v)
        o.wait("vector", vXW, vXW.v)
        LN.emit(o, G, P0, P1, yb, qs)

    o.flush(block)
    ctx.close()
    return vXW, vXW.v




def build_program():
    nc = bass.Bass("TRN2", target_bir_lowering=False, debug=False)
    octx = ExitStack()
    G = {}
    G["semreg"] = {}
    G["semctx"] = octx
    G["x_h"] = nc.dram_tensor("x", [T, D], F32, kind="ExternalInput").ap()
    G["out_h"] = nc.dram_tensor("out", [T, D], F32,
                                kind="ExternalOutput").ap()

    def din(nm, shape, dt=F32):
        G[nm + "_h"] = nc.dram_tensor(nm, list(shape), dt,
                                      kind="ExternalInput").ap()

    din("identf", [128, 128])
    din("identb", [128, 128], BF16)
    din("identfn", [128, 128])
    din("hmask16", [128, 8 * 16], BF16)
    din("selpair", [16, 8 * 128], BF16)
    din("onescol", [128, 1], BF16)
    din("onesrow", [1, 128], BF16)

    sbp = lambda n, s, d: octx.enter_context(nc.sbuf_tensor(n, s, d))
    for pi in range(4):
        G[f"P{pi}"] = octx.enter_context(
            nc.psum_tensor(f"g_P{pi}", [128, 1024], F32))
    G["x_bf"] = sbp("g_xbf", [128, NCH * T], BF16)
    G["identf"] = sbp("g_identf", [128, 128], F32)
    G["identb"] = sbp("g_identb", [128, 128], BF16)
    G["identfn"] = sbp("g_identfn", [128, 128], F32)
    G["hmask16"] = sbp("g_hm16", [128, 8 * 16], BF16)
    G["selpair"] = sbp("g_selp", [16, 8 * 128], BF16)
    G["onescol"] = sbp("g_onesc", [128, 1], BF16)
    G["onesrow"] = sbp("g_onesr", [1, 128], BF16)
    G["LN"] = LNPack(nc, octx)
    G["const_loads"] = [
        ("identf", G["identf"]), ("identb", G["identb"]),
        ("identfn", G["identfn"]),
        ("hmask16", G["hmask16"]), ("selpair", G["selpair"]),
        ("onescol", G["onescol"]), ("onesrow", G["onesrow"]),
    ]
    if NPHASE >= 1:
        din("a0_w", [1024, 3072])
        din("a0_b", [128, 24])
        din("a0_wo", [1024, 1024])
        din("a0_bo", [128, 8])
        din("mask_a0", [128, 6 * 512], BF16)
    if NPHASE >= 2:
        din("f0_wg", [2816, 1024])
        din("f0_wu", [2816, 1024])
        din("f0_wd", [1024, 2816])
        din("f0_clipx", [1, 1])
        din("f0_clipd", [1, 1])
        G["cc0_in"] = nc.dram_tensor("cc0_in", [1, 1], F32,
                                     kind="Internal").ap()
        G["cc0_out"] = nc.dram_tensor("cc0_out", [1, 8], F32,
                                      kind="Internal").ap()
        G["cc1_in"] = nc.dram_tensor("cc1_in", [1, 1], F32,
                                     kind="Internal").ap()
        G["cc1_out"] = nc.dram_tensor("cc1_out", [1, 8], F32,
                                      kind="Internal").ap()
    if NPHASE >= 3:
        din("a1_w", [1024, 3072])
        din("a1_b", [128, 24])
        din("a1_wo", [1024, 1024])
        din("a1_bo", [128, 8])
        din("mask_a1", [128, 4 * 512], BF16)

    if NPHASE >= 4:
        G["f1_Mscr"] = nc.dram_tensor("f1_Mscr", [4, 128, 4096], BF16,
                                      kind="Internal").ap()
        G["f1_Tscr"] = nc.dram_tensor("f1_Tscr", [128, 16384], BF16,
                                      kind="Internal").ap()
        din("f1_wexp", [1024, 4096])
        din("f1_bexp", [128, 32])
        din("f1_ubB", [128, 32])
        din("f1_bcon", [128, 8])
        din("f1_dbias", [128, 8])
        din("f1_wconB", [4096, 1024])
        for g_ in range(4):
            din(f"f1_U{g_}", [[7, 5, 7, 5][g_], 128, 4096])

    with nc.Block() as block:
        bar, barv = phase_in(nc, block, G)
        if NPHASE >= 1:
            bar, barv = phase_attention(nc, block, G, L0Cfg, bar, barv)
        if NPHASE >= 2:
            bar, barv = phase_ffn0(nc, block, G, bar, barv)
        if NPHASE >= 3:
            bar, barv = phase_attention(nc, block, G, L1Cfg, bar, barv)
        if NPHASE >= 4:
            bar, barv = phase_ffn1(nc, block, G, bar, barv)
        phase_out(nc, block, G, bar, barv)
    octx.close()
    return nc


_PROGRAM = None


def _get_program():
    global _PROGRAM
    if _PROGRAM is None:
        _PROGRAM = build_program()
    return _PROGRAM


def kernel(**inputs):
    nc = _get_program()
    x = np.ascontiguousarray(np.asarray(inputs["L0_attn_x"], np.float32))
    cm = host_prep(inputs)
    in_maps = []
    for c in range(8):
        m = dict(cm)
        m["x"] = np.ascontiguousarray(x[c])
        in_maps.append(m)
    res = run_bass_kernel_spmd(nc, in_maps, core_ids=list(range(8)))
    out = np.stack([res.results[c]["out"] for c in range(8)], axis=0)
    return out.astype(np.float32)


# version 04 (final)
# speedup vs baseline: 1.0035x; 1.0003x over previous
import numpy as np
import ml_dtypes
from contextlib import ExitStack
import concourse.bass as bass
import concourse.mybir as mybir
from concourse.bass_utils import run_bass_kernel_spmd

F32 = mybir.dt.float32
BF16 = mybir.dt.bfloat16
I32 = mybir.dt.int32
AF = mybir.ActivationFunctionType
ALU = mybir.AluOpType

T = 2048
D = 1024
H = 16
DH = 64
NCH = 8
QB = 512
NQB = T // QB
SCALE = DH ** -0.5
BF = ml_dtypes.bfloat16

NPHASE = 4
F0STEP = 4
DUMP = None

ENGINES = ("tensor", "scalar", "vector", "gpsimd", "sync")


class Sem:
    def __init__(self, nc, ctx, name):
        self.h = ctx.enter_context(nc.semaphore(name))
        self.v = 0


class Orch:

    def __init__(self, nc, ctx, reg=None, regctx=None):
        self.nc = nc
        self.ctx = ctx
        self.reg = reg if reg is not None else {}
        self.regctx = regctx if regctx is not None else ctx
        self.ops = {e: [] for e in ENGINES}

    def sem(self, name):
        if name not in self.reg:
            self.reg[name] = Sem(self.nc, self.regctx, name)
        return self.reg[name]

    def fop(self, eng, fn, sem=None, n=1):
        f = self.sem(f"fence_{eng}")
        if sem is None:
            sem = f
        else:
            self.op(eng, fn, sem, n)
            self.wait(eng, sem)
            return
        self.op(eng, fn, f, 1)
        self.wait(eng, f)

    def wait(self, eng, sem, val=None):
        v = sem.v if val is None else val
        assert v <= sem.v, f"wait on future value {v} > {sem.v}"
        if v <= 0:
            return
        self.ops[eng].append(lambda e, h=sem.h, v=v: e.wait_ge(h, v))

    def op(self, eng, fn, sem=None, n=1):
        if sem is None:
            self.ops[eng].append(lambda e, fn=fn: fn(e))
        else:
            sem.v += n
            self.ops[eng].append(
                lambda e, fn=fn, h=sem.h, n=n: fn(e).then_inc(h, n))

    def flush(self, block):
        for eng in ENGINES:
            lst = self.ops[eng]
            if not lst:
                continue

            def section(e, lst=lst):
                for fn in lst:
                    fn(e)
            getattr(block, eng)(section)
        self.ops = {e: [] for e in ENGINES}




def _bf_i_of_B(tau, pi):
    h = pi & 31
    l01 = pi >> 5
    return (h << 7) | (tau << 2) | l01


def _bf_stage_lookup(w, s, out_i, in_i):
    state = ((out_i & ((1 << s) - 1)) | (in_i & ~((2 << s) - 1))
             | (in_i & (1 << s)))
    low = state & ((1 << s) - 1)
    high = state >> (s + 1)
    k = high * (1 << s) + low
    return w[s, k, (out_i >> s) & 1, (in_i >> s) & 1]


def _bf_expand(w, stages, blayout):
    nst = len(stages)
    U = np.zeros((nst, 128, 4096), np.float32)
    pp = np.arange(128)
    smask = 0
    for s_ in stages:
        smask |= (1 << s_)
    for cc in range(32):
        if blayout:
            out_i = np.array([_bf_i_of_B(cc, q) for q in pp])[:, None]
            in_i = np.array([_bf_i_of_B(cc, q) for q in pp])[None, :]
            kmask = ~smask
        else:
            out_i = (128 * cc + pp)[:, None]
            in_i = (128 * cc + pp)[None, :]
            kmask = (~smask) & 127
        oi = np.broadcast_to(out_i, (128, 128))
        ii = np.broadcast_to(in_i, (128, 128))
        for si, s_ in enumerate(stages):
            U[si, :, 128 * cc:128 * (cc + 1)] = _bf_stage_lookup(w, s_, oi, ii)
        keep = ((oi ^ ii) & kmask) == 0
        U[0, :, 128 * cc:128 * (cc + 1)] = np.where(
            keep, U[0, :, 128 * cc:128 * (cc + 1)], 0.0)
    return np.ascontiguousarray(U)


def colchunk(vec):
    v = np.asarray(vec, np.float32)
    k = v.shape[0] // 128
    return np.ascontiguousarray(v.reshape(k, 128).T)


def l0_masks():
    out = np.zeros((6, 128, 512), BF)
    for i, r in enumerate([-128, 0, 128, 256, 384, 512]):
        p = np.arange(128)[:, None]
        t_ = np.arange(512)[None, :]
        out[i] = (np.abs(r + p - t_) <= 64).astype(BF)
    return np.ascontiguousarray(out.transpose(1, 0, 2).reshape(128, 6 * 512))


def l1_masks():
    out = np.zeros((4, 128, 512), BF)
    for i, r in enumerate([256, 384, 512, 640]):
        p = np.arange(128)[:, None]
        t_ = np.arange(512)[None, :]
        out[i] = ((r + p - t_) <= 256).astype(BF)
    return np.ascontiguousarray(out.transpose(1, 0, 2).reshape(128, 4 * 512))


def hmask16_host():
    m = np.zeros((128, 8 * 16), BF)
    for c in range(8):
        m[0:64, 16 * c + 2 * c] = 1
        m[64:128, 16 * c + 2 * c + 1] = 1
    return m


def selpair_host():
    m = np.zeros((16, 8 * 128), BF)
    for p in range(8):
        m[2 * p, 128 * p:128 * p + 64] = 1
        m[2 * p + 1, 128 * p + 64:128 * p + 128] = 1
    return m


def host_prep(inputs):
    cm = {}
    cm["identf"] = np.eye(128, dtype=np.float32)
    cm["identb"] = np.eye(128, dtype=BF)
    cm["identfn"] = (-np.eye(128)).astype(np.float32)
    cm["hmask16"] = hmask16_host()
    cm["selpair"] = selpair_host()
    cm["onescol"] = np.ones((128, 1), BF)
    cm["onesrow"] = np.ones((1, 128), BF)
    if NPHASE >= 1:
        cm["a0_w"] = np.ascontiguousarray(
            np.asarray(inputs["L0_attn_Wqkv"], np.float32))
        cm["a0_b"] = colchunk(inputs["L0_attn_bqkv"])
        cm["a0_wo"] = np.ascontiguousarray(
            np.asarray(inputs["L0_attn_Wout"], np.float32))
        cm["a0_bo"] = colchunk(inputs["L0_attn_bout"])
        cm["mask_a0"] = l0_masks()
    if NPHASE >= 2:
        cm["f0_wg"] = np.ascontiguousarray(
            np.asarray(inputs["L0_ffn_w_gate"], np.float32))
        cm["f0_wu"] = np.ascontiguousarray(
            np.asarray(inputs["L0_ffn_w_up"], np.float32))
        cm["f0_wd"] = np.ascontiguousarray(
            np.asarray(inputs["L0_ffn_w_down"], np.float32))
        cm["f0_clipx"] = np.asarray(inputs["L0_ffn_clip_x"],
                                    np.float32).reshape(1, 1)
        cm["f0_clipd"] = np.asarray(inputs["L0_ffn_clip_down"],
                                    np.float32).reshape(1, 1)
    if NPHASE >= 3:
        cm["a1_w"] = np.ascontiguousarray(
            np.asarray(inputs["L1_attn_in_proj_w"], np.float32).T)
        cm["a1_b"] = colchunk(inputs["L1_attn_in_proj_b"])
        cm["a1_wo"] = np.ascontiguousarray(
            np.asarray(inputs["L1_attn_out_proj_w"], np.float32).T)
        cm["a1_bo"] = colchunk(inputs["L1_attn_out_proj_b"])
        cm["mask_a1"] = l1_masks()
    if NPHASE >= 4:
        cm["f1_wexp"] = np.ascontiguousarray(
            np.asarray(inputs["L1_ffn_W_exp"], np.float32))
        cm["f1_bexp"] = colchunk(inputs["L1_ffn_b_exp"])
        ub = np.asarray(inputs["L1_ffn_up_bias"], np.float32)
        ubB = np.zeros((128, 32), np.float32)
        for tau in range(32):
            for pi in range(128):
                ubB[pi, tau] = ub[_bf_i_of_B(tau, pi)]
        cm["f1_ubB"] = ubB
        cm["f1_bcon"] = colchunk(inputs["L1_ffn_b_con"])
        cm["f1_dbias"] = colchunk(inputs["L1_ffn_down_bias"])
        wcon = np.asarray(inputs["L1_ffn_W_con"], np.float32)
        perm = np.array([_bf_i_of_B(jB // 128, jB % 128) for jB in range(4096)])
        cm["f1_wconB"] = np.ascontiguousarray(wcon[perm])
        uw = np.asarray(inputs["L1_ffn_up_weights"], np.float32)
        dw = np.asarray(inputs["L1_ffn_down_weights"], np.float32)
        cm["f1_U0"] = _bf_expand(uw, list(range(7)), False)
        cm["f1_U1"] = _bf_expand(uw, list(range(7, 12)), True)
        cm["f1_U2"] = _bf_expand(dw, list(range(7)), False)
        cm["f1_U3"] = _bf_expand(dw, list(range(7, 12)), True)
    return cm



class L0Cfg:
    name = "a0"
    nmask = 6
    kst_w = 768

    @staticmethod
    def window(qb):
        qs = qb * QB
        return max(0, qs - 128), min(T, qs + 640)

    @staticmethod
    def maskidx(qb, ts):
        r = ts - qb * QB
        return {-128: 0, 0: 1, 128: 2, 256: 3, 384: 4, 512: 5}[r]


class L1Cfg:
    name = "a1"
    nmask = 4
    kst_w = 2048

    @staticmethod
    def window(qb):
        qs = qb * QB
        return 0, min(T, qs + 768)

    @staticmethod
    def maskidx(qb, ts):
        r = ts - qb * QB
        return {256: 0, 384: 1, 512: 2, 640: 3}.get(r)




class LNPack:

    def __init__(self, nc, ctx):
        sb = lambda n, s, d: ctx.enter_context(nc.sbuf_tensor(n, s, d))
        self.ln_a = sb("ln_a", [1, 512], F32)
        self.ln_b = sb("ln_b", [1, 512], F32)
        self.mr = sb("ln_mr", [1, 1024], BF16)
        self.mrb = sb("ln_mrb", [128, 1024], BF16)
        self.y2 = [sb(f"ln_y2{i}", [128, 512], BF16) for i in range(2)]
        self.sLN = Sem(nc, ctx, "ln_sLN")
        self.tLS = Sem(nc, ctx, "ln_tLS")
        self.vLS = Sem(nc, ctx, "ln_vLS")
        self.sLR = Sem(nc, ctx, "ln_sLR")
        self.tLB = Sem(nc, ctx, "ln_tLB")
        self.sLB = Sem(nc, ctx, "ln_sLB")
        self.vXW = Sem(nc, ctx, "ln_vXW")

    def emit(self, o, G, PA, PB, yb, qs):
        x_bf = G["x_bf"]
        for c in range(8):
            if c >= 2:
                o.wait("scalar", self.tLS, self.tLS.v - 2)
            o.op("scalar", lambda s, c=c, y2=self.y2[c % 2]: s.activation(
                y2[:], yb[:, 512 * c:512 * (c + 1)], AF.Square), self.sLN)
            o.wait("tensor", self.sLN, self.sLN.v)
            o.op("tensor", lambda t, c=c: t.matmul(
                PA[0:1, 0:512], G["onescol"][:],
                yb[:, 512 * c:512 * (c + 1)],
                start=(c == 0), stop=(c == 7)), self.tLS)
            o.op("tensor", lambda t, c=c, y2=self.y2[c % 2]: t.matmul(
                PA[0:1, 512:1024], G["onescol"][:], y2[:],
                start=(c == 0), stop=(c == 7)), self.tLS)
        o.wait("vector", self.tLS, self.tLS.v)
        o.fop("vector", lambda v: v.tensor_scalar_mul(
            self.mr[0:1, 0:512], PA[0:1, 0:512], 1.0 / D))
        o.fop("vector", lambda v: v.tensor_scalar(
            self.ln_a[:], PA[0:1, 512:1024], 1.0 / D, 1e-5,
            ALU.mult, ALU.add))
        o.fop("vector", lambda v: v.tensor_mul(
            self.ln_b[:], self.mr[0:1, 0:512], self.mr[0:1, 0:512]))
        o.op("vector", lambda v: v.tensor_sub(
            self.ln_a[:], self.ln_a[:], self.ln_b[:]), self.vLS)
        o.wait("scalar", self.vLS, self.vLS.v)
        o.fop("scalar", lambda s: s.activation(
            self.ln_b[:], self.ln_a[:], AF.Ln))
        o.op("scalar", lambda s: s.activation(
            self.mr[0:1, 512:1024], self.ln_b[:], AF.Exp, scale=-0.5),
            self.sLR)
        o.wait("tensor", self.sLR, self.sLR.v)
        o.op("tensor", lambda t: t.matmul(
            PB[:, 0:512], G["onesrow"][:], self.mr[0:1, 0:512],
            start=True, stop=True), self.tLB)
        o.op("tensor", lambda t: t.matmul(
            PB[:, 512:1024], G["onesrow"][:], self.mr[0:1, 512:1024],
            start=True, stop=True), self.tLB)
        o.wait("scalar", self.tLB, self.tLB.v)
        o.op("scalar", lambda s: s.copy(
            self.mrb[:, 0:512], PB[:, 0:512]), self.sLB)
        o.op("scalar", lambda s: s.copy(
            self.mrb[:, 512:1024], PB[:, 512:1024]), self.sLB)
        o.wait("vector", self.sLB, self.sLB.v)
        for c in range(8):
            fn = o.fop if c == 7 else o.op
            fn("vector", lambda v, c=c: v.tensor_sub(
                yb[:, 512 * c:512 * (c + 1)], yb[:, 512 * c:512 * (c + 1)],
                self.mrb[:, 0:512]))
        for c in range(8):
            o.op("vector", lambda v, c=c, qs=qs: v.tensor_mul(
                x_bf[:, T * c + qs:T * c + qs + QB],
                yb[:, 512 * c:512 * (c + 1)], self.mrb[:, 512:1024]),
                self.vXW)


def phase_in(nc, block, G):
    ctx = ExitStack()
    o = Orch(nc, ctx, G['semreg'], G['semctx'])
    xin = ctx.enter_context(nc.sbuf_tensor("in_stage", [128, 2 * D], F32))
    pT = [G["P0"][:, 0:128], G["P0"][:, 512:640]]
    ldc = o.sem("in_ldc")
    ldx = [o.sem("in_ldx0"), o.sem("in_ldx1")]
    mmT = o.sem("in_mmT")
    evT = o.sem("in_evT")

    for nm, tens in G["const_loads"]:
        o.op("gpsimd", lambda g, nm=nm, tens=tens: g.dma_start(
            out=tens[:], in_=G[nm + "_h"]), ldc, 16)
    nconst = ldc.v

    for i in range(16):
        if i >= 2:
            o.wait("sync", mmT, 8 * (i - 1))
        o.op("sync", lambda s, i=i: s.dma_start(
            out=xin[:, (i % 2) * D:(i % 2 + 1) * D],
            in_=G["x_h"][i * 128:(i + 1) * 128, :]), ldx[i % 2], 16)
        o.wait("tensor", ldx[i % 2], 16 * (i // 2 + 1))
        if i == 0:
            o.wait("tensor", ldc, nconst)
        for d in range(NCH):
            j = 8 * i + d
            if j >= 2:
                o.wait("tensor", evT, j - 1)
            o.op("tensor", lambda t, i=i, d=d, j=j: t.matmul(
                pT[j % 2],
                xin[:, (i % 2) * D + d * 128:(i % 2) * D + (d + 1) * 128],
                G["identf"][:], start=True, stop=True), mmT)
            o.wait("vector", mmT, j + 1)
            o.op("vector", lambda v, i=i, d=d, j=j: v.tensor_copy(
                G["x_bf"][:, d * T + i * 128:d * T + (i + 1) * 128],
                pT[j % 2]), evT)
    o.flush(block)
    ctx.close()
    return evT, evT.v


def phase_out(nc, block, G, bar, barv):
    ctx = ExitStack()
    o = Orch(nc, ctx, G['semreg'], G['semctx'])
    xout = ctx.enter_context(nc.sbuf_tensor("out_stage", [128, 2 * D], F32))
    pO = [G["P1"][:, 0:128], G["P1"][:, 512:640]]
    mmO = o.sem("out_mmO")
    evO = o.sem("out_evO")
    stD = [o.sem("out_stD0"), o.sem("out_stD1")]
    o.wait("tensor", bar, barv)
    o.wait("vector", bar, barv)
    o.wait("sync", bar, barv)
    for i in range(16):
        for d in range(NCH):
            j = 8 * i + d
            if j >= 2:
                o.wait("tensor", evO, j - 1)
            if d == 0 and i >= 2:
                o.wait("tensor", stD[i % 2], 16 * (i // 2))
            src = G.get("dump_t", G["x_bf"])
            o.op("tensor", lambda t, i=i, d=d, j=j, src=src: t.matmul(
                pO[j % 2],
                src[:, d * T + i * 128:d * T + (i + 1) * 128],
                G["identb"][:], start=True, stop=True), mmO)
            o.wait("vector", mmO, j + 1)
            o.op("vector", lambda v, i=i, d=d, j=j: v.tensor_copy(
                xout[:, (i % 2) * D + d * 128:(i % 2) * D + (d + 1) * 128],
                pO[j % 2]), evO)
        o.wait("sync", evO, 8 * (i + 1))
        o.op("sync", lambda s, i=i: s.dma_start(
            out=G["out_h"][i * 128:(i + 1) * 128, :],
            in_=xout[:, (i % 2) * D:(i % 2 + 1) * D]), stD[i % 2], 16)
    o.wait("sync", stD[0], 128)
    o.wait("sync", stD[1], 128)
    o.flush(block)
    ctx.close()


def phase_attention(nc, block, G, cfg, bar, barv):
    name = cfg.name
    ctx = ExitStack()
    o = Orch(nc, ctx, G['semreg'], G['semctx'])
    sb = lambda n, s, d: ctx.enter_context(
        nc.sbuf_tensor(f"{name}S_{n}", s, d))
    ps = lambda n, s: ctx.enter_context(
        nc.psum_tensor(f"{name}P_{n}", s, F32))

    x_bf = G["x_bf"]
    identb = G["identb"]
    hm16 = G["hmask16"]
    selp = G["selpair"]
    masks = sb("maskar", [128, cfg.nmask * 512], BF16)
    mask_h = G[f"mask_{name}_h"]
    w_h = G[f"{name}_w_h"]
    b_h = G[f"{name}_b_h"]
    wo_h = G[f"{name}_wo_h"]
    bo_h = G[f"{name}_bo_h"]

    NKST = 2
    bqkv = sb("bqkv", [128, 24], F32)
    bo_s = sb("bo", [128, 8], F32)
    bq_s = sb("bqs", [128, 8], F32)
    k_pack = sb("kpack", [128, 8 * 2048], BF16)
    vt = sb("vt", [128, 16 * 16 * 65], BF16)
    kst = [sb(f"kst{i}", [128, cfg.kst_w], BF16) for i in range(NKST)]
    den = sb("den", [16, 512], F32)
    den4 = sb("den4", [128, 2048], BF16)
    rden = sb("rden", [16, 512], BF16)
    subctx = ExitStack()
    sb1 = lambda n, s, d: subctx.enter_context(
        nc.sbuf_tensor(f"{name}S_{n}", s, d))
    wkv = sb1("wkv", [128, 8 * 2048], BF16)
    vch = sb1("vch", [128, 8 * 2048], BF16)
    subctx.close()
    wq = sb("wq", [128, 8 * 1024], BF16)
    wo = sb("wo", [128, 8 * 1024], BF16)
    qt = sb("qt", [128, 16 * 512], BF16)
    macc = sb("macc", [128, 16], F32)
    rmn = sb("rmn", [128, 1], F32)
    negm = sb("negm", [16, 512], F32)
    psb = [sb(f"psb{i}", [128, 1024], BF16) for i in range(2)]
    araw = sb("araw", [128, 8 * 512], BF16)
    qp = araw
    rb = [sb(f"rb{i}", [128, 512], BF16) for i in range(2)]
    yb = sb("yb", [128, 8 * 512], BF16)

    P0, P1, P2, P3 = G["P0"], G["P1"], G["P2"], G["P3"]

    dWkv = o.sem("at_dWkv")
    dMK = o.sem("at_dMK")
    dWq = o.sem("at_dWq")
    dWo = o.sem("at_dWo")
    dB = o.sem("at_dB")
    dNM = o.sem("at_dNM")
    vPRE = o.sem("at_vPRE")
    tKV = o.sem("at_tKV")
    sKV = o.sem("at_sKV")
    tVT = o.sem("at_tVT")
    sVT = [o.sem("at_sVTs"), o.sem("at_sVTv")]
    tQ = o.sem("at_tQ")
    sQE = o.sem("at_sQE")
    tMX = o.sem("at_tMX")
    vMX = o.sem("at_vMX")
    sMX = o.sem("at_sMX")
    vKC = o.sem("at_vKC")
    tS = o.sem("at_tS")
    sEX = o.sem("at_sEX")
    vMK = o.sem("at_vMK")
    tPV = o.sem("at_tPV")
    sPV = o.sem("at_sPV")
    vRC = o.sem("at_vRC")
    dDN = o.sem("at_dDN")
    tBC = o.sem("at_tBC")
    sBC = o.sem("at_sBC")
    vNE = o.sem("at_vNE")
    tO = o.sem("at_tO")
    vRS = o.sem("at_vRS")
    LN = G["LN"]
    sLN, tLS, vLS = LN.sLN, LN.tLS, LN.vLS
    sLR, tLB, sLB, vXW = LN.sLR, LN.tLB, LN.sLB, LN.vXW

    kv0 = sKV.v
    tKV0 = tKV.v
    tVT0 = tVT.v
    sVT0 = [sVT[0].v, sVT[1].v]
    sQE0 = sQE.v
    o.wait("gpsimd", bar, barv)
    o.op("gpsimd", lambda g: g.dma_start(
        out=masks[:, 0:cfg.nmask * 512], in_=mask_h), dMK, 16)
    for k in range(8):
        o.op("gpsimd", lambda g, k=k: g.dma_start(
            out=wkv[:, 2048 * k:2048 * (k + 1)],
            in_=w_h[128 * k:128 * (k + 1), 1024:3072]), dWkv, 16)
    o.wait("sync", bar, barv)
    o.op("sync", lambda s: s.dma_start(out=bqkv[:], in_=b_h), dB, 16)
    o.op("sync", lambda s: s.dma_start(out=bo_s[:], in_=bo_h), dB, 16)

    o.wait("vector", bar, barv)
    o.wait("vector", dB, dB.v)
    o.op("vector", lambda v: v.tensor_scalar_mul(
        bq_s[:], bqkv[:, 0:8], float(SCALE)), vPRE)
    o.op("vector", lambda v: v.memset(vt[:, 64:16 * 16 * 65:65], 1.0), vPRE)
    for i in range(NKST):
        o.op("vector", lambda v, i=i: v.memset(kst[i][64:66, :], 0.0), vPRE)
        o.fop("vector", lambda v, i=i: v.memset(kst[i][64:65, :], 1.0), vPRE)
    o.op("vector", lambda v: v.memset(qt[64:66, :], 0.0), vPRE)

    slots_kv = [(P0, 0), (P0, 512), (P1, 0), (P1, 512)]
    o.wait("tensor", bar, barv)
    o.wait("tensor", dWkv, dWkv.v)
    nj = 0
    o.wait("scalar", dB, dB.v)
    for mi in range(16):
        for n in range(4):
            j = nj
            nj += 1
            Pp, off = slots_kv[j % 4]
            if j >= 4:
                o.wait("tensor", sKV, kv0 + j - 3)
            for k in range(8):
                o.op("tensor", lambda t, Pp=Pp, off=off, mi=mi, n=n, k=k:
                     t.matmul(
                         Pp[:, off:off + 512],
                         wkv[:, 2048 * k + 128 * mi:2048 * k + 128 * (mi + 1)],
                         x_bf[:, T * k + 512 * n:T * k + 512 * (n + 1)],
                         start=(k == 0), stop=(k == 7)),
                     tKV if k == 7 else None)
            o.wait("scalar", tKV, tKV0 + j + 1)
            dst = k_pack if mi < 8 else vch
            c = mi % 8
            o.op("scalar", lambda s, Pp=Pp, off=off, dst=dst, c=c, n=n,
                 mi=mi: s.activation(
                     dst[:, 2048 * c + 512 * n:2048 * c + 512 * (n + 1)],
                     Pp[:, off:off + 512], AF.Identity,
                     bias=bqkv[:, 8 + mi:9 + mi]), sKV)

    slots_vt = [(P0, 0), (P0, 512), (P1, 0), (P1, 512)]
    jj = 0
    for h in range(16):
        c = h // 2
        rbase = (h % 2) * 64
        for tt_ in range(16):
            j = jj
            jj += 1
            Pp, off = slots_vt[j % 4]
            o.wait("tensor", sKV, kv0 + (64 if j < 4 else min(64, (9 + c) * 4)))
            if j >= 4:
                jp = j - 4
                o.wait("tensor", sVT[jp % 2], sVT0[jp % 2] + jp // 2 + 1)
            o.op("tensor", lambda t, Pp=Pp, off=off, c=c, rbase=rbase,
                 tt_=tt_: t.matmul(
                     Pp[:, off:off + 64],
                     vch[rbase:rbase + 64,
                         2048 * c + 128 * tt_:2048 * c + 128 * (tt_ + 1)],
                     identb[rbase:rbase + 64, rbase:rbase + 64],
                     start=True, stop=True), tVT)
            eng = "scalar" if (j % 2 == 0) else "vector"
            o.wait(eng, tVT, tVT0 + j + 1)
            col = (h * 16 + tt_) * 65
            if eng == "scalar":
                o.op(eng, lambda e, Pp=Pp, off=off, col=col: e.copy(
                    vt[:, col:col + 64], Pp[:, off:off + 64]), sVT[0])
            else:
                o.op(eng, lambda e, Pp=Pp, off=off, col=col: e.tensor_copy(
                    vt[:, col:col + 64], Pp[:, off:off + 64]), sVT[1])

    o.wait("gpsimd", tVT, tVT.v)
    for k in range(8):
        o.op("gpsimd", lambda g, k=k: g.dma_start(
            out=wq[:, 1024 * k:1024 * (k + 1)],
            in_=w_h[128 * k:128 * (k + 1), 0:1024]), dWq, 16)
    for k in range(8):
        o.op("gpsimd", lambda g, k=k: g.dma_start(
            out=wo[:, 1024 * k:1024 * (k + 1)],
            in_=wo_h[128 * k:128 * (k + 1), :]), dWo, 16)

    ts_after_head = []
    vmx_after = []
    mxstep = [0]
    pv_after_group = []
    ex_after_group = []
    gg = 0
    first_pv = True

    for qb in range(NQB):
        qs = qb * QB
        w0, w1 = cfg.window(qb)
        ntile = (w1 - w0) // 128

        if qb > 0:
            o.wait("tensor", vXW, vXW.v)
            o.wait("tensor", sLB, sLB.v)

        for m in range(8):
            mg = qb * 8 + m
            Pp, off = (P0, 0) if m % 2 == 0 else (P0, 512)
            if mg == 0:
                o.wait("tensor", sKV, kv0 + 64)
                o.wait("tensor", dWq, dWq.v)
                o.wait("tensor", sVT[0], sVT[0].v)
                o.wait("tensor", sVT[1], sVT[1].v)
            if mg >= 2:
                o.wait("tensor", sQE, sQE0 + 3 * (mg - 1))
            for k in range(8):
                o.op("tensor", lambda t, Pp=Pp, off=off, m=m, k=k, qs=qs:
                     t.matmul(
                         Pp[:, off:off + 512],
                         wq[:, 1024 * k + 128 * m:1024 * k + 128 * (m + 1)],
                         x_bf[:, T * k + qs:T * k + qs + QB],
                         start=(k == 0), stop=(k == 7)), tQ)
            if m == 0:
                o.wait("scalar", vPRE, 1)
                o.wait("scalar", tS, tS.v)
                o.wait("scalar", tMX, tMX.v)
                o.wait("scalar", tO, tO.v)
                o.wait("scalar", vNE, vNE.v)
            o.wait("scalar", tQ, tQ.v)
            for half in range(2):
                hh = 2 * m + half
                o.op("scalar", lambda s, Pp=Pp, off=off, half=half, hh=hh,
                     m=m: s.activation(
                         qt[0:64, 512 * hh:512 * (hh + 1)],
                         Pp[64 * half:64 * (half + 1), off:off + 512],
                         AF.Identity, scale=float(SCALE),
                         bias=bq_s[64 * half:64 * (half + 1), m:m + 1]), sQE)
            o.op("scalar", lambda s, Pp=Pp, off=off, m=m: s.activation(
                qp[:, 512 * m:512 * (m + 1)], Pp[:, off:off + 512],
                AF.Identity, scale=float(SCALE),
                bias=bq_s[:, m:m + 1]), sQE)

        nchunk = (w1 - w0 + 511) // 512
        mslots = [(P2, 0), (P2, 512), (P3, 0), (P3, 512)]
        for tqs in range(4):
            for hp in range(8):
                o.wait("tensor", sQE, sQE.v)
                for tkc in range(nchunk):
                    st = mxstep[0]
                    mxstep[0] += 1
                    sl = mslots[(st % 2) * 2:(st % 2) * 2 + 2]
                    if st >= 2:
                        o.wait("tensor", vMX, vmx_after[st - 2])
                    tw = min(512, w1 - w0 - 512 * tkc)
                    for hh in range(2):
                        Pp, off = sl[hh]
                        o.op("tensor", lambda t, Pp=Pp, off=off, hp=hp,
                             hh=hh, tqs=tqs, tkc=tkc, tw=tw, qs=qs, w0=w0:
                             t.matmul(
                                 Pp[:, off:off + tw],
                                 qp[64 * hh:64 * (hh + 1),
                                    512 * hp + 128 * tqs:512 * hp
                                    + 128 * (tqs + 1)],
                                 k_pack[64 * hh:64 * (hh + 1),
                                        2048 * hp + w0 + 512 * tkc:
                                        2048 * hp + w0 + 512 * tkc + tw],
                                 start=True, stop=True,
                                 tile_position=(64 * hh, 0)), tMX)
                    o.wait("vector", tMX, tMX.v)
                    for hh in range(2):
                        Pp, off = sl[hh]
                        if tkc == 0:
                            o.fop("vector", lambda v, Pp=Pp, off=off, hp=hp,
                                  hh=hh, tw=tw: v.reduce_max(
                                      macc[:, 2 * hp + hh:2 * hp + hh + 1],
                                      Pp[:, off:off + tw],
                                      axis=mybir.AxisListType.X), vMX)
                        else:
                            o.fop("vector", lambda v, Pp=Pp, off=off, tw=tw:
                                  v.reduce_max(
                                      rmn[:], Pp[:, off:off + tw],
                                      axis=mybir.AxisListType.X), vMX)
                            o.fop("vector", lambda v, hp=hp, hh=hh:
                                  v.tensor_max(
                                      macc[:, 2 * hp + hh:2 * hp + hh + 1],
                                      macc[:, 2 * hp + hh:2 * hp + hh + 1],
                                      rmn[:]), vMX)
                    vmx_after.append(vMX.v)
            o.wait("tensor", vMX, vMX.v)
            o.wait("tensor", sMX, sMX.v)
            o.op("tensor", lambda t, tqs=tqs: t.matmul(
                P1[0:16, 0:128], macc[:], G["identfn"][:, 0:128],
                start=True, stop=True), tMX)
            o.wait("scalar", tMX, tMX.v)
            o.op("scalar", lambda s, tqs=tqs: s.copy(
                negm[:, 128 * tqs:128 * (tqs + 1)], P1[0:16, 0:128]), sMX)
        o.wait("gpsimd", sMX, sMX.v)
        if qb > 0:
            o.wait("gpsimd", tS, tS.v)
        o.op("gpsimd", lambda g: g.dma_start(
            out=qt[64:65, :], in_=negm[:]), dNM, 16)

        for h in range(16):
            hg = qb * 16 + h
            if hg >= NKST:
                o.wait("vector", tS, ts_after_head[hg - NKST])
            o.op("vector", lambda v, h=h, w0=w0, w1=w1, hg=hg: v.tensor_copy(
                kst[hg % NKST][0:64, 0:w1 - w0],
                k_pack[(h % 2) * 64:(h % 2) * 64 + 64,
                       2048 * (h // 2) + w0:2048 * (h // 2) + w1]), vKC)
            ngrp = (ntile + 1) // 2
            for g in range(ngrp):
                t0 = 2 * g
                tiles = [t0] if t0 + 1 >= ntile else [t0, t0 + 1]
                Pg = P2 if gg % 2 == 0 else P3
                pb = psb[gg % 2]
                o.wait("tensor", vKC, vKC.v)
                o.wait("tensor", dNM, dNM.v)
                o.wait("tensor", vMX, vMX.v)
                if gg >= 2:
                    o.wait("tensor", sEX, ex_after_group[gg - 2])

                for ti in tiles:
                    o.op("tensor", lambda t, Pg=Pg, ti=ti, t0=t0, h=h,
                         hg=hg: t.matmul(
                             Pg[:, 512 * (ti - t0):512 * (ti - t0 + 1)],
                             kst[hg % NKST][0:66, 128 * ti:128 * (ti + 1)],
                             qt[0:66, 512 * h:512 * (h + 1)],
                             start=True, stop=True), tS)
                o.wait("scalar", tS, tS.v)
                if gg >= 2:
                    o.wait("scalar", tPV, pv_after_group[gg - 2])
                nel = 512 * len(tiles)
                o.op("scalar", lambda s, Pg=Pg, pb=pb, nel=nel: s.activation(
                    pb[:, 0:nel], Pg[:, 0:nel], AF.Exp), sEX)
                ex_after_group.append(sEX.v)
                need_mask = []
                for ti in tiles:
                    midx = cfg.maskidx(qb, w0 + 128 * ti)
                    if midx is not None:
                        need_mask.append((ti - t0, midx))
                if need_mask:
                    o.wait("vector", sEX, sEX.v)
                    o.wait("vector", dMK, dMK.v)
                    for rel, midx in need_mask:
                        o.op("vector", lambda v, pb=pb, rel=rel, midx=midx:
                             v.tensor_mul(
                                 pb[:, 512 * rel:512 * (rel + 1)],
                                 pb[:, 512 * rel:512 * (rel + 1)],
                                 masks[:, 512 * midx:512 * (midx + 1)]), vMK)
                o.wait("tensor", sEX, sEX.v)
                if need_mask:
                    o.wait("tensor", vMK, vMK.v)
                if first_pv:
                    o.wait("tensor", vPRE, vPRE.v)
                    first_pv = False
                if g == 0 and hg > 0:
                    o.wait("tensor", sPV, sPV.v)
                for idx, ti in enumerate(tiles):
                    gt = (w0 + 128 * ti) // 128
                    o.op("tensor", lambda t, pb=pb, idx=idx, gt=gt, h=h,
                         first=(g == 0 and idx == 0),
                         last=(ti == ntile - 1): t.matmul(
                             P1[0:65, 512:1024],
                             vt[:, (h * 16 + gt) * 65:(h * 16 + gt) * 65 + 65],
                             pb[:, 512 * idx:512 * (idx + 1)],
                             start=first, stop=last), tPV)
                pv_after_group.append(tPV.v)
                gg += 1
            ts_after_head.append(tS.v)
            o.wait("scalar", tPV, tPV.v)
            o.wait("scalar", tMX, tMX.v)
            if hg == 0:
                o.wait("scalar", vPRE, vPRE.v)
            else:
                o.wait("scalar", tO, tO.v)
            o.op("scalar", lambda s, h=h: s.copy(
                araw[(h % 2) * 64:(h % 2) * 64 + 64,
                     512 * (h // 2):512 * (h // 2) + 512],
                P1[0:64, 512:1024]), sPV)
            o.op("scalar", lambda s, h=h: s.copy(
                den4[32 * (h // 4):32 * (h // 4) + 1,
                     512 * (h % 4):512 * (h % 4) + 512],
                P1[64:65, 512:1024]), sPV)

        o.wait("sync", sPV, sPV.v)
        o.wait("sync", vRC, vRC.v)
        o.op("sync", lambda sy: sy.dma_start(
            out=rden[:],
            in_=den4[0:128:32, :].rearrange("p (c e) -> p c e", e=512)),
            dDN, 16)
        o.wait("vector", dDN, dDN.v)
        o.fop("vector", lambda v: v.reciprocal(den[:], rden[:]))
        o.op("vector", lambda v: v.tensor_copy(rden[:], den[:]), vRC)
        for p in range(8):
            Pp, off = (P2, 0) if p % 2 == 0 else (P2, 512)
            o.wait("tensor", vRC, vRC.v)
            if p < 2:
                o.wait("tensor", sEX, sEX.v)
            else:
                o.wait("tensor", sBC, sBC.v - 1)
                o.wait("tensor", vNE, vNE.v - 1)
            o.op("tensor", lambda t, Pp=Pp, off=off, p=p: t.matmul(
                Pp[:, off:off + 512], selp[:, 128 * p:128 * (p + 1)],
                rden[:], start=True, stop=True), tBC)
            o.wait("scalar", tBC, tBC.v)
            o.op("scalar", lambda s, Pp=Pp, off=off, p=p: s.copy(
                rb[p % 2][:], Pp[:, off:off + 512]), sBC)
            o.wait("vector", sBC, sBC.v)
            o.op("vector", lambda v, p=p: v.tensor_mul(
                araw[:, 512 * p:512 * (p + 1)],
                araw[:, 512 * p:512 * (p + 1)], rb[p % 2][:]), vNE)

        o.wait("tensor", dWo, dWo.v)
        for m in range(8):
            Pp, off = (P3, 0) if m % 2 == 0 else (P3, 512)
            if m < 2:
                o.wait("tensor", sEX, sEX.v)
            else:
                o.wait("tensor", vRS, vRS.v - 1)
            o.wait("tensor", vNE, vNE.v)
            for k in range(8):
                o.op("tensor", lambda t, Pp=Pp, off=off, m=m, k=k: t.matmul(
                    Pp[:, off:off + 512],
                    wo[:, 1024 * k + 128 * m:1024 * k + 128 * (m + 1)],
                    araw[:, 512 * k:512 * (k + 1)],
                    start=(k == 0), stop=(k == 7)), tO)
            o.wait("vector", tO, tO.v)
            if m == 0:
                o.wait("vector", sLN, sLN.v)
            o.op("vector", lambda v, Pp=Pp, off=off, m=m, qs=qs:
                 v.scalar_tensor_tensor(
                     yb[:, 512 * m:512 * (m + 1)], Pp[:, off:off + 512],
                     bo_s[:, m:m + 1], x_bf[:, T * m + qs:T * m + qs + QB],
                     ALU.add, ALU.add), vRS)

        o.wait("scalar", vRS, vRS.v)
        o.wait("scalar", vXW, vXW.v)
        o.wait("tensor", sBC, sBC.v)
        o.wait("tensor", vRS, vRS.v)
        o.wait("vector", vXW, vXW.v)
        LN.emit(o, G, P2, P3, yb, qs)

    o.flush(block)
    ctx.close()
    return vXW, vXW.v



def emit_quant(o, eng_v, eng_s, out_ap, in_ap, rs_ap, Q, sem,
               clip=False):
    qv, qi, qf, qg, vq, sc = Q.qv, Q.qi, Q.qf, Q.qg, Q.vq, Q.sc
    w = out_ap.shape[-1]
    if Q.last is not None:
        o.wait(eng_v, Q.last[0], Q.last[1])
    o.wait(eng_v, sc, sc.v)
    o.op(eng_v, lambda v: v.tensor_scalar(
        qv[:, 0:w], in_ap, rs_ap, 0.5, ALU.mult, ALU.add), vq)
    o.wait(eng_s, vq, vq.v)
    o.fop(eng_s, lambda s: s.copy(qi[:, 0:w], qv[:, 0:w]))
    o.op(eng_s, lambda s: s.copy(qf[:, 0:w], qi[:, 0:w]), sc)
    o.wait(eng_v, sc, sc.v)
    o.fop(eng_v, lambda v: v.tensor_tensor(
        qg[:, 0:w], qf[:, 0:w], qv[:, 0:w], ALU.is_gt))
    if clip:
        o.fop(eng_v, lambda v: v.scalar_tensor_tensor(
            qv[:, 0:w], qg[:, 0:w], -1.0, qf[:, 0:w], ALU.mult, ALU.add))
        fin = sem if sem is not None else vq
        o.op(eng_v, lambda v: v.tensor_scalar(
            out_ap, qv[:, 0:w], -128.0, 127.0, ALU.max, ALU.min), fin)
    else:
        fin = sem if sem is not None else vq
        o.op(eng_v, lambda v: v.scalar_tensor_tensor(
            out_ap, qg[:, 0:w], -1.0, qf[:, 0:w], ALU.mult, ALU.add), fin)
    Q.last = (fin, fin.v)


class QPack:
    def __init__(self, qv, qi, qf, qg, vq, sc):
        self.qv, self.qi, self.qf, self.qg = qv, qi, qf, qg
        self.vq, self.sc = vq, sc
        self.last = None


def phase_ffn0(nc, block, G, bar, barv):
    name = "f0"
    ctx = ExitStack()
    o = Orch(nc, ctx, G['semreg'], G['semctx'])
    sb = lambda n, s, d: ctx.enter_context(
        nc.sbuf_tensor(f"{name}S_{n}", s, d))
    ps = lambda n, s: ctx.enter_context(
        nc.psum_tensor(f"{name}P_{n}", s, F32))
    x_bf = G["x_bf"]
    LN = G["LN"]
    NI = 22

    g_sb = sb("g", [128, NI * 2048], BF16)
    qv = sb("qv", [128, 512], F32)
    qi = sb("qi", [128, 512], I32)
    qf = sb("qf", [128, 512], BF16)
    qg = sb("qg", [128, 512], BF16)
    sgx = sb("sgx", [128, NI], F32)
    sux = sb("sux", [128, NI], F32)
    rsc = sb("rsc", [128, 4], F32)
    rmx = sb("rmx", [128, 1], F32)
    xm8 = sb("xm8", [128, 8], F32)
    gmA = sb("gmA", [128, 1], F32)
    gmT = sb("gmT", [128, 1], F32)
    sc1 = sb("sc1", [1, 16], F32)
    scb = sb("scb", [128, 4], F32)
    hh = [sb(f"hh{i}", [128, 512], BF16) for i in range(2)]
    sub = ExitStack()
    sb1 = lambda n, s, d: sub.enter_context(
        nc.sbuf_tensor(f"{name}S_{n}", s, d))
    xh = sb1("xh", [128, 8 * 2048], BF16)
    wnat = [sb1(f"wnat{i}", [128, 1024], F32) for i in range(2)]
    wqn = [sb1(f"wqn{i}", [128, 1024], BF16) for i in range(2)]
    wT = [sb1(f"wT{i}", [128, 8 * 128], BF16) for i in range(4)]
    sil = [sb1(f"sil{i}", [128, 512], F32) for i in range(2)]
    sub.close()
    wdT = sb("wdT", [128, NI * 1024], BF16)
    wdn = sb("wdn", [128, 1408], F32)
    wdq = sb("wdq", [128, 1024], BF16)
    yb = sb("yb", [128, 8 * 512], BF16)
    sdh = sb("sdh", [128, 8], F32)

    P0, P1, P2, P3 = G["P0"], G["P1"], G["P2"], G["P3"]

    wg_h = G["f0_wg_h"]
    wu_h = G["f0_wu_h"]
    wd_h = G["f0_wd_h"]
    clipx_h = G["f0_clipx_h"]
    clipd_h = G["f0_clipd_h"]
    cc0_in, cc0_out = G["cc0_in"], G["cc0_out"]
    cc1_in, cc1_out = G["cc1_in"], G["cc1_out"]

    dL = o.sem("at_dL")
    dCC = o.sem("at_dCC")
    cc = o.sem("at_cc")
    dBC = o.sem("at_dBC")
    vST = o.sem("at_vST")
    sST = o.sem("at_sST")
    tST = o.sem("at_tST")
    vXH = o.sem("at_vXH")
    vWQ = o.sem("at_vWQ")
    tWT = o.sem("at_tWT")
    sWT = o.sem("at_sWT")
    tGU = o.sem("at_tGU")
    sSI = o.sem("at_sSI")
    vG = o.sem("at_vG")
    vGM = o.sem("at_vGM")
    tDN = o.sem("at_tDN")
    vHQ = o.sem("at_vHQ")
    vRS = o.sem("at_vRS")
    vq_s = o.sem("at_vq")
    sc_s = o.sem("at_sc")
    Q = QPack(qv, qi, qf, qg, vq_s, sc_s)
    vXW = LN.vXW

    o.wait("vector", bar, barv)
    for c in range(8):
        (o.fop if c == 7 else o.op)("vector", lambda v, c=c: v.reduce_max(
            xm8[:, c:c + 1], x_bf[:, T * c:T * (c + 1)],
            axis=mybir.AxisListType.X, apply_absolute_value=True))
    o.op("vector", lambda v: v.reduce_max(
        rmx[:], xm8[:], axis=mybir.AxisListType.X), vST)
    o.wait("tensor", bar, barv)
    o.wait("tensor", vST, vST.v)
    o.op("tensor", lambda t: t.matmul(
        P0[0:1, 0:128], rmx[:], G["identf"][:], start=True, stop=True), tST)
    o.wait("vector", tST, tST.v)
    o.op("vector", lambda v: v.reduce_max(
        sc1[0:1, 0:1], P0[0:1, 0:128], axis=mybir.AxisListType.X), vST)
    o.wait("sync", bar, barv)
    o.wait("sync", vST, vST.v)
    o.op("sync", lambda sy: sy.dma_start(out=cc0_in, in_=sc1[0:1, 0:1]),
         dCC, 16)
    o.wait("gpsimd", bar, barv)
    o.wait("gpsimd", dCC, dCC.v)
    o.op("gpsimd", lambda g: g.collective_compute(
        "AllGather", ALU.bypass,
        replica_groups=[[0, 1, 2, 3, 4, 5, 6, 7]],
        ins=[cc0_in], outs=[cc0_out]), cc)
    o.wait("sync", cc, cc.v)
    o.op("sync", lambda sy: sy.dma_start(out=sc1[0:1, 8:16], in_=cc0_out),
         dCC, 16)
    o.op("sync", lambda sy: sy.dma_start(
        out=sc1[0:1, 1:2], in_=clipx_h), dCC, 16)
    o.op("sync", lambda sy: sy.dma_start(
        out=sc1[0:1, 2:3], in_=clipd_h), dCC, 16)
    o.wait("vector", dCC, dCC.v)
    o.fop("vector", lambda v: v.reduce_max(
        sc1[0:1, 3:4], sc1[0:1, 8:16], axis=mybir.AxisListType.X), vST)
    o.wait("scalar", bar, barv)
    o.wait("scalar", dCC, dCC.v)
    o.op("scalar", lambda s: s.activation(
        sc1[0:1, 4:5], sc1[0:1, 1:2], AF.Sigmoid), sST)
    o.wait("vector", sST, sST.v)
    o.fop("vector", lambda v: v.tensor_mul(
        sc1[0:1, 5:6], sc1[0:1, 4:5], sc1[0:1, 3:4]))
    o.fop("vector", lambda v: v.tensor_scalar_mul(
        sc1[0:1, 5:6], sc1[0:1, 5:6], 1.0 / 127.0))
    o.fop("vector", lambda v: v.reciprocal(
        sc1[0:1, 6:7], sc1[0:1, 5:6]), vST)
    o.wait("sync", vST, vST.v)
    o.op("sync", lambda sy: sy.dma_start(
        out=scb[:, 0:1], in_=sc1[0:1, 5:6].to_broadcast([1, 128, 1])),
        dBC, 16)
    o.op("sync", lambda sy: sy.dma_start(
        out=scb[:, 1:2], in_=sc1[0:1, 6:7].to_broadcast([1, 128, 1])),
        dBC, 16)
    o.wait("vector", dBC, dBC.v)
    for c in range(8):
        for n in range(4):
            emit_quant(o, "vector", "scalar",
                       xh[:, 2048 * c + 512 * n:2048 * c + 512 * (n + 1)],
                       x_bf[:, T * c + 512 * n:T * c + 512 * (n + 1)],
                       scb[:, 1:2], Q, vXH, clip=True)

    if DUMP == "xh":
        G["dump_t"] = xh
        o.wait("vector", dBC, dBC.v)
        o.op("vector", lambda v: v.tensor_copy(xh[0:1, 16352:16368],
                                               sc1[0:1, 0:16]))
        o.op("vector", lambda v: v.tensor_copy(xh[:, 16368:16372],
                                               scb[:, 0:4]), vXH)
    if DUMP == "g":
        G["dump_t"] = g_sb
    if F0STEP < 2:
        o.flush(block)
        ctx.close()
        return vXH, vXH.v
    o.wait("sync", bar, barv)
    gslots = [(P0, 0), (P0, 512), (P1, 0), (P1, 512)]
    uslots = [(P2, 0), (P2, 512), (P3, 0), (P3, 512)]
    silu_after = {}
    g_after = {}
    wq_after = []
    g_after_gidx = []
    for i in range(NI):
        bw = i % 2
        if i >= 1:
            o.wait("sync", vWQ, wq_after[i - 1])
        o.op("sync", lambda sy, i=i: sy.dma_start(
            out=wnat[0][:], in_=wg_h[128 * i:128 * (i + 1), :]), dL, 16)
        o.op("sync", lambda sy, i=i: sy.dma_start(
            out=wnat[1][:], in_=wu_h[128 * i:128 * (i + 1), :]), dL, 16)
        o.wait("vector", dL, dL.v)
        for wi, sall in enumerate((sgx, sux)):
            wn = wnat[wi]
            wq_ = wqn[wi]
            o.fop("vector", lambda v, wn=wn: v.reduce_max(
                rmx[:], wn[:], axis=mybir.AxisListType.X,
                apply_absolute_value=True))
            o.fop("vector", lambda v, sall=sall, i=i: v.tensor_scalar_mul(
                sall[:, i:i + 1], rmx[:], 1.0 / 127.0))
            rcol = rsc[:, wi:wi + 1]
            o.fop("vector", lambda v, rcol=rcol, sall=sall, i=i:
                 v.reciprocal(rcol, sall[:, i:i + 1]))
            if i >= 1:
                o.wait("vector", tWT, 16 * i)
            for n2 in range(2):
                emit_quant(o, "vector", "scalar",
                           wq_[:, 512 * n2:512 * (n2 + 1)],
                           wn[:, 512 * n2:512 * (n2 + 1)],
                           rcol, Q, vWQ if n2 == 1 else None)
            o.op("vector", lambda v, sall=sall, i=i: v.tensor_scalar(
                sall[:, i:i + 1], sall[:, i:i + 1], scb[:, 0:1], None,
                ALU.mult), vWQ)
        wq_after.append(vWQ.v)
        o.wait("tensor", vWQ, vWQ.v)
        if i >= 1:
            o.wait("tensor", sSI, silu_after[(i - 1, 3)])
            o.wait("tensor", vG, g_after[(i - 1, 3)])
        if i == 0:
            o.wait("tensor", vST, 2)
        for wi in range(2):
            wq_ = wqn[wi]
            wt_ = wT[2 * wi + bw]
            for hc in range(8):
                pslot = P0 if wi == 0 else P1
                poff = 0 if hc % 2 == 0 else 512
                jj = i * 16 + wi * 8 + hc
                if jj >= 2:
                    o.wait("tensor", sWT, jj - 1)
                o.op("tensor", lambda t, wq_=wq_, hc=hc, pslot=pslot,
                     poff=poff: t.matmul(
                         pslot[:, poff:poff + 128],
                         wq_[:, 128 * hc:128 * (hc + 1)], G["identb"][:],
                         start=True, stop=True), tWT)
                o.wait("scalar", tWT, tWT.v)
                o.op("scalar", lambda s, wt_=wt_, hc=hc, pslot=pslot,
                     poff=poff: s.copy(
                         wt_[:, 128 * hc:128 * (hc + 1)],
                         pslot[:, poff:poff + 128]), sWT)
        o.wait("tensor", sWT, sWT.v)
        if i == 0:
            o.wait("tensor", vXH, 32)
        for n in range(4):
            gP, go = gslots[n]
            uP, uo = uslots[n]
            if i >= 1:
                o.wait("tensor", sSI, silu_after[(i - 1, n)])
                o.wait("tensor", vG, g_after[(i - 1, n)])
            for hc in range(8):
                o.op("tensor", lambda t, gP=gP, go=go, bw=bw, hc=hc, n=n:
                     t.matmul(
                         gP[:, go:go + 512],
                         wT[0 + bw][:, 128 * hc:128 * (hc + 1)],
                         xh[:, 2048 * hc + 512 * n:2048 * hc + 512 * (n + 1)],
                         start=(hc == 0), stop=(hc == 7)), None)
                o.op("tensor", lambda t, uP=uP, uo=uo, bw=bw, hc=hc, n=n:
                     t.matmul(
                         uP[:, uo:uo + 512],
                         wT[2 + bw][:, 128 * hc:128 * (hc + 1)],
                         xh[:, 2048 * hc + 512 * n:2048 * hc + 512 * (n + 1)],
                         start=(hc == 0), stop=(hc == 7)),
                     tGU if hc == 7 else None)
            o.wait("scalar", tGU, tGU.v)
            gidx = 4 * i + n
            if gidx >= 2:
                o.wait("scalar", vG, g_after_gidx[gidx - 2])
            o.op("scalar", lambda s, gP=gP, go=go, i=i, n=n: s.activation(
                sil[n % 2][:], gP[:, go:go + 512], AF.Silu,
                scale=sgx[:, i:i + 1]), sSI)
            silu_after[(i, n)] = sSI.v
            o.wait("vector", sSI, sSI.v)
            o.op("vector", lambda v, uP=uP, uo=uo, i=i, n=n:
                 v.scalar_tensor_tensor(
                     g_sb[:, 2048 * i + 512 * n:2048 * i + 512 * (n + 1)],
                     uP[:, uo:uo + 512], sux[:, i:i + 1], sil[n % 2][:],
                     ALU.mult, ALU.mult), vG)
            g_after[(i, n)] = vG.v
            g_after_gidx.append(vG.v)
        o.wait("vector", vG, vG.v)
        o.fop("vector", lambda v, i=i: v.reduce_max(
            gmT[:], g_sb[:, 2048 * i:2048 * (i + 1)],
            axis=mybir.AxisListType.X, apply_absolute_value=True))
        if i == 0:
            o.fop("vector", lambda v: v.tensor_copy(gmA[:], gmT[:]))
        elif i < NI - 1:
            o.fop("vector", lambda v: v.tensor_max(gmA[:], gmA[:], gmT[:]))
        else:
            o.fop("vector", lambda v: v.tensor_max(
                gmA[:], gmA[:], gmT[:]), vGM)

    if DUMP == "wt":
        G["dump_t"] = g_sb
        o.wait("vector", vG, vG.v)
        o.op("vector", lambda v: v.tensor_copy(g_sb[:, 0:1024], wT[0][:]))
        o.op("vector", lambda v: v.tensor_copy(g_sb[:, 1024:2048],
                                               wqn[0][:]))
        o.op("vector", lambda v: v.tensor_copy(g_sb[:, 2048:2070],
                                               sgx[:, 0:22]))
        o.op("vector", lambda v: v.tensor_copy(g_sb[:, 2070:2092],
                                               sux[:, 0:22]), vG)
    if F0STEP < 3:
        o.flush(block)
        ctx.close()
        return vG, vG.v
    o.wait("tensor", sSI, sSI.v)
    o.wait("tensor", vG, vG.v)
    o.wait("tensor", vGM, 1)
    o.op("tensor", lambda t: t.matmul(
        P0[0:1, 0:128], gmA[:], G["identf"][:], start=True, stop=True), tST)
    o.wait("vector", tST, tST.v)
    o.op("vector", lambda v: v.reduce_max(
        sc1[0:1, 9:10], P0[0:1, 0:128], axis=mybir.AxisListType.X), vST)
    o.wait("sync", vST, vST.v)
    o.op("sync", lambda sy: sy.dma_start(out=cc1_in, in_=sc1[0:1, 9:10]),
         dCC, 16)
    o.wait("gpsimd", dCC, dCC.v)
    o.op("gpsimd", lambda g: g.collective_compute(
        "AllGather", ALU.bypass,
        replica_groups=[[0, 1, 2, 3, 4, 5, 6, 7]],
        ins=[cc1_in], outs=[cc1_out]), cc)
    o.wait("sync", cc, cc.v)
    o.op("sync", lambda sy: sy.dma_start(out=sc1[0:1, 8:16], in_=cc1_out),
         dCC, 16)
    o.wait("vector", dCC, dCC.v)
    o.fop("vector", lambda v: v.reduce_max(
        sc1[0:1, 10:11], sc1[0:1, 8:16], axis=mybir.AxisListType.X), vST)
    o.wait("scalar", vST, vST.v)
    o.op("scalar", lambda s: s.activation(
        sc1[0:1, 11:12], sc1[0:1, 2:3], AF.Sigmoid), sST)
    o.wait("vector", sST, sST.v)
    o.fop("vector", lambda v: v.tensor_mul(
        sc1[0:1, 12:13], sc1[0:1, 11:12], sc1[0:1, 10:11]))
    o.fop("vector", lambda v: v.tensor_scalar_mul(
        sc1[0:1, 12:13], sc1[0:1, 12:13], 1.0 / 127.0))
    o.fop("vector", lambda v: v.reciprocal(
        sc1[0:1, 13:14], sc1[0:1, 12:13]), vST)
    o.wait("sync", vST, vST.v)
    o.op("sync", lambda sy: sy.dma_start(
        out=scb[:, 2:3], in_=sc1[0:1, 12:13].to_broadcast([1, 128, 1])),
        dBC, 16)
    o.op("sync", lambda sy: sy.dma_start(
        out=scb[:, 3:4], in_=sc1[0:1, 13:14].to_broadcast([1, 128, 1])),
        dBC, 16)

    o.wait("sync", tGU, tGU.v)
    o.wait("sync", vG, vG.v)
    segw = [1024, 1024, 768]
    for c in range(8):
        o.wait("sync", vST, vST.v)
        o.wait("sync", vWQ, vWQ.v)
        o.op("sync", lambda sy, c=c: sy.dma_start(
            out=wdn[:], in_=wd_h[128 * c:128 * (c + 1), 0:1408]), dL, 16)
        o.wait("vector", dL, dL.v)
        o.fop("vector", lambda v: v.reduce_max(
            rmx[:], wdn[:], axis=mybir.AxisListType.X,
            apply_absolute_value=True), vST)
        o.wait("sync", vST, vST.v)
        o.op("sync", lambda sy, c=c: sy.dma_start(
            out=wdn[:], in_=wd_h[128 * c:128 * (c + 1), 1408:2816]), dL, 16)
        o.wait("vector", dL, dL.v)
        o.fop("vector", lambda v: v.reduce_max(
            gmT[:], wdn[:], axis=mybir.AxisListType.X,
            apply_absolute_value=True))
        o.fop("vector", lambda v: v.tensor_max(rmx[:], rmx[:], gmT[:]))
        o.fop("vector", lambda v, c=c: v.tensor_scalar_mul(
            sdh[:, c:c + 1], rmx[:], 1.0 / 127.0))
        o.fop("vector", lambda v, c=c: v.reciprocal(
            rsc[:, 0:1], sdh[:, c:c + 1]), vST)
        for seg in range(3):
            sw = segw[seg]
            s0 = 1024 * seg
            o.wait("sync", vWQ, vWQ.v)
            o.wait("sync", vST, vST.v)
            o.op("sync", lambda sy, c=c, s0=s0, sw=sw: sy.dma_start(
                out=wdn[:, 0:sw], in_=wd_h[128 * c:128 * (c + 1),
                                           s0:s0 + sw]), dL, 16)
            o.wait("vector", dL, dL.v)
            o.wait("vector", tWT, tWT.v)
            for n3 in range(2):
                w0c = 512 * n3
                w1c = min(sw, w0c + 512)
                if w0c >= sw:
                    continue
                emit_quant(o, "vector", "scalar",
                           wdq[:, w0c:w1c], wdn[:, w0c:w1c], rsc[:, 0:1],
                           Q, vWQ)
            o.wait("tensor", vWQ, vWQ.v)
            if c == 0 and seg == 0:
                o.wait("tensor", vST, vST.v)
            nic = sw // 128
            for ic2 in range(nic):
                ic = seg * 8 + ic2
                jj = c * NI + ic
                slot = jj % 8
                Pp = (P0, P1, P2, P3)[slot // 2]
                poff = (slot % 2) * 512
                if jj >= 8:
                    o.wait("tensor", sWT, sWT.v - 7)
                o.op("tensor", lambda t, ic2=ic2, Pp=Pp, poff=poff: t.matmul(
                    Pp[:, poff:poff + 128],
                    wdq[:, 128 * ic2:128 * (ic2 + 1)], G["identb"][:],
                    start=True, stop=True), tWT)
                o.wait("scalar", tWT, tWT.v)
                o.op("scalar", lambda s, ic=ic, c=c, Pp=Pp, poff=poff: s.copy(
                    wdT[:, 1024 * ic + 128 * c:1024 * ic + 128 * (c + 1)],
                    Pp[:, poff:poff + 128]), sWT)
    o.wait("vector", dBC, dBC.v)
    o.fop("vector", lambda v: v.tensor_scalar(
        sdh[:], sdh[:], scb[:, 2:3], None, ALU.mult), vST)

    if F0STEP < 4:
        o.flush(block)
        ctx.close()
        return vST, vST.v
    dslots = [(P0, 0), (P0, 512), (P1, 0), (P1, 512),
              (P2, 0), (P2, 512), (P3, 0), (P3, 512)]
    for tc in range(4):
        qs = 512 * tc
        for ic in range(NI):
            b = ic % 2
            if ic >= 2:
                o.wait("vector", tDN, tDN.v - 8)
            if ic == 0:
                o.wait("vector", vST, vST.v)
            emit_quant(o, "vector", "scalar",
                       hh[b][:], g_sb[:, 2048 * ic + qs:2048 * ic + qs + 512],
                       scb[:, 3:4], Q, vHQ, clip=True)
            o.wait("tensor", vHQ, vHQ.v)
            if ic == 0:
                o.wait("tensor", sWT, sWT.v)
                o.wait("tensor", vRS, vRS.v)
                o.wait("tensor", LN.vLS, LN.vLS.v)
                o.wait("tensor", LN.sLB, LN.sLB.v)
            for dm in range(8):
                Pp, poff = dslots[dm]
                o.op("tensor", lambda t, ic=ic, dm=dm, b=b, Pp=Pp, poff=poff:
                     t.matmul(
                         Pp[:, poff:poff + 512],
                         wdT[:, 1024 * ic + 128 * dm:
                             1024 * ic + 128 * (dm + 1)],
                         hh[b][:], start=(ic == 0), stop=(ic == NI - 1)),
                     tDN)
        o.wait("vector", tDN, tDN.v)
        o.wait("vector", vXW, vXW.v)
        for dm in range(8):
            Pp, poff = dslots[dm]
            o.op("vector", lambda v, dm=dm, Pp=Pp, poff=poff, qs=qs:
                 v.scalar_tensor_tensor(
                     yb[:, 512 * dm:512 * (dm + 1)], Pp[:, poff:poff + 512],
                     sdh[:, dm:dm + 1], x_bf[:, T * dm + qs:T * dm + qs + QB],
                     ALU.mult, ALU.add), vRS)
        o.wait("scalar", vRS, vRS.v)
        o.wait("scalar", vXW, vXW.v)
        o.wait("tensor", vRS, vRS.v)
        o.wait("vector", vXW, vXW.v)
        LN.emit(o, G, P0, P1, yb, qs)

    o.flush(block)
    ctx.close()
    return vXW, vXW.v



def phase_ffn1(nc, block, G, bar, barv):
    name = "f1"
    ctx = ExitStack()
    o = Orch(nc, ctx, G['semreg'], G['semctx'])
    sb = lambda n, s, d: ctx.enter_context(
        nc.sbuf_tensor(f"{name}S_{n}", s, d))
    x_bf = G["x_bf"]
    LN = G["LN"]
    P0, P1, P2, P3 = G["P0"], G["P1"], G["P2"], G["P3"]

    Mbuf = [sb(f"Mb{i}", [128, 32 * 128], BF16) for i in range(2)]
    bexp = sb("bexp", [128, 32], F32)
    ubB = sb("ubB", [128, 32], F32)
    bsum = sb("bsum", [128, 8], F32)
    bc1 = sb("bc1", [128, 8], F32)
    bc2 = sb("bc2", [128, 8], F32)
    b0 = sb("b0", [128, 32 * 512], BF16)
    b1 = sb("b1", [128, 32 * 512], BF16)
    yb = sb("yb", [128, 8 * 512], BF16)
    wcon = [sb(f"wcon{i}", [128, 1024], BF16) for i in range(2)]
    sub = ExitStack()
    sb1 = lambda n, s, d: sub.enter_context(
        nc.sbuf_tensor(f"{name}S_{n}", s, d))
    uacc = sb1("uacc", [128, 4096], F32)
    ubuf = [sb1(f"ubuf{i}", [128, 4096], F32) for i in range(2)]
    mres = [sb1(f"mres{i}", [128, 4096], BF16) for i in range(2)]
    sub.close()
    wexp = sb("wexp", [128, 8 * 4096], BF16)

    dU = [o.sem(f"{name}_dU0"), o.sem(f"{name}_dU1")]
    dMS = [o.sem(f"{name}_dMS0"), o.sem(f"{name}_dMS1")]
    dMB = [o.sem(f"{name}_dMB0"), o.sem(f"{name}_dMB1")]
    vCM = o.sem(f"{name}_vCM")
    dWX = o.sem(f"{name}_dWX")
    dWC = [o.sem(f"{name}_dWC0"), o.sem(f"{name}_dWC1")]
    dB2 = o.sem(f"{name}_dB2")
    vBS = o.sem(f"{name}_vBS")
    tEX = o.sem(f"{name}_tEX")
    sEV = o.sem(f"{name}_sEV")
    tAP = o.sem(f"{name}_tAP")
    sAE = o.sem(f"{name}_sAE")
    dTN = o.sem(f"{name}_dTN")
    tCN = o.sem(f"{name}_tCN")
    vRS = o.sem(f"{name}_vRS")
    vXW = LN.vXW

    NST = [7, 5, 7, 5]
    UH = [G[f"f1_U{g}_h"] for g in range(4)]

    o.wait("sync", bar, barv)
    nld = 0
    for g in range(4):
        for st in range(NST[g]):
            if nld >= 2:
                o.wait("sync", vCM, nld - 1)
            o.op("sync", lambda sy, g=g, st=st, b=nld % 2: sy.dma_start(
                out=ubuf[b][:], in_=UH[g][st]), dU[nld % 2], 16)
            o.wait("vector", dU[nld % 2], dU[nld % 2].v)
            if g == 0 and st == 0:
                o.wait("vector", bar, barv)
            if st == 0:
                if g >= 2:
                    o.wait("vector", dMS[g % 2], 16)
                o.fop("vector", lambda v, b=nld % 2: v.tensor_copy(
                    uacc[:], ubuf[b][:]), vCM)
            elif st < NST[g] - 1:
                o.fop("vector", lambda v, b=nld % 2: v.tensor_mul(
                    uacc[:], uacc[:], ubuf[b][:]), vCM)
            else:
                o.fop("vector", lambda v, g=g, b=nld % 2: v.tensor_mul(
                    mres[g % 2][:], uacc[:], ubuf[b][:]), vCM)
                o.wait("sync", vCM, vCM.v)
                o.op("sync", lambda sy, g=g: sy.dma_start(
                    out=G["f1_Mscr"][g], in_=mres[g % 2][:]), dMS[g % 2], 16)
            nld += 1

    o.wait("gpsimd", bar, barv)
    o.wait("gpsimd", vCM, vCM.v)
    o.wait("gpsimd", dMS[0], 32)
    o.wait("gpsimd", dMS[1], 32)
    for k in range(8):
        o.op("gpsimd", lambda gg, k=k: gg.dma_start(
            out=wexp[:, 4096 * k:4096 * (k + 1)],
            in_=G["f1_wexp_h"][128 * k:128 * (k + 1), :]), dWX, 16)
    o.op("sync", lambda sy: sy.dma_start(
        out=bexp[:], in_=G["f1_bexp_h"]), dB2, 16)
    o.op("sync", lambda sy: sy.dma_start(
        out=ubB[:], in_=G["f1_ubB_h"]), dB2, 16)
    o.op("sync", lambda sy: sy.dma_start(
        out=bc1[:], in_=G["f1_bcon_h"]), dB2, 16)
    o.op("sync", lambda sy: sy.dma_start(
        out=bc2[:], in_=G["f1_dbias_h"]), dB2, 16)
    o.wait("vector", dB2, dB2.v)
    o.fop("vector", lambda v: v.tensor_add(bsum[:], bc1[:], bc2[:]), vBS)

    aslots = [(P0, 0), (P0, 512), (P1, 0), (P1, 512)]
    dslots = [(P0, 0), (P0, 512), (P1, 0), (P1, 512),
              (P2, 0), (P2, 512), (P3, 0), (P3, 512)]
    ap_after = []

    mstep = [0]

    def emit_apply(gidx, src, dst, qs, bias=None, gelu=False):
        mi = mstep[0]
        mstep[0] += 1
        mb = Mbuf[mi % 2]
        if mi >= 2:
            o.wait("gpsimd", tAP, ap_after[mi - 2])
        o.wait("gpsimd", dMS[0], 32)
        o.wait("gpsimd", dMS[1], 32)
        o.op("gpsimd", lambda gg, gidx=gidx, mb=mb: gg.dma_start(
            out=mb[:], in_=G["f1_Mscr"][gidx]), dMB[mi % 2], 16)
        for c in range(32):
            Pp, off = aslots[c % 4]
            if c >= 4:
                o.wait("tensor", sEV, sEV.v - 3)
            else:
                o.wait("tensor", sEV, max(0, sEV.v - 3 + c))
            if c == 0:
                o.wait("tensor", dMB[mi % 2], dMB[mi % 2].v)
            o.op("tensor", lambda t, Pp=Pp, off=off, c=c, mb=mb, src=src:
                 t.matmul(
                     Pp[:, off:off + 512], mb[:, 128 * c:128 * (c + 1)],
                     src[:, 512 * c:512 * (c + 1)],
                     start=True, stop=True), tAP)
            o.wait("scalar", tAP, tAP.v)
            if gelu:
                o.op("scalar", lambda ss, Pp=Pp, off=off, c=c, dst=dst:
                     ss.activation(
                         dst[:, 512 * c:512 * (c + 1)], Pp[:, off:off + 512],
                         AF.Gelu, bias=ubB[:, c:c + 1]), sEV)
            elif bias is not None:
                o.op("scalar", lambda ss, Pp=Pp, off=off, c=c, dst=dst,
                     bias=bias: ss.activation(
                         dst[:, 512 * c:512 * (c + 1)], Pp[:, off:off + 512],
                         AF.Identity, bias=bias[:, c:c + 1]), sEV)
            else:
                o.op("scalar", lambda ss, Pp=Pp, off=off, c=c, dst=dst:
                     ss.copy(dst[:, 512 * c:512 * (c + 1)],
                             Pp[:, off:off + 512]), sEV)
        ap_after.append(tAP.v)

    def emit_turn_AB(src, dst):
        for tau in range(32):
            o.op("sync", lambda sy, tau=tau, src=src, dst=dst: sy.dma_start(
                out=dst[:, 512 * tau:512 * (tau + 1)],
                in_=src[4 * tau:4 * tau + 4, :].rearrange(
                    "p (c e) -> p c e", e=512)), dTN, 16)

    def emit_turn_BA(src, dst):
        scr = G["f1_Tscr"]
        o.op("sync", lambda sy, src=src: sy.dma_start(
            out=scr, in_=src[:]), dTN, 16)
        o.wait("sync", dTN, dTN.v)
        sv = scr.rearrange("(l c) (t e) -> c t l e", c=32, e=512)
        for c in range(32):
            o.op("sync", lambda sy, c=c, dst=dst, sv=sv: sy.dma_start(
                out=dst[:, 512 * c:512 * (c + 1)].rearrange(
                    "p (x e) -> p x e", x=1),
                in_=sv[c]), dTN, 16)

    for tc in range(4):
        qs = 512 * tc
        o.wait("tensor", bar, barv)
        if tc == 0:
            o.wait("tensor", dWX, dWX.v)
            o.wait("tensor", vCM, vCM.v)
        for m in range(32):
            Pp, off = aslots[m % 4]
            if tc > 0 or m >= 4:
                o.wait("tensor", sEV, sEV.v - 3)
            if tc > 0 and m == 0:
                o.wait("tensor", dTN, dTN.v)
                o.wait("tensor", vRS, vRS.v)
                o.wait("tensor", LN.vLS, LN.vLS.v)
                o.wait("tensor", LN.sLB, LN.sLB.v)
            for k in range(8):
                o.op("tensor", lambda t, Pp=Pp, off=off, m=m, k=k, qs=qs:
                     t.matmul(
                         Pp[:, off:off + 512],
                         wexp[:, 4096 * k + 128 * m:4096 * k + 128 * (m + 1)],
                         x_bf[:, T * k + qs:T * k + qs + QB],
                         start=(k == 0), stop=(k == 7)), tEX)
            o.wait("scalar", tEX, tEX.v)
            if tc == 0 and m == 0:
                o.wait("scalar", dB2, dB2.v)
            if tc > 0 and m == 0:
                o.wait("scalar", dTN, dTN.v)
            o.op("scalar", lambda ss, Pp=Pp, off=off, m=m: ss.activation(
                b0[:, 512 * m:512 * (m + 1)], Pp[:, off:off + 512],
                AF.Identity, bias=bexp[:, m:m + 1]), sEV)
        emit_apply(0, b0, b1, qs)
        o.wait("sync", sEV, sEV.v)
        emit_turn_AB(b1, b0)
        o.wait("tensor", dTN, dTN.v)
        emit_apply(1, b0, b1, qs, gelu=True)
        o.wait("sync", sEV, sEV.v)
        emit_turn_BA(b1, b0)
        o.wait("tensor", dTN, dTN.v)
        emit_apply(2, b0, b1, qs)
        o.wait("sync", sEV, sEV.v)
        emit_turn_AB(b1, b0)
        o.wait("tensor", dTN, dTN.v)
        emit_apply(3, b0, b1, qs)
        for k in range(32):
            bw = k % 2
            if k >= 2:
                o.wait("gpsimd", tCN, tCN.v - 8)
            o.op("gpsimd", lambda gy, k=k, bw=bw: gy.dma_start(
                out=wcon[bw][:],
                in_=G["f1_wconB_h"][128 * k:128 * (k + 1), :]), dWC[bw], 16)
            o.wait("tensor", dWC[bw], dWC[bw].v)
            if k == 0:
                o.wait("tensor", sEV, sEV.v)
                o.wait("tensor", vRS, vRS.v)
                o.wait("tensor", LN.vLS, LN.vLS.v)
                o.wait("tensor", LN.sLB, LN.sLB.v)
            for dm in range(8):
                Pp, off = dslots[dm]
                o.op("tensor", lambda t, Pp=Pp, off=off, k=k, dm=dm, bw=bw:
                     t.matmul(
                         Pp[:, off:off + 512],
                         wcon[bw][:, 128 * dm:128 * (dm + 1)],
                         b1[:, 512 * k:512 * (k + 1)],
                         start=(k == 0), stop=(k == 31)), tCN)
        o.wait("vector", tCN, tCN.v)
        o.wait("vector", vBS, 1)
        o.wait("vector", vXW, vXW.v)
        for dm in range(8):
            Pp, off = dslots[dm]
            o.op("vector", lambda v, Pp=Pp, off=off, dm=dm, qs=qs:
                 v.scalar_tensor_tensor(
                     yb[:, 512 * dm:512 * (dm + 1)], Pp[:, off:off + 512],
                     bsum[:, dm:dm + 1], x_bf[:, T * dm + qs:T * dm + qs + QB],
                     ALU.add, ALU.add), vRS)
        o.wait("scalar", vRS, vRS.v)
        o.wait("scalar", vXW, vXW.v)
        o.wait("tensor", vRS, vRS.v)
        o.wait("vector", vXW, vXW.v)
        LN.emit(o, G, P0, P1, yb, qs)

    o.flush(block)
    ctx.close()
    return vXW, vXW.v




def build_program():
    nc = bass.Bass("TRN2", target_bir_lowering=False, debug=False)
    octx = ExitStack()
    G = {}
    G["semreg"] = {}
    G["semctx"] = octx
    G["x_h"] = nc.dram_tensor("x", [T, D], F32, kind="ExternalInput").ap()
    G["out_h"] = nc.dram_tensor("out", [T, D], F32,
                                kind="ExternalOutput").ap()

    def din(nm, shape, dt=F32):
        G[nm + "_h"] = nc.dram_tensor(nm, list(shape), dt,
                                      kind="ExternalInput").ap()

    din("identf", [128, 128])
    din("identb", [128, 128], BF16)
    din("identfn", [128, 128])
    din("hmask16", [128, 8 * 16], BF16)
    din("selpair", [16, 8 * 128], BF16)
    din("onescol", [128, 1], BF16)
    din("onesrow", [1, 128], BF16)

    sbp = lambda n, s, d: octx.enter_context(nc.sbuf_tensor(n, s, d))
    for pi in range(4):
        G[f"P{pi}"] = octx.enter_context(
            nc.psum_tensor(f"g_P{pi}", [128, 1024], F32))
    G["x_bf"] = sbp("g_xbf", [128, NCH * T], BF16)
    G["identf"] = sbp("g_identf", [128, 128], F32)
    G["identb"] = sbp("g_identb", [128, 128], BF16)
    G["identfn"] = sbp("g_identfn", [128, 128], F32)
    G["hmask16"] = sbp("g_hm16", [128, 8 * 16], BF16)
    G["selpair"] = sbp("g_selp", [16, 8 * 128], BF16)
    G["onescol"] = sbp("g_onesc", [128, 1], BF16)
    G["onesrow"] = sbp("g_onesr", [1, 128], BF16)
    G["LN"] = LNPack(nc, octx)
    G["const_loads"] = [
        ("identf", G["identf"]), ("identb", G["identb"]),
        ("identfn", G["identfn"]),
        ("hmask16", G["hmask16"]), ("selpair", G["selpair"]),
        ("onescol", G["onescol"]), ("onesrow", G["onesrow"]),
    ]
    if NPHASE >= 1:
        din("a0_w", [1024, 3072])
        din("a0_b", [128, 24])
        din("a0_wo", [1024, 1024])
        din("a0_bo", [128, 8])
        din("mask_a0", [128, 6 * 512], BF16)
    if NPHASE >= 2:
        din("f0_wg", [2816, 1024])
        din("f0_wu", [2816, 1024])
        din("f0_wd", [1024, 2816])
        din("f0_clipx", [1, 1])
        din("f0_clipd", [1, 1])
        G["cc0_in"] = nc.dram_tensor("cc0_in", [1, 1], F32,
                                     kind="Internal").ap()
        G["cc0_out"] = nc.dram_tensor("cc0_out", [1, 8], F32,
                                      kind="Internal").ap()
        G["cc1_in"] = nc.dram_tensor("cc1_in", [1, 1], F32,
                                     kind="Internal").ap()
        G["cc1_out"] = nc.dram_tensor("cc1_out", [1, 8], F32,
                                      kind="Internal").ap()
    if NPHASE >= 3:
        din("a1_w", [1024, 3072])
        din("a1_b", [128, 24])
        din("a1_wo", [1024, 1024])
        din("a1_bo", [128, 8])
        din("mask_a1", [128, 4 * 512], BF16)

    if NPHASE >= 4:
        G["f1_Mscr"] = nc.dram_tensor("f1_Mscr", [4, 128, 4096], BF16,
                                      kind="Internal").ap()
        G["f1_Tscr"] = nc.dram_tensor("f1_Tscr", [128, 16384], BF16,
                                      kind="Internal").ap()
        din("f1_wexp", [1024, 4096])
        din("f1_bexp", [128, 32])
        din("f1_ubB", [128, 32])
        din("f1_bcon", [128, 8])
        din("f1_dbias", [128, 8])
        din("f1_wconB", [4096, 1024])
        for g_ in range(4):
            din(f"f1_U{g_}", [[7, 5, 7, 5][g_], 128, 4096])

    with nc.Block() as block:
        bar, barv = phase_in(nc, block, G)
        if NPHASE >= 1:
            bar, barv = phase_attention(nc, block, G, L0Cfg, bar, barv)
        if NPHASE >= 2:
            bar, barv = phase_ffn0(nc, block, G, bar, barv)
        if NPHASE >= 3:
            bar, barv = phase_attention(nc, block, G, L1Cfg, bar, barv)
        if NPHASE >= 4:
            bar, barv = phase_ffn1(nc, block, G, bar, barv)
        phase_out(nc, block, G, bar, barv)
    octx.close()
    return nc


_PROGRAM = None


def _get_program():
    global _PROGRAM
    if _PROGRAM is None:
        _PROGRAM = build_program()
    return _PROGRAM


def kernel(**inputs):
    nc = _get_program()
    x = np.ascontiguousarray(np.asarray(inputs["L0_attn_x"], np.float32))
    cm = host_prep(inputs)
    in_maps = []
    for c in range(8):
        m = dict(cm)
        m["x"] = np.ascontiguousarray(x[c])
        in_maps.append(m)
    res = run_bass_kernel_spmd(nc, in_maps, core_ids=list(range(8)))
    out = np.stack([res.results[c]["out"] for c in range(8)], axis=0)
    return out.astype(np.float32)
